# version 13 (final)
# baseline (speedup 1.0000x reference)
import numpy as np
from contextlib import ExitStack

import concourse.bass as bass
import concourse.mybir as mybir
from concourse.alu_op_type import AluOpType as OP
from concourse.bass_utils import run_bass_kernel_spmd

F32 = mybir.dt.float32
BF16 = mybir.dt.bfloat16
U32 = mybir.dt.uint32
AF = mybir.ActivationFunctionType
AX = mybir.AxisListType.X

D = 1024
KC = D // 128
TT = 512
NEG = -1e30
DEBUG_TAPS = False
USE_PRECAST = False


class _Op:
    __slots__ = ("idx", "eng", "fn", "deps", "need_inc", "inc_val", "dma", "tag")

    def __init__(self, idx, eng, fn, dma, tag):
        self.idx, self.eng, self.fn, self.dma, self.tag = idx, eng, fn, dma, tag
        self.deps = []
        self.need_inc = False
        self.inc_val = None

    @property
    def semname(self):
        return self.eng + ("_dma" if self.dma else "_c")


class Prog:
    ENG_ORDER = ("sync", "gpsimd", "tensor", "vector", "scalar")

    def __init__(self, ctx):
        self.ctx = ctx
        self.ops = []
        self.last_writer = {}
        self.readers = {}
        self.fence_tails = []

    def op(self, eng, fn, reads=(), writes=(), early=(), dma=False, tag=""):
        o = _Op(len(self.ops), eng, fn, dma, tag)
        deps = {}
        for k in tuple(reads) + tuple(early):
            w = self.last_writer.get(k)
            if w is not None:
                deps[w.idx] = w
        for k in writes:
            w = self.last_writer.get(k)
            if w is not None:
                deps[w.idx] = w
            for r in self.readers.get(k, ()):
                deps[r.idx] = r
        for t in self.fence_tails:
            deps[t.idx] = t
        o.deps = list(deps.values())
        for k in tuple(reads) + tuple(early):
            self.readers.setdefault(k, []).append(o)
        for k in writes:
            self.last_writer[k] = o
            self.readers[k] = []
        self.ops.append(o)
        return o

    def fence(self):
        tails = {}
        for o in self.ops:
            tails[o.semname] = o
        self.fence_tails = list(tails.values())

    def emit(self, block):
        for o in self.ops:
            for d in o.deps:
                d.need_inc = True
        counters = {}
        for o in self.ops:
            if o.dma:
                o.need_inc = True
            if o.need_inc:
                k = 16 if o.dma else 1
                counters[o.semname] = counters.get(o.semname, 0) + k
                o.inc_val = counters[o.semname]
        for eng in self.ENG_ORDER:
            ops_e = [o for o in self.ops if o.eng == eng]
            if not ops_e:
                continue

            def make_section(ops_e):
                def section(e):
                    water = {}
                    for o in ops_e:
                        waits = {}
                        for d in o.deps:
                            if d.inc_val is None:
                                raise RuntimeError(f"dep without inc: {d.tag}")
                            s = d.semname
                            waits[s] = max(waits.get(s, 0), d.inc_val)
                        for s, v in sorted(waits.items()):
                            if water.get(s, 0) >= v:
                                continue
                            e.wait_ge(self.ctx.sem(s).h, v)
                            water[s] = v
                        insts = o.fn(e)
                        if o.need_inc:
                            last = insts[-1] if isinstance(insts, (list, tuple)) else insts
                            if last is None:
                                raise RuntimeError(f"op fn returned None: {o.tag}")
                            last.then_inc(self.ctx.sem(o.semname).h, 16 if o.dma else 1)
                return section

            getattr(block, eng)(make_section(ops_e))

    def final_waits(self):
        tails = {}
        for o in self.ops:
            tails[o.semname] = o
        def fn(e):
            return e.nop()
        o = self.op("gpsimd", fn, tag="final")
        o.deps = [t for t in tails.values() if t is not o]


class Ctx:
    def __init__(self, nc, stack, T, n_seq):
        self.nc = nc
        self.stack = stack
        self.T = T
        self.NT = T // TT
        self.n_seq = n_seq
        self.sems = {}
        self.uid = 0

    def sem(self, name):
        if name not in self.sems:
            self.sems[name] = Sem(self, name)
        return self.sems[name]

    def name(self, base):
        self.uid += 1
        return f"{base}_{self.uid}"

    def tap(self, prog, name, sb, readkeys, shape=None):
        if not DEBUG_TAPS:
            return
        shape = shape or list(sb.shape)
        d = self.nc.dram_tensor("tap_" + name, list(shape), sb.ap().dtype,
                                kind="ExternalOutput").ap()
        prog.op("gpsimd", lambda e: e.dma_start(out=d[:], in_=sb[:]),
                reads=list(readkeys), dma=True, tag="tap_" + name)

    def constcol(self, prog, value):
        if not hasattr(self, "_ccols"):
            self._ccols = {}
        v = float(value)
        if v not in self._ccols:
            t = self.stack.enter_context(
                self.nc.sbuf_tensor(self.name("cc"), [128, 1], F32, side="right"))
            key = f"cc:{v}"
            prog.op("vector", lambda e: e.memset(t[:], v), writes=[key], tag=key)
            self._ccols[v] = (t, key)
        return self._ccols[v]


class Sem:
    def __init__(self, ctx, name):
        self.h = ctx.stack.enter_context(ctx.nc.semaphore(name))
        self.n = 0


def sbuf(ctx, scope, base, shape, dtype):
    return scope.enter_context(ctx.nc.sbuf_tensor(ctx.name(base), shape, dtype))


def psumt(ctx, scope, base, dtype=F32):
    shape = [128, 512] if dtype == F32 else [128, 1024]
    return scope.enter_context(ctx.nc.psum_tensor(ctx.name(base), shape, dtype))


def fm3(ap_2d, inner):
    return ap_2d.rearrange("(c p) t -> p c t", p=128)


def cm3(sb_ap, tcols):
    return sb_ap.rearrange("p (c t) -> p c t", t=tcols)



def emit_ln_tile(ctx, prog, bufs, src_tile_ap, src_key, eps, tagpfx, out_sb,
                 out_cols, out_key, T_out=None):
    xt, xbn, xsq = bufs["xt"], bufs["xbn"], bufs["xsq"]
    meanpl, varpl, rstdpl = bufs["mean"], bufs["var"], bufs["rstd"]
    ps1, ps2 = bufs["ps1"], bufs["ps2"]
    K = bufs["k"]
    kxt, kxbn, kxsq = K + "xt", K + "xbn", K + "xsq"
    T_out = T_out or ctx.T

    if src_tile_ap is not None:
        prog.op("sync", lambda e: e.dma_start(out=cm3(xt[:], TT), in_=fm3(src_tile_ap, TT)),
                reads=[src_key], writes=[kxt], dma=True, tag=tagpfx + "ld")
    prog.op("scalar", lambda e: [e.copy(xbn[:, c * TT:(c + 1) * TT], xt[:, c * TT:(c + 1) * TT])
                                 for c in range(KC)],
            reads=[kxt], writes=[kxbn], tag=tagpfx + "cast")
    prog.op("vector", lambda e: [e.tensor_mul(xsq[:, c * TT:(c + 1) * TT],
                                              xt[:, c * TT:(c + 1) * TT],
                                              xt[:, c * TT:(c + 1) * TT]) for c in range(KC)],
            reads=[kxt], writes=[kxsq], tag=tagpfx + "sq")
    ones = ctx.ones128
    prog.op("tensor", lambda e: [e.matmul(ps1[:], ones[:], xbn[:, c * TT:(c + 1) * TT],
                                          start=(c == 0), stop=(c == KC - 1)) for c in range(KC)],
            reads=[kxbn, "c:ones"], writes=[K + "ps1"], tag=tagpfx + "mm1")
    prog.op("tensor", lambda e: [e.matmul(ps2[:], ones[:], xsq[:, c * TT:(c + 1) * TT],
                                          start=(c == 0), stop=(c == KC - 1)) for c in range(KC)],
            reads=[kxsq, "c:ones"], writes=[K + "ps2"], tag=tagpfx + "mm2")
    kmean, kvar, krstd = K + "mean", K + "var", K + "rstd"

    prog.op("vector", lambda e: e.tensor_scalar(meanpl[:], ps1[:], 1.0 / D, None, OP.mult),
            reads=[K + "ps1"], writes=[kmean], tag=tagpfx + "st1")
    prog.op("vector", lambda e: e.tensor_scalar(varpl[:], ps2[:], 1.0 / D, None, OP.mult),
            reads=[K + "ps2"], writes=[kvar], tag=tagpfx + "st2")
    prog.op("vector", lambda e: e.tensor_mul(ps1[:], meanpl[:], meanpl[:]),
            reads=[kmean], writes=[K + "ps1"], tag=tagpfx + "st3")
    prog.op("vector", lambda e: e.tensor_sub(varpl[:], varpl[:], ps1[:]),
            reads=[K + "ps1", kvar], writes=[kvar], tag=tagpfx + "st4")

    epst, epsk = ctx.constcol(prog, float(eps))
    prog.op("scalar", lambda e: e.activation(varpl[:], varpl[:], AF.Ln, bias=epst[:]),
            reads=[kvar], early=[epsk], writes=[kvar], tag=tagpfx + "ln")
    prog.op("scalar", lambda e: e.activation(rstdpl[:], varpl[:], AF.Exp, scale=-0.5),
            reads=[kvar], writes=[krstd], tag=tagpfx + "exp")

    dst = out_sb if out_sb is not None else xbn
    dkey = out_key if out_sb is not None else kxbn
    oc = out_cols if out_sb is not None else 0
    stride = T_out if out_sb is not None else TT

    prog.op("vector", lambda e: [e.tensor_sub(xt[:, c * TT:(c + 1) * TT],
                                              xt[:, c * TT:(c + 1) * TT], meanpl[:])
                                 for c in range(KC)],
            reads=[kmean, kxt], writes=[kxt], tag=tagpfx + "sub")
    prog.op("vector", lambda e: [e.tensor_mul(dst[:, c * stride + oc:c * stride + oc + TT],
                                              xt[:, c * TT:(c + 1) * TT], rstdpl[:])
                                 for c in range(KC)],
            reads=[krstd, kxt], writes=[dkey], tag=tagpfx + "mul")
    return kxbn


def make_ln_bufs(ctx, scope):
    k = ctx.name("lnb")
    return {
        "k": k,
        "xt": sbuf(ctx, scope, "ln_xt", [128, KC * TT], F32),
        "xbn": sbuf(ctx, scope, "ln_xbn", [128, KC * TT], BF16),
        "xsq": sbuf(ctx, scope, "ln_xsq", [128, KC * TT], BF16),
        "mean": sbuf(ctx, scope, "ln_mean", [128, TT], F32),
        "var": sbuf(ctx, scope, "ln_var", [128, TT], F32),
        "rstd": sbuf(ctx, scope, "ln_rstd", [128, TT], F32),
        "ps1": psumt(ctx, scope, "ln_ps1"),
        "ps2": psumt(ctx, scope, "ln_ps2"),
    }


def load_weight_bf16(ctx, prog, scope, dram_ap, kdim, mdim, key, tag):
    sb = sbuf(ctx, scope, tag, [128, (kdim // 128) * mdim], BF16)
    prog.op("gpsimd", lambda e: e.dma_start(out=cm3(sb[:], mdim), in_=fm3(dram_ap, mdim)),
            writes=[key], dma=True, tag="ldw_" + tag)
    return sb



def emit_attn_a(ctx, prog, src, dst, W, C, sidx):
    nc, T, NT, n_seq = ctx.nc, ctx.T, ctx.NT, ctx.n_seq
    NQ = T // 128
    NS_ALL = T // 64
    SKM = min(32, NS_ALL)
    pfx = f"s{sidx}a_"

    with ExitStack() as sc:
        qb = sbuf(ctx, sc, "aa_q", [128, KC * T], BF16)
        kb = sbuf(ctx, sc, "aa_k", [128, KC * T], BF16)
        vb = sbuf(ctx, sc, "aa_v", [128, (T // 128) * D], BF16)
        kstrb = sbuf(ctx, sc, "aa_kst", [128, KC * NS_ALL], BF16)
        vstr = sbuf(ctx, sc, "aa_vst", [32, D], BF16)
        selstr = sbuf(ctx, sc, "aa_sel", [128, (T // 128) * NS_ALL], BF16)
        wbt = sbuf(ctx, sc, "aa_wbt", [128, 256], F32)
        attn_out = sbuf(ctx, sc, "aa_ao", [128, KC * TT], BF16)
        prog.op("gpsimd", lambda e: e.dma_start(
                out=selstr[:].rearrange("p (c n) -> p c n", n=NS_ALL),
                in_=C["c_selstr"].rearrange("(c p) n -> p c n", p=128)),
                writes=[pfx + "sel"], dma=True, tag=pfx + "ldsel")
        prog.op("sync", lambda e: e.dma_start(out=wbt[:], in_=C["aa_wb"][:]),
                writes=[pfx + "wb"], dma=True, tag=pfx + "ldwb")

        for s in range(ctx.n_seq):
            spfx = f"{pfx}{s}_"
            with ExitStack() as ph:
                lnb = make_ln_bufs(ctx, ph)
                wqkv = load_weight_bf16(ctx, prog, ph, W["aa_wqkv"], D, 3072,
                                        pfx + "wqkv", pfx + f"wqkv{s}")
                pj = [psumt(ctx, ph, "aa_pj") for _ in range(4)]
                for ti in range(NT):
                    kp = f"{spfx}t{ti}_"
                    kxn = emit_ln_tile(ctx, prog, lnb, src[s, ti], f"xr:{s}:{ti}",
                                       1e-5, kp, None, 0, None)
                    xn = lnb["xbn"]
                    for which, off, buf, bkey in (("q", 0, qb, pfx + "Q"), ("k", D, kb, pfx + "K")):
                        for mo in range(KC):
                            bank = (mo % 4)
                            def mm(e, off=off, mo=mo, bank=bank, xn=xn):
                                return [e.matmul(pj[bank][:],
                                                 wqkv[:, c * 3072 + off + mo * 128:
                                                      c * 3072 + off + mo * 128 + 128],
                                                 xn[:, c * TT:(c + 1) * TT],
                                                 start=(c == 0), stop=(c == KC - 1))
                                        for c in range(KC)]
                            prog.op("tensor", mm, reads=[pfx + "wqkv", kxn], writes=[f"{pfx}pj{bank}"],
                                    tag=kp + which + f"mm{mo}")
                            def ev(e, mo=mo, bank=bank, buf=buf, ti=ti):
                                return e.tensor_copy(buf[:, mo * T + ti * TT: mo * T + ti * TT + TT],
                                                     pj[bank][:])
                            prog.op("vector", ev, reads=[f"{pfx}pj{bank}"], writes=[bkey],
                                    tag=kp + which + f"ev{mo}")
                    for st in range(TT // 128):
                        for half in range(2):
                            bank = (st * 2 + half) % 4
                            def mmv(e, st=st, half=half, bank=bank, xn=xn):
                                return [e.matmul(pj[bank][:],
                                                 xn[:, c * TT + st * 128: c * TT + st * 128 + 128],
                                                 wqkv[:, c * 3072 + 2048 + half * 512:
                                                      c * 3072 + 2048 + half * 512 + 512],
                                                 start=(c == 0), stop=(c == KC - 1))
                                        for c in range(KC)]
                            prog.op("tensor", mmv, reads=[pfx + "wqkv", kxn], writes=[f"{pfx}pj{bank}"],
                                    tag=kp + f"vmm{st}{half}")
                            tok = ti * (TT // 128) + st
                            def evv(e, tok=tok, half=half, bank=bank):
                                return e.tensor_copy(vb[:, tok * D + half * 512:
                                                        tok * D + half * 512 + 512], pj[bank][:])
                            prog.op("vector", evv, reads=[f"{pfx}pj{bank}"], writes=[pfx + "V"],
                                    tag=kp + f"vev{st}{half}")
                prog.op("vector", lambda e: [e.tensor_copy(
                        cm3(kstrb[:], NS_ALL)[:, c, :],
                        kb[:].rearrange("p (c t) -> p c t", t=T)[:, c, 0:T:64])
                        for c in range(KC)],
                        reads=[pfx + "K"], writes=[pfx + "kst"], tag=spfx + "kstr")
                for half in range(2):
                    def vsmm(e, half=half):
                        return [e.matmul(pj[half][0:NS_ALL, 0:512],
                                         selstr[:, tc * NS_ALL: (tc + 1) * NS_ALL],
                                         vb[:, tc * D + half * 512: tc * D + half * 512 + 512],
                                         start=(tc == 0), stop=(tc == T // 128 - 1))
                                for tc in range(T // 128)]
                    prog.op("tensor", vsmm, reads=[pfx + "sel", pfx + "V"],
                            writes=[f"{pfx}pj{half}"], tag=spfx + f"vsmm{half}")
                    prog.op("vector", lambda e, half=half: e.tensor_copy(
                            vstr[0:NS_ALL, half * 512:(half + 1) * 512],
                            pj[half][0:NS_ALL, 0:512]),
                            reads=[f"{pfx}pj{half}"], writes=[pfx + "vstr"],
                            tag=spfx + f"vsev{half}")
                prog.fence()

            with ExitStack() as ph:
                wout = load_weight_bf16(ctx, prog, ph, W["aa_wout"], D, D,
                                        pfx + "wout", pfx + f"wout{s}")
                scores = sbuf(ctx, ph, "aa_sc", [128, 320], F32)
                pbuf = [sbuf(ctx, ph, f"aa_p{i}", [128, 320], BF16) for i in range(2)]
                ptb = sbuf(ctx, ph, "aa_pt", [128, 512], BF16)
                negmax = sbuf(ctx, ph, "aa_nm", [128, 1], F32)
                lrow = sbuf(ctx, ph, "aa_l", [128, 1], F32)
                linv = sbuf(ctx, ph, "aa_li", [128, 1], F32)
                xrt = sbuf(ctx, ph, "aa_xr", [128, KC * TT], F32)
                sc_ps = [psumt(ctx, ph, "aa_scps") for _ in range(2)]
                pt_ps = [psumt(ctx, ph, "aa_ptps", BF16) for _ in range(2)]
                av_ps = [psumt(ctx, ph, "aa_avps") for _ in range(2)]
                oj_ps = [psumt(ctx, ph, "aa_ojps") for _ in range(2)]

                def geom(qt):
                    wk = 128 if qt == 0 else 256
                    woff = 0 if qt == 0 else (qt - 1) * 128
                    ns = max(0, 2 * (qt - 1))
                    return wk, woff, ns

                def stageA(qt, h):
                    hc, hp = h // 2, 64 * (h % 2)
                    wk, woff, ns = geom(qt)
                    tot = wk + ns
                    pb = pbuf[h % 2]
                    pk = f"{pfx}P{h % 2}"
                    hkp = f"{spfx}q{qt}h{h}_"
                    def smm(e, hc=hc, hp=hp, qt=qt, wk=wk, woff=woff, ns=ns):
                        insts = [e.matmul(sc_ps[0][:, 0:wk],
                                          qb[hp:hp + 64, hc * T + qt * 128: hc * T + qt * 128 + 128],
                                          kb[hp:hp + 64, hc * T + woff: hc * T + woff + wk],
                                          start=True, stop=True)]
                        if ns > 0:
                            insts.append(e.matmul(sc_ps[1][:, 0:ns],
                                                  qb[hp:hp + 64, hc * T + qt * 128: hc * T + qt * 128 + 128],
                                                  kstrb[hp:hp + 64, hc * NS_ALL: hc * NS_ALL + ns],
                                                  start=True, stop=True))
                        return insts
                    prog.op("tensor", smm, reads=[pfx + "Q", pfx + "K", pfx + "kst"],
                            writes=[f"{pfx}scps0", f"{pfx}scps1"], tag=hkp + "smm")
                    def sev(e, qt=qt, wk=wk, ns=ns):
                        insts = [e.scalar_tensor_tensor(
                            scores[:, 0:wk], sc_ps[0][:, 0:wk], 0.125,
                            wbt[:, 256 - wk:256], OP.mult, OP.add)]
                        if ns > 0:
                            insts.append(e.tensor_scalar(
                                scores[:, wk:wk + ns], sc_ps[1][:, 0:ns],
                                0.125, None, OP.mult))
                        return insts
                    prog.op("vector", sev, reads=[f"{pfx}scps0", f"{pfx}scps1", pfx + "wb"],
                            writes=[pfx + "scores"], tag=hkp + "sev")
                    prog.op("vector", lambda e, tot=tot: e.reduce_max(
                            negmax[:], scores[:, 0:tot], AX, negate=True),
                            reads=[pfx + "scores"], writes=[pfx + "negmax"], tag=hkp + "neg")
                    if 0 < ns < SKM:
                        prog.op("vector", lambda e, wk=wk, tot=tot, pb=pb: e.memset(
                                pb[:, tot:wk + SKM], 0.0),
                                writes=[pk], tag=hkp + "pz")
                    prog.op("scalar", lambda e, tot=tot, pb=pb: e.activation(
                            pb[:, 0:tot], scores[:, 0:tot], AF.Exp,
                            bias=negmax[:], accum_out=lrow[:]),
                            reads=[pfx + "scores"], early=[pfx + "negmax"],
                            writes=[pk, pfx + "l"], tag=hkp + "exp")
                    prog.op("vector", lambda e: e.reciprocal(linv[:], lrow[:]),
                            reads=[pfx + "l"], writes=[pfx + "linv"], tag=hkp + "rcp")
                    prog.op("vector", lambda e, tot=tot, pb=pb: e.tensor_scalar(
                            pb[:, 0:tot], pb[:, 0:tot], linv[:], None, OP.mult),
                            reads=[pk], early=[pfx + "linv"], writes=[pk],
                            tag=hkp + "pscale")

                def stageB(qt, h):
                    hc, hp = h // 2, 64 * (h % 2)
                    qq = qt % 4
                    wk, woff, ns = geom(qt)
                    nwc = wk // 128
                    pb = pbuf[h % 2]
                    pk = f"{pfx}P{h % 2}"
                    hkp = f"{spfx}q{qt}h{h}_"
                    def tp(e, nwc=nwc, ns=ns, pb=pb):
                        insts = [e.transpose(pt_ps[0][:, j * 128:(j + 1) * 128],
                                             pb[:, j * 128:(j + 1) * 128], ctx.ident[:])
                                 for j in range(nwc)]
                        if ns > 0:
                            insts.append(e.transpose(pt_ps[1][0:SKM, 0:128],
                                                     pb[:, nwc * 128:nwc * 128 + SKM],
                                                     ctx.ident[:]))
                        return insts
                    prog.op("tensor", tp, reads=[pk, "c:ident"],
                            writes=[f"{pfx}ptps0", f"{pfx}ptps1"], tag=hkp + "tp")
                    def tpe(e, nwc=nwc, ns=ns):
                        insts = [e.copy(ptb[:, 0:nwc * 128], pt_ps[0][:, 0:nwc * 128])]
                        if ns > 0:
                            insts.append(e.copy(ptb[0:SKM, 384:512], pt_ps[1][0:SKM, 0:128]))
                        return insts
                    prog.op("scalar", tpe, reads=[f"{pfx}ptps0", f"{pfx}ptps1"],
                            writes=[pfx + "Pt"], tag=hkp + "tpe")
                    ab = h % 2
                    def av(e, h=h, ab=ab, qt=qt, nwc=nwc, ns=ns, woff=woff):
                        insts = []
                        for j in range(nwc):
                            tc = woff // 128 + j
                            insts.append(e.matmul(av_ps[ab][0:64, 0:128],
                                                  vb[:, tc * D + h * 64: tc * D + h * 64 + 64],
                                                  ptb[:, j * 128:(j + 1) * 128],
                                                  start=(j == 0), stop=(j == nwc - 1 and ns == 0)))
                        if ns > 0:
                            insts.append(e.matmul(av_ps[ab][0:64, 0:128],
                                                  vstr[0:SKM, h * 64: h * 64 + 64],
                                                  ptb[0:SKM, 384:512],
                                                  start=False, stop=True))
                        return insts
                    prog.op("tensor", av, reads=[pfx + "Pt", pfx + "V", pfx + "vstr"],
                            writes=[f"{pfx}avps{ab}"], tag=hkp + "av")
                    def ave(e, hc=hc, hp=hp, ab=ab, qq=qq):
                        return e.tensor_copy(
                            attn_out[hp:hp + 64, hc * TT + qq * 128: hc * TT + qq * 128 + 128],
                            av_ps[ab][0:64, 0:128])
                    prog.op("vector", ave, reads=[f"{pfx}avps{ab}"],
                            writes=[pfx + "ao"], tag=hkp + "ave")

                def phaseC(g):
                    prog.op("sync", lambda e, g=g, s=s: e.dma_start(out=cm3(xrt[:], TT),
                                                                    in_=fm3(src[s, g], TT)),
                            reads=[f"xr:{s}:{g}"], writes=[pfx + "xrt"], dma=True,
                            tag=f"{spfx}xr{g}")
                    for mo in range(KC):
                        ob = mo % 2
                        def omm(e, mo=mo, ob=ob):
                            return [e.matmul(oj_ps[ob][:],
                                             wout[:, c * D + mo * 128: c * D + mo * 128 + 128],
                                             attn_out[:, c * TT:(c + 1) * TT],
                                             start=(c == 0), stop=(c == KC - 1))
                                    for c in range(KC)]
                        prog.op("tensor", omm, reads=[pfx + "wout", pfx + "ao"],
                                writes=[f"{pfx}ojps{ob}"], tag=f"{spfx}omm{g}_{mo}")
                        def oev(e, mo=mo, ob=ob):
                            return e.tensor_add(xrt[:, mo * TT:(mo + 1) * TT],
                                                xrt[:, mo * TT:(mo + 1) * TT], oj_ps[ob][:])
                        prog.op("vector", oev, reads=[f"{pfx}ojps{ob}", pfx + "xrt"],
                                writes=[pfx + "xrt"], tag=f"{spfx}oev{g}_{mo}")
                    prog.op("sync", lambda e, g=g, s=s: e.dma_start(out=fm3(dst[s, g], TT),
                                                                    in_=cm3(xrt[:], TT)),
                            reads=[pfx + "xrt"], writes=[f"xr:{s}:{g}"],
                            dma=True, tag=f"{spfx}st{g}")

                items = [(qt, h) for qt in range(NQ) for h in range(16)]
                prev = None
                for it in items + [None]:
                    if it is not None:
                        stageA(*it)
                    if prev is not None:
                        stageB(*prev)
                        if prev[1] == 15 and prev[0] % 4 == 3:
                            phaseC(prev[0] // 4)
                    prev = it
                prog.fence()
        prog.fence()



def emit_moe(ctx, prog, src, dst, W, C, sidx):
    nc, T, NT, n_seq = ctx.nc, ctx.T, ctx.NT, ctx.n_seq
    pfx = f"s{sidx}m_"
    NS = TT // 128

    with ExitStack() as sc:
        xnorm = sbuf(ctx, sc, "m_xn", [128, KC * T], BF16)
        acc = sbuf(ctx, sc, "m_acc", [128, KC * T], F32)
        gfm = sbuf(ctx, sc, "m_gfm", [8, T], F32)
        ones1r = sbuf(ctx, sc, "m_o1r", [1, 128], F32)
        identf = sbuf(ctx, sc, "m_idf", [128, 128], F32)
        gwt = sbuf(ctx, sc, "m_gw", [128, KC * 8], BF16)
        gbias_sb = sbuf(ctx, sc, "m_gb", [1, 8], F32)
        gbrep = sbuf(ctx, sc, "m_gbr", [128, 8], F32)
        glog = sbuf(ctx, sc, "m_glog", [128, NS * 8], F32)
        gsb = sbuf(ctx, sc, "m_gsb", [128, 8], F32)
        gmax8 = sbuf(ctx, sc, "m_gm8", [128, 8], F32)
        gmask = sbuf(ctx, sc, "m_gmk", [128, 8], F32)
        gnm = sbuf(ctx, sc, "m_gnm", [128, 1], F32)
        gl = sbuf(ctx, sc, "m_gl", [128, 1], F32)
        gli = sbuf(ctx, sc, "m_gli", [128, 1], F32)
        gps = psumt(ctx, sc, "m_gps")
        gtps = psumt(ctx, sc, "m_gtps")

        sel8 = sbuf(ctx, sc, "m_sel", [8, 8 * 128], F32)
        prog.op("sync", lambda e: e.dma_start(out=sel8[:], in_=C["c_sel8"][:]),
                writes=[pfx + "sel8"], dma=True, tag=pfx + "ldsel")
        prog.op("sync", lambda e: e.dma_start(out=ones1r[:], in_=C["c_ones1r"][:]),
                writes=[pfx + "o1r"], dma=True, tag=pfx + "ldo1r")
        prog.op("sync", lambda e: e.dma_start(out=identf[:], in_=C["c_identf"][:]),
                writes=[pfx + "idf"], dma=True, tag=pfx + "ldidf")
        prog.op("sync", lambda e: e.dma_start(out=gbias_sb[:], in_=W["moe_gbias"][:]),
                writes=[pfx + "gb"], dma=True, tag=pfx + "ldgb")
        prog.op("gpsimd", lambda e: e.dma_start(out=cm3(gwt[:], 8), in_=fm3(W["moe_gw"], 8)),
                writes=[pfx + "gw"], dma=True, tag=pfx + "ldgw")
        prog.op("tensor", lambda e: e.matmul(gps[0:128, 0:8], ones1r[:], gbias_sb[:],
                                             start=True, stop=True),
                reads=[pfx + "o1r", pfx + "gb"], writes=[pfx + "gps"], tag=pfx + "gbrep_mm")
        prog.op("vector", lambda e: e.tensor_copy(gbrep[:], gps[0:128, 0:8]),
                reads=[pfx + "gps"], writes=[pfx + "gbrep"], tag=pfx + "gbrep_ev")

        for s in range(n_seq):
            spfx = f"{pfx}{s}_"
            prog.op("vector", lambda e: [e.memset(acc[:, c * T:(c + 1) * T], 0.0)
                                         for c in range(KC)],
                    writes=[pfx + "acc"], tag=spfx + "accz")
            with ExitStack() as ph:
                lnb = make_ln_bufs(ctx, ph)
                for ti in range(NT):
                    kp = f"{spfx}t{ti}_"
                    emit_ln_tile(ctx, prog, lnb, src[s, ti], f"xr:{s}:{ti}", 1e-5,
                                 kp, xnorm, ti * TT, pfx + "xn")
                    for st in range(NS):
                        def gmm(e, ti=ti, st=st):
                            return [e.matmul(gps[:, st * 8:(st + 1) * 8],
                                             xnorm[:, c * T + ti * TT + st * 128:
                                                   c * T + ti * TT + st * 128 + 128],
                                             gwt[:, c * 8:(c + 1) * 8],
                                             start=(c == 0), stop=(c == KC - 1))
                                    for c in range(KC)]
                        prog.op("tensor", gmm, reads=[pfx + "xn", pfx + "gw"],
                                writes=[pfx + "gps"], tag=kp + f"gmm{st}")
                    for st in range(NS):
                        sl = slice(st * 8, st * 8 + 8)
                        prog.op("vector", lambda e, sl=sl: e.tensor_copy(glog[:, sl], gps[:, sl]),
                                reads=[pfx + "gps"], writes=[pfx + "glog"], tag=kp + f"glev{st}")
                        tok0 = ti * TT + st * 128
                        prog.op("vector", lambda e, sl=sl: e.reduce_max(gnm[:], glog[:, sl], AX, negate=True),
                                reads=[pfx + "glog"], writes=[pfx + "gnm"], tag=kp + f"gmax{st}")
                        prog.op("scalar", lambda e, sl=sl: e.activation(glog[:, sl], glog[:, sl], AF.Exp,
                                                                        bias=gnm[:], accum_out=gl[:]),
                                reads=[pfx + "glog"], early=[pfx + "gnm"],
                                writes=[pfx + "glog", pfx + "gl"], tag=kp + f"gexp{st}")
                        prog.op("vector", lambda e: e.reciprocal(gli[:], gl[:]),
                                reads=[pfx + "gl"], writes=[pfx + "gli"], tag=kp + f"grcp{st}")
                        prog.op("vector", lambda e, sl=sl: e.tensor_scalar(glog[:, sl], glog[:, sl],
                                                                           gli[:], None, OP.mult),
                                reads=[pfx + "glog"], early=[pfx + "gli"],
                                writes=[pfx + "glog"], tag=kp + f"gsc{st}")
                        prog.op("vector", lambda e, sl=sl: e.tensor_add(gsb[:], glog[:, sl], gbrep[:]),
                                reads=[pfx + "glog", pfx + "gbrep"], writes=[pfx + "gsb"],
                                tag=kp + f"gsb{st}")
                        prog.op("vector", lambda e: e.max(out=gmax8[:], in_=gsb[:]),
                                reads=[pfx + "gsb"], writes=[pfx + "gm8"], tag=kp + f"gm8{st}")
                        prog.op("vector", lambda e: e.tensor_scalar(gmask[:], gsb[:], gmax8[:, 1:2],
                                                                    None, OP.is_ge),
                                reads=[pfx + "gsb"], early=[pfx + "gm8"],
                                writes=[pfx + "gmask"], tag=kp + f"gge{st}")
                        prog.op("vector", lambda e, sl=sl: e.tensor_mul(gmask[:], glog[:, sl], gmask[:]),
                                reads=[pfx + "glog", pfx + "gmask"], writes=[pfx + "gmask"],
                                tag=kp + f"ggate{st}")
                        prog.op("tensor", lambda e: e.transpose(gtps[0:8, 0:128], gmask[:, 0:8], identf[:]),
                                reads=[pfx + "gmask", pfx + "idf"], writes=[pfx + "gtps"],
                                tag=kp + f"gtp{st}")
                        prog.op("vector", lambda e, tok0=tok0: e.tensor_copy(
                                gfm[0:8, tok0:tok0 + 128], gtps[0:8, 0:128]),
                                reads=[pfx + "gtps"], writes=[pfx + "gfm"], tag=kp + f"gfev{st}")
                prog.fence()
            if s == 0:
                ctx.tap(prog, "xnorm", xnorm, [pfx + "xn"])
                ctx.tap(prog, "gfm", gfm, [pfx + "gfm"])
                ctx.tap(prog, "glog", glog, [pfx + "glog"])
                ctx.tap(prog, "gsb", gsb, [pfx + "gsb"])
                ctx.tap(prog, "gmask", gmask, [pfx + "gmask"])
                ctx.tap(prog, "gm8", gmax8, [pfx + "gm8"])

            with ExitStack() as ph:
                w1b = sbuf(ctx, ph, "m_w1", [128, KC * 1024], BF16)
                w3b = sbuf(ctx, ph, "m_w3", [128, KC * 1024], BF16)
                w2b = sbuf(ctx, ph, "m_w2", [128, KC * 1024], BF16)
                s1 = sbuf(ctx, ph, "m_s1", [128, KC * TT], BF16)
                h3g = sbuf(ctx, ph, "m_h3g", [128, KC * TT], BF16)
                hh = [sbuf(ctx, ph, f"m_hh{i}", [128, KC * TT], BF16) for i in range(2)]
                gp_sb = sbuf(ctx, ph, "m_gp", [128, TT], F32)
                pj = [psumt(ctx, ph, "m_pj") for _ in range(4)]
                gpl = psumt(ctx, ph, "m_gpl")

                def h_stage(e_, ti):
                    tk = f"{spfx}e{e_}t{ti}_"
                    if e_ < 8:
                        prog.op("tensor", lambda e, e_=e_, ti=ti: e.matmul(
                                gpl[:], sel8[:, e_ * 128:(e_ + 1) * 128],
                                gfm[0:8, ti * TT:(ti + 1) * TT],
                                start=True, stop=True),
                                reads=[pfx + "sel8", pfx + "gfm"], writes=[pfx + "gpl"],
                                tag=tk + "gplmm")
                        prog.op("vector", lambda e: e.tensor_copy(gp_sb[:], gpl[:]),
                                reads=[pfx + "gpl"], writes=[pfx + "gp"], tag=tk + "gplev")
                    for which, wb in (("h1", w1b), ("h3", w3b)):
                        for mo in range(KC):
                            bank = mo % 4
                            def mm(e, wb=wb, mo=mo, bank=bank, ti=ti):
                                return [e.matmul(pj[bank][:],
                                                 wb[:, c * 1024 + mo * 128: c * 1024 + mo * 128 + 128],
                                                 xnorm[:, c * T + ti * TT: c * T + ti * TT + TT],
                                                 start=(c == 0), stop=(c == KC - 1))
                                        for c in range(KC)]
                            prog.op("tensor", mm,
                                    reads=[pfx + ("w1" if which == "h1" else "w3"), pfx + "xn"],
                                    writes=[f"{pfx}pj{bank}"], tag=tk + which + f"mm{mo}")
                            if which == "h1":
                                def ev(e, mo=mo, bank=bank):
                                    return e.activation(s1[:, mo * TT:(mo + 1) * TT],
                                                        pj[bank][:], AF.Silu)
                                prog.op("scalar", ev, reads=[f"{pfx}pj{bank}"],
                                        writes=[pfx + "s1"], tag=tk + which + f"ev{mo}")
                            elif e_ < 8:
                                def ev(e, mo=mo, bank=bank):
                                    return e.tensor_mul(h3g[:, mo * TT:(mo + 1) * TT],
                                                        pj[bank][:], gp_sb[:])
                                prog.op("vector", ev, reads=[f"{pfx}pj{bank}", pfx + "gp"],
                                        writes=[pfx + "h3g"], tag=tk + which + f"ev{mo}")
                            else:
                                def ev(e, mo=mo, bank=bank):
                                    return e.tensor_copy(h3g[:, mo * TT:(mo + 1) * TT],
                                                         pj[bank][:])
                                prog.op("vector", ev, reads=[f"{pfx}pj{bank}"],
                                        writes=[pfx + "h3g"], tag=tk + which + f"ev{mo}")
                    hb_ = hh[ti % 2]
                    hbk = f"{pfx}hh{ti % 2}"
                    prog.op("vector", lambda e, hb_=hb_: [e.tensor_mul(
                            hb_[:, c * TT:(c + 1) * TT], s1[:, c * TT:(c + 1) * TT],
                            h3g[:, c * TT:(c + 1) * TT]) for c in range(KC)],
                            reads=[pfx + "s1", pfx + "h3g"], writes=[hbk],
                            tag=tk + "comb")

                def w2_stage(e_, ti):
                    tk = f"{spfx}e{e_}t{ti}_"
                    hb_ = hh[ti % 2]
                    hbk = f"{pfx}hh{ti % 2}"
                    for mo in range(KC):
                        bank = mo % 4
                        def mm2(e, mo=mo, bank=bank, hb_=hb_):
                            return [e.matmul(pj[bank][:],
                                             w2b[:, c * 1024 + mo * 128: c * 1024 + mo * 128 + 128],
                                             hb_[:, c * TT:(c + 1) * TT],
                                             start=(c == 0), stop=(c == KC - 1))
                                    for c in range(KC)]
                        prog.op("tensor", mm2, reads=[pfx + "w2", hbk],
                                writes=[f"{pfx}pj{bank}"], tag=tk + f"w2mm{mo}")
                        def ev2(e, mo=mo, bank=bank, ti=ti):
                            col = mo * T + ti * TT
                            return e.tensor_add(acc[:, col:col + TT], acc[:, col:col + TT],
                                                pj[bank][:])
                        prog.op("vector", ev2, reads=[f"{pfx}pj{bank}", pfx + "acc"],
                                writes=[pfx + "acc"], tag=tk + f"w2ev{mo}")

                def ld(nm, buf, e_):
                    if USE_PRECAST:
                        def ldw(e, nm=nm, buf=buf, e_=e_):
                            return e.dma_start(out=cm3(buf[:], 1024),
                                               in_=fm3(ctx.moe_bf[nm + "x"][e_], 1024))
                        prog.op("gpsimd", ldw, reads=[f"moebf:{nm}x:{e_}"],
                                writes=[pfx + nm], dma=True,
                                tag=f"{spfx}e{e_}ld{nm}")
                    else:
                        def ldw(e, nm=nm, buf=buf, e_=e_):
                            return e.dma_start(out=cm3(buf[:], 1024),
                                               in_=fm3(W[f"moe_{nm}x"][e_], 1024))
                        prog.op("gpsimd", ldw, writes=[pfx + nm], dma=True,
                                tag=f"{spfx}e{e_}ld{nm}")

                prev = None
                for e_ in range(10):
                    ld("w1", w1b, e_)
                    ld("w3", w3b, e_)
                    if NT == 1:
                        ld("w2", w2b, e_)
                        h_stage(e_, 0)
                        w2_stage(e_, 0)
                        continue
                    for ti in range(NT):
                        h_stage(e_, ti)
                        if prev is not None:
                            w2_stage(*prev)
                        if ti == 0:
                            ld("w2", w2b, e_)
                        prev = (e_, ti)
                if prev is not None:
                    w2_stage(*prev)
                prog.fence()

            with ExitStack() as ph:
                xrta = [sbuf(ctx, ph, f"m_xr{i}", [128, KC * TT], F32) for i in range(2)]
                for ti in range(NT):
                    xrt = xrta[ti % 2]
                    xk = pfx + "xrt" + str(ti % 2)
                    prog.op("sync", lambda e, ti=ti, s=s, xrt=xrt: e.dma_start(
                            out=cm3(xrt[:], TT), in_=fm3(src[s, ti], TT)),
                            reads=[f"xr:{s}:{ti}"], writes=[xk], dma=True,
                            tag=f"{spfx}rld{ti}")
                    def radd(e, ti=ti, xrt=xrt):
                        return [e.tensor_add(xrt[:, c * TT:(c + 1) * TT],
                                             xrt[:, c * TT:(c + 1) * TT],
                                             acc[:, c * T + ti * TT: c * T + ti * TT + TT])
                                for c in range(KC)]
                    prog.op("vector", radd, reads=[pfx + "acc", xk],
                            writes=[xk], tag=f"{spfx}radd{ti}")
                    prog.op("sync", lambda e, ti=ti, s=s, xrt=xrt: e.dma_start(
                            out=fm3(dst[s, ti], TT), in_=cm3(xrt[:], TT)),
                            reads=[xk], writes=[f"xr:{s}:{ti}"], dma=True,
                            tag=f"{spfx}rst{ti}")
                prog.fence()
        prog.fence()



def emit_l1ffn(ctx, prog, src, dst, W, C, sidx):
    nc, T, NT, n_seq = ctx.nc, ctx.T, ctx.NT, ctx.n_seq
    pfx = f"s{sidx}f_"
    TP = T + 1

    with ExitStack() as sc:
        xn = sbuf(ctx, sc, "f_xn", [128, KC * TP], BF16)
        mur = sbuf(ctx, sc, "f_mur", [128, KC], F32)
        muk = sbuf(ctx, sc, "f_muk", [128, KC], F32)
        wr = load_weight_bf16(ctx, prog, sc, W["f1_wr"], D, D, pfx + "wr", pfx + "wr")
        wk = load_weight_bf16(ctx, prog, sc, W["f1_wk"], D, D, pfx + "wk", pfx + "wk")
        wv = load_weight_bf16(ctx, prog, sc, W["f1_wv"], D, D, pfx + "wv", pfx + "wv")
        prog.op("sync", lambda e: e.dma_start(out=mur[:], in_=W["f1_mur"][:]),
                writes=[pfx + "mur"], dma=True, tag=pfx + "ldmur")
        prog.op("sync", lambda e: e.dma_start(out=muk[:], in_=W["f1_muk"][:]),
                writes=[pfx + "muk"], dma=True, tag=pfx + "ldmuk")
        prog.op("vector", lambda e: [e.memset(xn[:, c * TP:c * TP + 1], 0.0)
                                     for c in range(KC)],
                writes=[pfx + "z0"], tag=pfx + "zcol")

        for s in range(ctx.n_seq):
            spfx = f"{pfx}{s}_"
            with ExitStack() as ph:
                with ExitStack() as lnph:
                    lnb = make_ln_bufs(ctx, lnph)
                    for ti in range(NT):
                        kp = f"{spfx}t{ti}_"
                        emit_ln_tile(ctx, prog, lnb, src[s, ti], f"xr:{s}:{ti}", 1e-5,
                                     kp, xn, ti * TT + 1, pfx + "xn", T_out=TP)
                    prog.fence()
                t1a = [sbuf(ctx, ph, f"f_t1{i}", [128, KC * TT], BF16) for i in range(2)]
                xmra = [sbuf(ctx, ph, f"f_xmr{i}", [128, KC * TT], BF16) for i in range(2)]
                xmka = [sbuf(ctx, ph, f"f_xmk{i}", [128, KC * TT], BF16) for i in range(2)]
                rbufa = [sbuf(ctx, ph, "f_r", [128, KC * TT], BF16)] * 2
                kka = [sbuf(ctx, ph, f"f_kk{i}", [128, KC * TT], BF16) for i in range(2)]
                tka = [sbuf(ctx, ph, f"f_tk{i}", [128, KC * TT], BF16) for i in range(2)]
                xrta = [sbuf(ctx, ph, f"f_xrt{i}", [128, KC * TT], F32) for i in range(2)]
                pj = [psumt(ctx, ph, "f_pj") for _ in range(4)]

                for ti in range(NT):
                    kp = f"{spfx}t{ti}_"
                    par = ti % 2
                    t1, xmr, xmk = t1a[par], xmra[par], xmka[par]
                    rbuf, kk, tk_, xrt = rbufa[par], kka[par], tka[par], xrta[par]
                    P_ = str(par)
                    zdep = [pfx + "z0"] if ti == 0 else []
                    def shift_ops(e, ti=ti, t1=t1):
                        return [e.tensor_sub(t1[:, c * TT:(c + 1) * TT],
                                             xn[:, c * TP + 1 + ti * TT: c * TP + 1 + ti * TT + TT],
                                             xn[:, c * TP + ti * TT: c * TP + ti * TT + TT])
                                for c in range(KC)]
                    prog.op("vector", shift_ops, reads=[pfx + "xn"] + zdep,
                            writes=[pfx + "t1" + P_], tag=kp + "sh")
                    for which, mu, dstb, dkey in (("r", mur, xmr, pfx + "xmr" + P_),
                                                  ("k", muk, xmk, pfx + "xmk" + P_)):
                        def mix(e, ti=ti, mu=mu, dstb=dstb, t1=t1):
                            return [e.scalar_tensor_tensor(
                                dstb[:, c * TT:(c + 1) * TT], t1[:, c * TT:(c + 1) * TT],
                                mu[:, c:c + 1],
                                xn[:, c * TP + ti * TT: c * TP + ti * TT + TT],
                                OP.mult, OP.add) for c in range(KC)]
                        prog.op("vector", mix,
                                reads=[pfx + "t1" + P_, pfx + "xn", pfx + "mu" + which] + zdep,
                                writes=[dkey], tag=kp + "mix" + which)
                    for mo in range(KC):
                        bank = mo % 4
                        def mmr(e, mo=mo, bank=bank, xmr=xmr):
                            return [e.matmul(pj[bank][:],
                                             wr[:, c * D + mo * 128: c * D + mo * 128 + 128],
                                             xmr[:, c * TT:(c + 1) * TT],
                                             start=(c == 0), stop=(c == KC - 1))
                                    for c in range(KC)]
                        prog.op("tensor", mmr, reads=[pfx + "wr", pfx + "xmr" + P_],
                                writes=[f"{pfx}pj{bank}"], tag=kp + f"rmm{mo}")
                        prog.op("scalar", lambda e, mo=mo, bank=bank, rbuf=rbuf:
                                e.activation(rbuf[:, mo * TT:(mo + 1) * TT], pj[bank][:],
                                             AF.Sigmoid),
                                reads=[f"{pfx}pj{bank}"], writes=[pfx + "r"],
                                tag=kp + f"rev{mo}")
                    for mo in range(KC):
                        bank = mo % 4
                        def mmk(e, mo=mo, bank=bank, xmk=xmk):
                            return [e.matmul(pj[bank][:],
                                             wk[:, c * D + mo * 128: c * D + mo * 128 + 128],
                                             xmk[:, c * TT:(c + 1) * TT],
                                             start=(c == 0), stop=(c == KC - 1))
                                    for c in range(KC)]
                        prog.op("tensor", mmk, reads=[pfx + "wk", pfx + "xmk" + P_],
                                writes=[f"{pfx}pj{bank}"], tag=kp + f"kmm{mo}")
                        prog.op("vector", lambda e, mo=mo, bank=bank, tk_=tk_:
                                e.tensor_scalar(tk_[:, mo * TT:(mo + 1) * TT], pj[bank][:],
                                                0.0, None, OP.max),
                                reads=[f"{pfx}pj{bank}"], writes=[pfx + "tk" + P_],
                                tag=kp + f"krelu{mo}")
                    prog.op("vector", lambda e, kk=kk, tk_=tk_: [e.tensor_mul(
                            kk[:, c * TT:(c + 1) * TT],
                            tk_[:, c * TT:(c + 1) * TT],
                            tk_[:, c * TT:(c + 1) * TT]) for c in range(KC)],
                            reads=[pfx + "tk" + P_], writes=[pfx + "kk" + P_], tag=kp + "ksq")
                    prog.op("sync", lambda e, ti=ti, s=s, xrt=xrt: e.dma_start(
                            out=cm3(xrt[:], TT), in_=fm3(src[s, ti], TT)),
                            reads=[f"xr:{s}:{ti}"], writes=[pfx + "xrt" + P_], dma=True,
                            tag=kp + "xrld")
                    for mo in range(KC):
                        bank = mo % 4
                        def mmv(e, mo=mo, bank=bank, kk=kk):
                            return [e.matmul(pj[bank][:],
                                             wv[:, c * D + mo * 128: c * D + mo * 128 + 128],
                                             kk[:, c * TT:(c + 1) * TT],
                                             start=(c == 0), stop=(c == KC - 1))
                                    for c in range(KC)]
                        prog.op("tensor", mmv, reads=[pfx + "wv", pfx + "kk" + P_],
                                writes=[f"{pfx}pj{bank}"], tag=kp + f"vmm{mo}")
                        prog.op("vector", lambda e, mo=mo, bank=bank, tk_=tk_, rbuf=rbuf:
                                e.tensor_mul(tk_[:, mo * TT:(mo + 1) * TT], pj[bank][:],
                                             rbuf[:, mo * TT:(mo + 1) * TT]),
                                reads=[f"{pfx}pj{bank}", pfx + "r"], writes=[pfx + "tk" + P_],
                                tag=kp + f"vev{mo}")
                    prog.op("vector", lambda e, xrt=xrt, tk_=tk_: [e.tensor_add(
                            xrt[:, c * TT:(c + 1) * TT],
                            xrt[:, c * TT:(c + 1) * TT],
                            tk_[:, c * TT:(c + 1) * TT]) for c in range(KC)],
                            reads=[pfx + "tk" + P_, pfx + "xrt" + P_], writes=[pfx + "xrt" + P_],
                            tag=kp + "radd")
                    prog.op("sync", lambda e, ti=ti, s=s, xrt=xrt: e.dma_start(
                            out=fm3(dst[s, ti], TT), in_=cm3(xrt[:], TT)),
                            reads=[pfx + "xrt" + P_], writes=[f"xr:{s}:{ti}"], dma=True,
                            tag=kp + "xrst")
                prog.fence()
        prog.fence()



def emit_l1aa(ctx, prog, src, dst, W, C, sidx):
    nc, T, NT, n_seq = ctx.nc, ctx.T, ctx.NT, ctx.n_seq
    pfx = f"s{sidx}g_"
    RMS_EPS = 1.1920929e-07

    with ExitStack() as sc:
        xnf = sbuf(ctx, sc, "g_xn", [128, KC * T], BF16)
        qb = sbuf(ctx, sc, "g_q", [128, KC * T], BF16)
        kb = sbuf(ctx, sc, "g_k", [128, 2 * T], BF16)
        vb = sbuf(ctx, sc, "g_v", [128, (T // 128) * 256], BF16)
        ropepl = sbuf(ctx, sc, "g_rope", [128, T], F32)
        normw = sbuf(ctx, sc, "g_nw", [128, KC], F32)
        attn_out = sbuf(ctx, sc, "g_ao", [128, KC * TT], BF16)
        prog.op("sync", lambda e: e.dma_start(out=ropepl[:], in_=C["c_ropea1"][:]),
                writes=[pfx + "rope"], dma=True, tag=pfx + "ldrope")
        prog.op("sync", lambda e: e.dma_start(out=normw[:], in_=W["a1_normw"][:]),
                writes=[pfx + "nw"], dma=True, tag=pfx + "ldnw")

        for s in range(n_seq):
            spfx = f"{pfx}{s}_"
            with ExitStack() as ph:
                lnb = make_ln_bufs(ctx, ph)
                wq = load_weight_bf16(ctx, prog, ph, W["a1_wq"], D, D, pfx + "wq",
                                      pfx + f"wq{s}")
                wk = load_weight_bf16(ctx, prog, ph, W["a1_wk"], D, 256, pfx + "wk",
                                      pfx + f"wk{s}")
                wv = load_weight_bf16(ctx, prog, ph, W["a1_wv"], D, 256, pfx + "wv",
                                      pfx + f"wv{s}")
                xn2 = sbuf(ctx, ph, "g_xn2", [128, KC * TT], BF16)
                sqs = sbuf(ctx, ph, "g_sqs", [128, KC * TT], BF16)
                rstd2 = sbuf(ctx, ph, "g_rstd2", [128, TT], F32)
                tpl = sbuf(ctx, ph, "g_tpl", [128, TT], F32)
                ps3 = psumt(ctx, ph, "g_ps3")
                pj = [psumt(ctx, ph, "g_pj") for _ in range(4)]
                epst, epsk = ctx.constcol(prog, RMS_EPS)

                for ti in range(NT):
                    kp = f"{spfx}t{ti}_"
                    emit_ln_tile(ctx, prog, lnb, src[s, ti], f"xr:{s}:{ti}", 1e-5,
                                 kp, xnf, ti * TT, pfx + "xn")
                    prog.op("vector", lambda e, ti=ti: [e.tensor_mul(
                            sqs[:, c * TT:(c + 1) * TT],
                            xnf[:, c * T + ti * TT: c * T + ti * TT + TT],
                            xnf[:, c * T + ti * TT: c * T + ti * TT + TT]) for c in range(KC)],
                            reads=[pfx + "xn"], writes=[pfx + "sqs"], tag=kp + "rsq")
                    prog.op("tensor", lambda e: [e.matmul(ps3[:], ctx.ones128[:],
                                                          sqs[:, c * TT:(c + 1) * TT],
                                                          start=(c == 0), stop=(c == KC - 1))
                                                 for c in range(KC)],
                            reads=[pfx + "sqs", "c:ones"], writes=[pfx + "ps3"], tag=kp + "rmm")
                    prog.op("vector", lambda e: e.tensor_scalar(tpl[:], ps3[:], 1.0 / D, None, OP.mult),
                            reads=[pfx + "ps3"], writes=[pfx + "tpl"], tag=kp + "rsc")
                    prog.op("scalar", lambda e: e.activation(tpl[:], tpl[:], AF.Ln, bias=epst[:]),
                            reads=[pfx + "tpl"], early=[epsk], writes=[pfx + "tpl"], tag=kp + "rln")
                    prog.op("scalar", lambda e: e.activation(rstd2[:], tpl[:], AF.Exp, scale=-0.5),
                            reads=[pfx + "tpl"], writes=[pfx + "rstd2"], tag=kp + "rexp")
                    prog.op("vector", lambda e, ti=ti: [e.tensor_mul(
                            xn2[:, c * TT:(c + 1) * TT],
                            xnf[:, c * T + ti * TT: c * T + ti * TT + TT], rstd2[:])
                            for c in range(KC)],
                            reads=[pfx + "xn", pfx + "rstd2"], writes=[pfx + "xn2"],
                            tag=kp + "rmul")
                    prog.op("vector", lambda e: [e.tensor_scalar(
                            xn2[:, c * TT:(c + 1) * TT], xn2[:, c * TT:(c + 1) * TT],
                            normw[:, c:c + 1], None, OP.mult) for c in range(KC)],
                            reads=[pfx + "xn2"], early=[pfx + "nw"], writes=[pfx + "xn2"],
                            tag=kp + "rnw")
                    for which, wbuf, wkey, nmo, obuf, okey in (
                            ("q", wq, pfx + "wq", KC, qb, pfx + "Q"),
                            ("k", wk, pfx + "wk", 2, kb, pfx + "K")):
                        md = D if which == "q" else 256
                        for mo in range(nmo):
                            bank = mo % 4
                            def mm(e, wbuf=wbuf, mo=mo, bank=bank, md=md):
                                return [e.matmul(pj[bank][:],
                                                 wbuf[:, c * md + mo * 128: c * md + mo * 128 + 128],
                                                 xn2[:, c * TT:(c + 1) * TT],
                                                 start=(c == 0), stop=(c == KC - 1))
                                        for c in range(KC)]
                            prog.op("tensor", mm, reads=[wkey, pfx + "xn2"],
                                    writes=[f"{pfx}pj{bank}"], tag=kp + which + f"mm{mo}")
                            def ev(e, mo=mo, bank=bank, obuf=obuf, ti=ti):
                                return e.tensor_mul(obuf[:, mo * T + ti * TT: mo * T + ti * TT + TT],
                                                    pj[bank][:], ropepl[:, ti * TT:(ti + 1) * TT])
                            prog.op("vector", ev, reads=[f"{pfx}pj{bank}", pfx + "rope"],
                                    writes=[okey], tag=kp + which + f"ev{mo}")
                    for st in range(TT // 128):
                        bank = st % 4
                        def mmv(e, st=st, bank=bank, ti=ti):
                            return [e.matmul(pj[bank][0:128, 0:256],
                                             xn2[:, c * TT + st * 128: c * TT + st * 128 + 128],
                                             wv[:, c * 256:(c + 1) * 256],
                                             start=(c == 0), stop=(c == KC - 1))
                                    for c in range(KC)]
                        prog.op("tensor", mmv, reads=[pfx + "wv", pfx + "xn2"],
                                writes=[f"{pfx}pj{bank}"], tag=kp + f"vmm{st}")
                        tok = ti * (TT // 128) + st
                        prog.op("vector", lambda e, tok=tok, bank=bank: e.tensor_copy(
                                vb[:, tok * 256:(tok + 1) * 256], pj[bank][0:128, 0:256]),
                                reads=[f"{pfx}pj{bank}"], writes=[pfx + "V"],
                                tag=kp + f"vev{st}")
                prog.fence()

            with ExitStack() as ph:
                wo = load_weight_bf16(ctx, prog, ph, W["a1_wo"], D, D, pfx + "wo",
                                      pfx + f"wo{s}")
                scores = sbuf(ctx, ph, "g_sc", [128, T], F32)
                pbuf = [sbuf(ctx, ph, f"g_p{i}", [128, T], BF16) for i in range(2)]
                ptb = sbuf(ctx, ph, "g_pt", [128, T], BF16)
                mxc = sbuf(ctx, ph, "g_mx", [128, 1], F32)
                negmax = sbuf(ctx, ph, "g_nm", [128, 1], F32)
                lrow = sbuf(ctx, ph, "g_l", [128, 1], F32)
                linv = sbuf(ctx, ph, "g_li", [128, 1], F32)
                btile = [sbuf(ctx, ph, "g_bias", [128, T], F32) for _ in range(2)]
                xrt = sbuf(ctx, ph, "g_xr", [128, KC * TT], F32)
                sc_ps = [psumt(ctx, ph, "g_scps") for _ in range(2)]
                pt_ps = [psumt(ctx, ph, "g_ptps", BF16) for _ in range(2)]
                av_ps = [psumt(ctx, ph, "g_avps") for _ in range(2)]
                oj_ps = [psumt(ctx, ph, "g_ojps") for _ in range(2)]

                def stageA(qt, h):
                    hc, hp = h // 2, 64 * (h % 2)
                    kvh = h % 4
                    kc_, kp_ = kvh // 2, 64 * (kvh % 2)
                    bk = qt % 2
                    nkc = qt // 4 + 1
                    pb = pbuf[h % 2]
                    pk = f"{pfx}P{h % 2}"
                    hkp = f"{spfx}q{qt}h{h}_"
                    if h == 0:
                        prog.op("sync", lambda e, qt=qt, bk=bk, nkc=nkc: e.dma_start(
                                out=btile[bk][:, 0:nkc * 512],
                                in_=C["a1_bias"][qt * 128:(qt + 1) * 128, 0:nkc * 512]),
                                writes=[f"{pfx}bias{bk}"], dma=True, tag=f"{spfx}b{qt}")
                    for j in range(nkc):
                        sb_ = j % 2
                        def smm(e, hc=hc, hp=hp, qt=qt, j=j, sb_=sb_, kc_=kc_, kp_=kp_):
                            return e.matmul(sc_ps[sb_][:],
                                            qb[hp:hp + 64, hc * T + qt * 128: hc * T + qt * 128 + 128],
                                            kb[kp_:kp_ + 64, kc_ * T + j * 512: kc_ * T + j * 512 + 512],
                                            start=True, stop=True)
                        prog.op("tensor", smm, reads=[pfx + "Q", pfx + "K"],
                                writes=[f"{pfx}scps{sb_}"], tag=hkp + f"smm{j}")
                        def sev(e, j=j, sb_=sb_, bk=bk):
                            return e.scalar_tensor_tensor(
                                scores[:, j * 512:(j + 1) * 512], sc_ps[sb_][:],
                                0.125, btile[bk][:, j * 512:(j + 1) * 512],
                                OP.mult, OP.add)
                        prog.op("vector", sev,
                                reads=[f"{pfx}scps{sb_}", f"{pfx}bias{bk}"],
                                writes=[pfx + "scores"], tag=hkp + f"sev{j}")
                    prog.op("vector", lambda e, nkc=nkc: e.reduce_max(
                            negmax[:], scores[:, 0:nkc * 512], AX, negate=True),
                            reads=[pfx + "scores"], writes=[pfx + "negmax"], tag=hkp + "neg")
                    prog.op("scalar", lambda e, nkc=nkc, pb=pb: e.activation(
                            pb[:, 0:nkc * 512], scores[:, 0:nkc * 512], AF.Exp,
                            bias=negmax[:], accum_out=lrow[:]),
                            reads=[pfx + "scores"], early=[pfx + "negmax"],
                            writes=[pk, pfx + "l"], tag=hkp + "exp")
                    prog.op("vector", lambda e: e.reciprocal(linv[:], lrow[:]),
                            reads=[pfx + "l"], writes=[pfx + "linv"], tag=hkp + "rcp")
                    prog.op("vector", lambda e, nkc=nkc, pb=pb: e.tensor_scalar(
                            pb[:, 0:nkc * 512], pb[:, 0:nkc * 512], linv[:], None, OP.mult),
                            reads=[pk], early=[pfx + "linv"], writes=[pk],
                            tag=hkp + "pscale")

                def stageB(qt, h):
                    hc, hp = h // 2, 64 * (h % 2)
                    kvh = h % 4
                    qq = qt % 4
                    nkt = (qt // 4 + 1) * 4
                    pb = pbuf[h % 2]
                    pk = f"{pfx}P{h % 2}"
                    hkp = f"{spfx}q{qt}h{h}_"
                    nh_half = (nkt + 1) // 2
                    for half in range(2):
                        j0 = half * nh_half
                        j1 = min(nkt, j0 + nh_half)
                        if j0 >= j1:
                            continue
                        def tp(e, j0=j0, j1=j1, half=half, pb=pb):
                            return [e.transpose(pt_ps[half][:, (j - j0) * 128:(j - j0 + 1) * 128],
                                                pb[:, j * 128:(j + 1) * 128],
                                                ctx.ident[:])
                                    for j in range(j0, j1)]
                        prog.op("tensor", tp, reads=[pk, "c:ident"],
                                writes=[f"{pfx}ptps{half}"], tag=hkp + f"tp{half}")
                        def tpe(e, j0=j0, j1=j1, half=half):
                            return e.copy(ptb[:, j0 * 128:j1 * 128],
                                          pt_ps[half][:, 0:(j1 - j0) * 128])
                        prog.op("scalar", tpe, reads=[f"{pfx}ptps{half}"],
                                writes=[pfx + "Pt"], tag=hkp + f"tpe{half}")
                    ab = h % 2
                    def av(e, kvh=kvh, ab=ab, nkt=nkt):
                        return [e.matmul(av_ps[ab][0:64, 0:128],
                                         vb[:, tc * 256 + kvh * 64: tc * 256 + kvh * 64 + 64],
                                         ptb[:, tc * 128:(tc + 1) * 128],
                                         start=(tc == 0), stop=(tc == nkt - 1))
                                for tc in range(nkt)]
                    prog.op("tensor", av, reads=[pfx + "Pt", pfx + "V"],
                            writes=[f"{pfx}avps{ab}"], tag=hkp + "av")
                    def ave(e, hc=hc, hp=hp, ab=ab, qq=qq):
                        return e.copy(
                            attn_out[hp:hp + 64, hc * TT + qq * 128: hc * TT + qq * 128 + 128],
                            av_ps[ab][0:64, 0:128])
                    prog.op("scalar", ave, reads=[f"{pfx}avps{ab}"],
                            writes=[pfx + "ao"], tag=hkp + "ave")

                def phaseC(g):
                    prog.op("sync", lambda e, g=g, s=s: e.dma_start(out=cm3(xrt[:], TT),
                                                                    in_=fm3(src[s, g], TT)),
                            reads=[f"xr:{s}:{g}"], writes=[pfx + "xrt"], dma=True,
                            tag=f"{spfx}xr{g}")
                    prog.op("vector", lambda e, g=g: [e.tensor_add(
                            xrt[:, c * TT:(c + 1) * TT], xrt[:, c * TT:(c + 1) * TT],
                            xnf[:, c * T + g * TT: c * T + g * TT + TT]) for c in range(KC)],
                            reads=[pfx + "xn", pfx + "xrt"], writes=[pfx + "xrt"],
                            tag=f"{spfx}lnadd{g}")
                    for mo in range(KC):
                        ob = mo % 2
                        def omm(e, mo=mo, ob=ob):
                            return [e.matmul(oj_ps[ob][:],
                                             wo[:, c * D + mo * 128: c * D + mo * 128 + 128],
                                             attn_out[:, c * TT:(c + 1) * TT],
                                             start=(c == 0), stop=(c == KC - 1))
                                    for c in range(KC)]
                        prog.op("tensor", omm, reads=[pfx + "wo", pfx + "ao"],
                                writes=[f"{pfx}ojps{ob}"], tag=f"{spfx}omm{g}_{mo}")
                        def oev(e, mo=mo, ob=ob):
                            return e.tensor_add(xrt[:, mo * TT:(mo + 1) * TT],
                                                xrt[:, mo * TT:(mo + 1) * TT], oj_ps[ob][:])
                        prog.op("vector", oev, reads=[f"{pfx}ojps{ob}", pfx + "xrt"],
                                writes=[pfx + "xrt"], tag=f"{spfx}oev{g}_{mo}")
                    prog.op("sync", lambda e, g=g, s=s: e.dma_start(out=fm3(dst[s, g], TT),
                                                                    in_=cm3(xrt[:], TT)),
                            reads=[pfx + "xrt"], writes=[f"xr:{s}:{g}"],
                            dma=True, tag=f"{spfx}st{g}")

                NQ = T // 128
                items = [(qt, h) for qt in range(NQ) for h in range(16)]
                prev = None
                for it in items + [None]:
                    if it is not None:
                        stageA(*it)
                    if prev is not None:
                        stageB(*prev)
                        if prev[1] == 15 and prev[0] % 4 == 3:
                            phaseC(prev[0] // 4)
                    prev = it
                prog.fence()
        prog.fence()



def emit_mamba(ctx, prog, src, dst, W, C, sidx):
    nc, T, NT, n_seq = ctx.nc, ctx.T, ctx.NT, ctx.n_seq
    pfx = f"s{sidx}b_"
    NCH = 18
    DI = 2048
    IPW = 2336
    CS = 515
    RMS_EPS = 1.1920929e-07
    mbxn = nc.dram_tensor(ctx.name("mbxn"), [n_seq, NT, D, TT], BF16,
                          kind="Internal").ap()
    mbz = nc.dram_tensor(ctx.name("mbz"), [n_seq, NT, DI, TT], BF16,
                         kind="Internal").ap()

    with ExitStack() as sc:
        ipw = load_weight_bf16(ctx, prog, sc, W["mb_ipwr"], D, IPW, pfx + "ipw", pfx + "ipw")
        triu = sbuf(ctx, sc, "b_triu", [128, 128], F32)
        tribf = sbuf(ctx, sc, "b_tribf", [128, 128], BF16)
        sel32 = sbuf(ctx, sc, "b_sel", [32, 32 * 128], F32)
        convw = sbuf(ctx, sc, "b_cw", [128, NCH * 4], F32)
        convb = sbuf(ctx, sc, "b_cb", [128, NCH], F32)
        dtbpl = sbuf(ctx, sc, "b_dtb", [128, 32], F32)
        apl = sbuf(ctx, sc, "b_apl", [128, 32], F32)
        dpl = sbuf(ctx, sc, "b_dpl", [128, 32], F32)
        gnw = sbuf(ctx, sc, "b_gnw", [128, 16], F32)
        nw = sbuf(ctx, sc, "b_nw", [128, KC], F32)
        ones1r = sbuf(ctx, sc, "b_o1r", [1, 128], F32)
        row32 = sbuf(ctx, sc, "b_r32", [1, 32], F32)
        pfcol = sbuf(ctx, sc, "b_pf", [128, 1], F32)
        pmcol = sbuf(ctx, sc, "b_pm", [128, 1], F32)
        state = sbuf(ctx, sc, "b_S", [64, 32 * 64], F32)
        sbf = sbuf(ctx, sc, "b_Sbf", [64, 64], BF16)
        xmsum = sbuf(ctx, sc, "b_xms", [128, KC], F32)
        tred = sbuf(ctx, sc, "b_tred", [128, 1], F32)
        initsc = ExitStack()
        gps = psumt(ctx, initsc, "b_gps")

        for nm, buf in (("c_triu", triu), ("c_sel32", sel32)):
            prog.op("sync", lambda e, buf=buf, nm=nm: e.dma_start(out=buf[:], in_=C[nm][:]),
                    writes=[pfx + nm], dma=True, tag=pfx + "ld" + nm)
        prog.op("gpsimd", lambda e: e.dma_start(out=tribf[:], in_=C["c_triu"][:]),
                writes=[pfx + "c_tri_bf"], dma=True, tag=pfx + "ldtribf")
        for nm, buf in (("mb_convw", convw), ("mb_convb", convb), ("mb_gnw", gnw),
                        ("mb_nw", nw)):
            prog.op("sync", lambda e, buf=buf, nm=nm: e.dma_start(out=buf[:], in_=W[nm][:]),
                    writes=[pfx + nm], dma=True, tag=pfx + "ld" + nm)
        prog.op("sync", lambda e: e.dma_start(out=ones1r[:], in_=C["c_ones1r"][:]),
                writes=[pfx + "o1r"], dma=True, tag=pfx + "ldo1r")

        def repl32(wname, dstbuf, dkey, transform=None):
            prog.op("sync", lambda e: e.dma_start(out=row32[:], in_=W[wname][:]),
                    writes=[pfx + "r32"], dma=True, tag=pfx + "ldr_" + wname)
            if transform == "negexp":
                prog.op("scalar", lambda e: e.activation(row32[:], row32[:], AF.Exp),
                        reads=[pfx + "r32"], writes=[pfx + "r32"], tag=pfx + "nx1" + wname)
                prog.op("vector", lambda e: e.tensor_scalar(row32[:], row32[:], -1.0, None, OP.mult),
                        reads=[pfx + "r32"], writes=[pfx + "r32"], tag=pfx + "nx2" + wname)
            prog.op("tensor", lambda e: e.matmul(gps[0:128, 0:32], ones1r[:], row32[:],
                                                 start=True, stop=True),
                    reads=[pfx + "o1r", pfx + "r32"], writes=[pfx + "gps"], tag=pfx + "rp_" + wname)
            prog.op("vector", lambda e: e.tensor_copy(dstbuf[:], gps[0:128, 0:32]),
                    reads=[pfx + "gps"], writes=[dkey], tag=pfx + "rpe_" + wname)
        repl32("mb_dtb", dtbpl, pfx + "dtb")
        repl32("mb_alog", apl, pfx + "apl", transform="negexp")
        repl32("mb_dpar", dpl, pfx + "dpl")
        prog.fence()
        initsc.close()

        for s in range(n_seq):
            spfx = f"{pfx}{s}_"
            prog.op("vector", lambda e: [e.memset(state[:, i * 512:(i + 1) * 512], 0.0)
                                         for i in range(4)],
                    writes=[pfx + "S"], tag=spfx + "Sz")
            prog.op("vector", lambda e: e.memset(xmsum[:], 0.0),
                    writes=[pfx + "xms"], tag=spfx + "xmz")
            with ExitStack() as ph:
                lnb = make_ln_bufs(ctx, ph)
                xnt = sbuf(ctx, ph, "b_xnt", [128, KC * TT], BF16)
                sqs = sbuf(ctx, ph, "b_sqs", [128, KC * TT], BF16)
                rstd2 = sbuf(ctx, ph, "b_rstd2", [128, TT], F32)
                tpl = sbuf(ctx, ph, "b_tpl", [128, TT], F32)
                xr0 = sbuf(ctx, ph, "b_xr0", [128, KC * TT], F32)
                ipwz = load_weight_bf16(ctx, prog, ph, W["mb_ipwz"], D, DI,
                                        pfx + "ipwz", pfx + f"ipwz{s}")
                zt1 = sbuf(ctx, ph, "b_zt1", [128, 16 * TT], BF16)
                ps3 = psumt(ctx, ph, "b_ps3")
                pjz = [psumt(ctx, ph, "b_pjz") for _ in range(2)]
                epst, epsk = ctx.constcol(prog, RMS_EPS)
                for ti in range(NT):
                    kp = f"{spfx}t{ti}_"
                    emit_ln_tile(ctx, prog, lnb, src[s, ti], f"xr:{s}:{ti}", 1e-5,
                                 kp, xnt, 0, pfx + "xnt", T_out=TT)
                    KB = lnb["k"]
                    xt_, meanpl_ = lnb["xt"], lnb["mean"]
                    prog.op("vector", lambda e: [e.tensor_add(
                            xt_[:, c * TT:(c + 1) * TT], xt_[:, c * TT:(c + 1) * TT],
                            meanpl_[:]) for c in range(KC)],
                            reads=[KB + "xt", KB + "mean"], writes=[KB + "xt"],
                            tag=kp + "unmean")
                    prog.op("vector", lambda e: [e.tensor_add(
                            xr0[:, c * TT:(c + 1) * TT], xt_[:, c * TT:(c + 1) * TT],
                            xnt[:, c * TT:(c + 1) * TT]) for c in range(KC)],
                            reads=[KB + "xt", pfx + "xnt"], writes=[pfx + "xr0"],
                            tag=kp + "lnadd")
                    prog.op("sync", lambda e, ti=ti, s=s: e.dma_start(
                            out=fm3(dst[s, ti], TT), in_=cm3(xr0[:], TT)),
                            reads=[pfx + "xr0"], writes=[f"xr:{s}:{ti}"], dma=True,
                            tag=kp + "st0")
                    prog.op("vector", lambda e: [e.tensor_mul(
                            sqs[:, c * TT:(c + 1) * TT], xnt[:, c * TT:(c + 1) * TT],
                            xnt[:, c * TT:(c + 1) * TT]) for c in range(KC)],
                            reads=[pfx + "xnt"], writes=[pfx + "sqs"], tag=kp + "rsq")
                    prog.op("tensor", lambda e: [e.matmul(ps3[:], ctx.ones128[:],
                                                          sqs[:, c * TT:(c + 1) * TT],
                                                          start=(c == 0), stop=(c == KC - 1))
                                                 for c in range(KC)],
                            reads=[pfx + "sqs", "c:ones"], writes=[pfx + "ps3"], tag=kp + "rmm")
                    prog.op("vector", lambda e: e.tensor_scalar(tpl[:], ps3[:], 1.0 / D, None, OP.mult),
                            reads=[pfx + "ps3"], writes=[pfx + "tpl"], tag=kp + "rsc")
                    prog.op("scalar", lambda e: e.activation(tpl[:], tpl[:], AF.Ln, bias=epst[:]),
                            reads=[pfx + "tpl"], early=[epsk], writes=[pfx + "tpl"], tag=kp + "rln")
                    prog.op("scalar", lambda e: e.activation(rstd2[:], tpl[:], AF.Exp, scale=-0.5),
                            reads=[pfx + "tpl"], writes=[pfx + "rstd2"], tag=kp + "rexp")
                    prog.op("vector", lambda e: [e.tensor_mul(
                            xnt[:, c * TT:(c + 1) * TT], xnt[:, c * TT:(c + 1) * TT],
                            rstd2[:]) for c in range(KC)],
                            reads=[pfx + "xnt", pfx + "rstd2"], writes=[pfx + "xnt"],
                            tag=kp + "rmul")
                    prog.op("vector", lambda e: [e.tensor_scalar(
                            xnt[:, c * TT:(c + 1) * TT], xnt[:, c * TT:(c + 1) * TT],
                            nw[:, c:c + 1], None, OP.mult) for c in range(KC)],
                            reads=[pfx + "xnt"], early=[pfx + "mb_nw"], writes=[pfx + "xnt"],
                            tag=kp + "rnw")
                    prog.op("sync", lambda e, ti=ti, s=s: e.dma_start(
                            out=fm3(mbxn[s, ti], TT), in_=cm3(xnt[:], TT)),
                            reads=[pfx + "xnt"], writes=[f"{pfx}mbxn:{s}:{ti}"], dma=True,
                            tag=kp + "xnst")
                    prog.op("vector", lambda e: [e.reduce_sum(tred[:] if c == 0 else tred[:],
                                                              xnt[:, c * TT:(c + 1) * TT], AX)
                                                 for c in range(0)],
                            tag=kp + "nop") if False else None
                    for c in range(KC):
                        prog.op("vector", lambda e, c=c: e.reduce_sum(
                                tred[:], xnt[:, c * TT:(c + 1) * TT], AX),
                                reads=[pfx + "xnt"], writes=[pfx + "tred"], tag=kp + f"mr{c}")
                        prog.op("vector", lambda e, c=c: e.tensor_add(
                                xmsum[:, c:c + 1], xmsum[:, c:c + 1], tred[:]),
                                reads=[pfx + "tred", pfx + "xms"], writes=[pfx + "xms"],
                                tag=kp + f"ma{c}")
                    for mo in range(16):
                        bank = mo % 2
                        def mmz(e, mo=mo, bank=bank):
                            return [e.matmul(pjz[bank][:],
                                             ipwz[:, c * DI + mo * 128: c * DI + mo * 128 + 128],
                                             xnt[:, c * TT:(c + 1) * TT],
                                             start=(c == 0), stop=(c == KC - 1))
                                    for c in range(KC)]
                        prog.op("tensor", mmz, reads=[pfx + "ipwz", pfx + "xnt"],
                                writes=[f"{pfx}pjz{bank}"], tag=kp + f"zmm{mo}")
                        prog.op("scalar", lambda e, mo=mo, bank=bank: e.activation(
                                zt1[:, mo * TT:(mo + 1) * TT], pjz[bank][:], AF.Silu),
                                reads=[f"{pfx}pjz{bank}"], writes=[pfx + "zt1"],
                                tag=kp + f"zev{mo}")
                    prog.op("sync", lambda e, ti=ti, s=s: e.dma_start(
                            out=fm3(mbz[s, ti], TT), in_=cm3(zt1[:], TT)),
                            reads=[pfx + "zt1"], writes=[f"{pfx}mbz:{s}:{ti}"], dma=True,
                            tag=kp + "zst")
                prog.fence()

            with ExitStack() as ph:
                xmb = sbuf(ctx, ph, "b_xmb", [128, KC], BF16)
                rw1 = load_weight_bf16(ctx, prog, ph, W["mb_rw1"], D, 32, pfx + "rw1",
                                       pfx + f"rw1{s}")
                rw2 = sbuf(ctx, ph, "b_rw2", [32, 3], F32)
                rb1 = sbuf(ctx, ph, "b_rb1", [32, 1], F32)
                rb2 = sbuf(ctx, ph, "b_rb2", [1, 3], F32)
                hrow = sbuf(ctx, ph, "b_h", [32, 1], F32)
                probs = sbuf(ctx, ph, "b_probs", [1, 3], F32)
                pnm = sbuf(ctx, ph, "b_pnm", [1, 1], F32)
                pl_ = sbuf(ctx, ph, "b_pl", [1, 1], F32)
                pli = sbuf(ctx, ph, "b_pli", [1, 1], F32)
                pm1 = sbuf(ctx, ph, "b_pm1", [1, 1], F32)
                rps = psumt(ctx, ph, "b_rps")
                prog.op("sync", lambda e: e.dma_start(out=rw2[:], in_=W["mb_rw2"][:]),
                        writes=[pfx + "rw2"], dma=True, tag=spfx + "ldrw2")
                prog.op("sync", lambda e: e.dma_start(out=rb1[:], in_=W["mb_rb1"][:]),
                        writes=[pfx + "rb1"], dma=True, tag=spfx + "ldrb1")
                prog.op("sync", lambda e: e.dma_start(out=rb2[:], in_=W["mb_rb2"][:]),
                        writes=[pfx + "rb2"], dma=True, tag=spfx + "ldrb2")
                prog.op("vector", lambda e: e.tensor_scalar(xmb[:], xmsum[:], 1.0 / T, None, OP.mult),
                        reads=[pfx + "xms"], writes=[pfx + "xmb"], tag=spfx + "xmsc")
                prog.op("tensor", lambda e: [e.matmul(rps[0:32, 0:1],
                                                      rw1[:, c * 32:(c + 1) * 32],
                                                      xmb[:, c:c + 1],
                                                      start=(c == 0), stop=(c == KC - 1))
                                             for c in range(KC)],
                        reads=[pfx + "rw1", pfx + "xmb"], writes=[pfx + "rps"], tag=spfx + "r1mm")
                prog.op("vector", lambda e: e.tensor_add(hrow[:], rps[0:32, 0:1], rb1[:]),
                        reads=[pfx + "rps", pfx + "rb1"], writes=[pfx + "h"], tag=spfx + "r1b")
                prog.op("scalar", lambda e: e.activation(hrow[:], hrow[:], AF.Silu),
                        reads=[pfx + "h"], writes=[pfx + "h"], tag=spfx + "r1silu")
                prog.op("tensor", lambda e: e.matmul(rps[0:1, 0:3], hrow[:], rw2[:],
                                                     start=True, stop=True),
                        reads=[pfx + "h", pfx + "rw2"], writes=[pfx + "rps"], tag=spfx + "r2mm")
                prog.op("vector", lambda e: e.tensor_add(probs[:], rps[0:1, 0:3], rb2[:]),
                        reads=[pfx + "rps", pfx + "rb2"], writes=[pfx + "probs"], tag=spfx + "r2b")
                prog.op("vector", lambda e: e.reduce_max(pnm[:], probs[:], AX, negate=True),
                        reads=[pfx + "probs"], writes=[pfx + "pnm"], tag=spfx + "pmax")
                prog.op("scalar", lambda e: e.activation(probs[:], probs[:], AF.Exp,
                                                         bias=pnm[:], accum_out=pl_[:]),
                        reads=[pfx + "probs"], early=[pfx + "pnm"],
                        writes=[pfx + "probs", pfx + "plsum"], tag=spfx + "pexp")
                prog.op("vector", lambda e: e.reciprocal(pli[:], pl_[:]),
                        reads=[pfx + "plsum"], writes=[pfx + "pli"], tag=spfx + "prcp")
                prog.op("vector", lambda e: e.tensor_scalar(probs[:], probs[:], pli[:], None, OP.mult),
                        reads=[pfx + "probs"], early=[pfx + "pli"], writes=[pfx + "probs"],
                        tag=spfx + "pscale")
                prog.op("vector", lambda e: e.reduce_sum(pm1[:], probs[:, 1:3], AX),
                        reads=[pfx + "probs"], writes=[pfx + "pm1"], tag=spfx + "pmsum")
                prog.op("tensor", lambda e: e.matmul(rps[0:128, 0:1], ones1r[:], probs[0:1, 0:1],
                                                     start=True, stop=True),
                        reads=[pfx + "o1r", pfx + "probs"], writes=[pfx + "rps"], tag=spfx + "pfmm")
                prog.op("vector", lambda e: e.tensor_copy(pfcol[:], rps[0:128, 0:1]),
                        reads=[pfx + "rps"], writes=[pfx + "pfc"], tag=spfx + "pfe")
                prog.op("tensor", lambda e: e.matmul(rps[0:128, 0:1], ones1r[:], pm1[:],
                                                     start=True, stop=True),
                        reads=[pfx + "o1r", pfx + "pm1"], writes=[pfx + "rps"], tag=spfx + "pmmm")
                prog.op("vector", lambda e: e.tensor_copy(pmcol[:], rps[0:128, 0:1]),
                        reads=[pfx + "rps"], writes=[pfx + "pmc"], tag=spfx + "pme")
                prog.fence()

            with ExitStack() as ph:
                xnt2 = sbuf(ctx, ph, "b_xnt2", [128, KC * TT], BF16)
                zts = sbuf(ctx, ph, "b_zt", [128, 16 * TT], BF16)
                xbc = sbuf(ctx, ph, "b_xbc", [128, NCH * CS], BF16)
                hsv = sbuf(ctx, ph, "b_hsv", [128, NCH * 3], BF16)
                dtt = sbuf(ctx, ph, "b_dt", [128, 4 * 32], F32)
                dtp = sbuf(ctx, ph, "b_dtp", [128, 32], F32)
                dta = sbuf(ctx, ph, "b_dta", [128, 32], F32)
                ltok = sbuf(ctx, ph, "b_ltok", [128, 32], F32)
                lfm = sbuf(ctx, ph, "b_lfm", [32, 128], F32)
                gsb_ = sbuf(ctx, ph, "b_G", [128, 128], BF16)
                dcu = sbuf(ctx, ph, "b_dcu", [128, 128], F32)
                epl = sbuf(ctx, ph, "b_epl", [128, 128], F32)
                dcm = sbuf(ctx, ph, "b_dcm", [128, 128], BF16)
                msb = sbuf(ctx, ph, "b_M", [128, 128], BF16)
                cesb = sbuf(ctx, ph, "b_Ce", [64, 128], BF16)
                xtok = sbuf(ctx, ph, "b_xtok", [128, 16 * 128], BF16)
                btok = sbuf(ctx, ph, "b_btok", [128, 64], BF16)
                bw = sbuf(ctx, ph, "b_bw", [128, 64], BF16)
                ybuf = sbuf(ctx, ph, "b_y", [128, 16 * TT], BF16)
                gpl2 = sbuf(ctx, ph, "b_gpl2", [128, TT], F32)
                xrt = sbuf(ctx, ph, "b_xrt", [128, KC * TT], F32)
                t512 = sbuf(ctx, ph, "b_t512", [128, TT], F32)
                t512b = sbuf(ctx, ph, "b_t512b", [128, TT], F32)
                ysqb = [sbuf(ctx, ph, f"b_ysq{i}", [128, TT], BF16) for i in range(2)]
                opw = load_weight_bf16(ctx, prog, ph, W["mb_opw"], DI, D, pfx + "opw",
                                       pfx + f"opw{s}")
                pj = [psumt(ctx, ph, "b_pj") for _ in range(2)]
                ptp = [psumt(ctx, ph, "b_ptp", BF16) for _ in range(2)]
                pyy = [psumt(ctx, ph, "b_pyy")]
                pst_ = [psumt(ctx, ph, "b_pst")]
                plr = [psumt(ctx, ph, "b_plr") for _ in range(2)]
                xbk = pfx + "xbc"

                for ti in range(NT):
                    kp = f"{spfx}t{ti}_"
                    prog.op("sync", lambda e, ti=ti, s=s: e.dma_start(
                            out=cm3(xnt2[:], TT), in_=fm3(mbxn[s, ti], TT)),
                            reads=[f"{pfx}mbxn:{s}:{ti}"], writes=[pfx + "xnt2"], dma=True,
                            tag=kp + "xnld")
                    if ti == 0:
                        prog.op("vector", lambda e: [e.memset(
                                xbc[:, c * CS:c * CS + 3], 0.0) for c in range(NCH)],
                                writes=[xbk + "h"], tag=kp + "halo0")
                    else:
                        prog.op("vector", lambda e: [e.tensor_copy(
                                xbc[:, c * CS:c * CS + 3], hsv[:, c * 3:(c + 1) * 3])
                                for c in range(NCH)],
                                reads=[pfx + "hsv"], writes=[xbk + "h"], tag=kp + "halo")
                    prog.op("sync", lambda e, ti=ti, s=s: e.dma_start(
                            out=cm3(zts[:], TT), in_=fm3(mbz[s, ti], TT)),
                            reads=[f"{pfx}mbz:{s}:{ti}"], writes=[pfx + "zt"], dma=True,
                            tag=kp + "zld")
                    for mo in range(18):
                        bank = mo % 2
                        def mmip(e, mo=mo, bank=bank):
                            return [e.matmul(pj[bank][:],
                                             ipw[:, c * IPW + mo * 128: c * IPW + mo * 128 + 128],
                                             xnt2[:, c * TT:(c + 1) * TT],
                                             start=(c == 0), stop=(c == KC - 1))
                                    for c in range(KC)]
                        prog.op("tensor", mmip, reads=[pfx + "ipw", pfx + "xnt2"],
                                writes=[f"{pfx}pj{bank}"], tag=kp + f"ipmm{mo}")
                        cch = mo
                        def evx(e, cch=cch, bank=bank):
                            return e.tensor_copy(xbc[:, cch * CS + 3:cch * CS + 3 + TT],
                                                 pj[bank][:])
                        prog.op("vector", evx, reads=[f"{pfx}pj{bank}"],
                                writes=[xbk], tag=kp + f"ipxe{cch}")
                    for st in range(4):
                        bank = st % 2
                        def mmdt(e, st=st, bank=bank):
                            return [e.matmul(pj[bank][0:128, 0:32],
                                             xnt2[:, c * TT + st * 128: c * TT + st * 128 + 128],
                                             ipw[:, c * IPW + 2304: c * IPW + 2336],
                                             start=(c == 0), stop=(c == KC - 1))
                                    for c in range(KC)]
                        prog.op("tensor", mmdt, reads=[pfx + "ipw", pfx + "xnt2"],
                                writes=[f"{pfx}pj{bank}"], tag=kp + f"dtmm{st}")
                        prog.op("vector", lambda e, st=st, bank=bank: e.tensor_copy(
                                dtt[:, st * 32:(st + 1) * 32], pj[bank][0:128, 0:32]),
                                reads=[f"{pfx}pj{bank}"], writes=[pfx + "dtt"],
                                tag=kp + f"dte{st}")
                    prog.op("vector", lambda e: [e.tensor_copy(
                            hsv[:, c * 3:(c + 1) * 3], xbc[:, c * CS + 512:c * CS + 515])
                            for c in range(NCH)],
                            reads=[xbk], writes=[pfx + "hsv"], tag=kp + "hsave")
                    for cch in range(NCH):
                        tb = t512 if cch % 2 == 0 else t512b
                        tbk = pfx + ("t512" if cch % 2 == 0 else "t512b")
                        def cv0(e, cch=cch, tb=tb):
                            return e.tensor_scalar(tb[:], xbc[:, cch * CS:cch * CS + TT],
                                                   convw[:, cch * 4:cch * 4 + 1], None, OP.mult)
                        prog.op("vector", cv0, reads=[xbk, xbk + "h"],
                                early=[pfx + "mb_convw"],
                                writes=[tbk], tag=kp + f"cv0_{cch}")
                        for kk_ in range(1, 4):
                            def cvk(e, cch=cch, kk_=kk_, tb=tb):
                                return e.scalar_tensor_tensor(
                                    tb[:], xbc[:, cch * CS + kk_:cch * CS + kk_ + TT],
                                    convw[:, cch * 4 + kk_:cch * 4 + kk_ + 1], tb[:],
                                    OP.mult, OP.add)
                            prog.op("vector", cvk, reads=[xbk, xbk + "h", tbk],
                                    early=[pfx + "mb_convw"],
                                    writes=[tbk], tag=kp + f"cv{kk_}_{cch}")
                        def cvs(e, cch=cch, tb=tb):
                            return e.activation(xbc[:, cch * CS + 3:cch * CS + 3 + TT], tb[:],
                                                AF.Silu, bias=convb[:, cch:cch + 1])
                        prog.op("scalar", cvs, reads=[tbk, pfx + "hsv"],
                                early=[pfx + "mb_convb"],
                                writes=[xbk], tag=kp + f"cvs{cch}")
                    for sub in range(4):
                        ck = kp + f"c{sub}_"
                        dsl = slice(sub * 32, sub * 32 + 32)
                        cB = 16 * CS + 3 + sub * 128
                        cC = 17 * CS + 3 + sub * 128
                        prog.op("vector", lambda e, dsl=dsl: e.tensor_add(dtp[:], dtt[:, dsl], dtbpl[:]),
                                reads=[pfx + "dtt", pfx + "dtb"], writes=[pfx + "dtp"], tag=ck + "d1")
                        prog.op("scalar", lambda e: e.activation(dta[:], dtp[:], AF.Abs),
                                reads=[pfx + "dtp"], writes=[pfx + "dta"], tag=ck + "d2")
                        prog.op("scalar", lambda e: e.activation(dta[:], dta[:], AF.Exp, scale=-1.0),
                                reads=[pfx + "dta"], writes=[pfx + "dta"], tag=ck + "d3")
                        one_c, one_k = ctx.constcol(prog, 1.0)
                        prog.op("scalar", lambda e: e.activation(dta[:], dta[:], AF.Ln, bias=one_c[:]),
                                reads=[pfx + "dta"], early=[one_k], writes=[pfx + "dta"], tag=ck + "d4")
                        prog.op("vector", lambda e: e.tensor_scalar(dtp[:], dtp[:], 0.0, None, OP.max),
                                reads=[pfx + "dtp"], writes=[pfx + "dtp"], tag=ck + "d5")
                        prog.op("vector", lambda e: e.tensor_add(dtp[:], dtp[:], dta[:]),
                                reads=[pfx + "dtp", pfx + "dta"], writes=[pfx + "dtp"], tag=ck + "d6")
                        prog.op("vector", lambda e: e.tensor_mul(dta[:], dtp[:], apl[:]),
                                reads=[pfx + "dtp", pfx + "apl"], writes=[pfx + "dta"], tag=ck + "d7")
                        prog.op("tensor", lambda e: e.matmul(pj[0][0:128, 0:32], triu[:], dta[:],
                                                             start=True, stop=True),
                                reads=[pfx + "c_triu", pfx + "dta"], writes=[f"{pfx}pj0"],
                                tag=ck + "lmm")
                        prog.op("vector", lambda e: e.tensor_copy(ltok[:], pj[0][0:128, 0:32]),
                                reads=[f"{pfx}pj0"], writes=[pfx + "ltok"], tag=ck + "le")
                        prog.op("tensor", lambda e: e.matmul(pj[1][0:32, 0:128], dta[:], triu[:],
                                                             start=True, stop=True),
                                reads=[pfx + "c_triu", pfx + "dta"], writes=[f"{pfx}pj1"],
                                tag=ck + "lfmm")
                        prog.op("vector", lambda e: e.tensor_copy(lfm[:], pj[1][0:32, 0:128]),
                                reads=[f"{pfx}pj1"], writes=[pfx + "lfm"], tag=ck + "lfe")
                        prog.op("tensor", lambda e, cB=cB, cC=cC: e.matmul(
                                pj[0][:, 0:128], xbc[0:64, cB:cB + 128],
                                xbc[0:64, cC:cC + 128], start=True, stop=True),
                                reads=[xbk], writes=[f"{pfx}pj0"], tag=ck + "gmm")
                        prog.op("vector", lambda e: e.tensor_copy(gsb_[:], pj[0][:, 0:128]),
                                reads=[f"{pfx}pj0"], writes=[pfx + "G"], tag=ck + "ge")
                        for xc in range(8):
                            bnk = xc % 2
                            def xtp(e, xc=xc, sub=sub, bnk=bnk):
                                return [e.transpose(ptp[bnk][:, j * 128:(j + 1) * 128],
                                                    xbc[:, (xc * 2 + j) * CS + 3 + sub * 128:
                                                        (xc * 2 + j) * CS + 3 + sub * 128 + 128],
                                                    ctx.ident[:]) for j in range(2)]
                            prog.op("tensor", xtp, reads=[xbk, "c:ident"],
                                    writes=[f"{pfx}ptp{bnk}"], tag=ck + f"xtp{xc}")
                            def xte(e, xc=xc, bnk=bnk):
                                insts = []
                                for j in range(2):
                                    for hh in range(2):
                                        h = (xc * 2 + j) * 2 + hh
                                        insts.append(e.tensor_scalar(
                                            xtok[:, h * 64:(h + 1) * 64],
                                            ptp[bnk][:, j * 128 + hh * 64: j * 128 + hh * 64 + 64],
                                            dtp[:, h:h + 1], None, OP.mult))
                                return insts
                            prog.op("vector", xte, reads=[f"{pfx}ptp{bnk}", pfx + "dtp"],
                                    writes=[pfx + "xtok"], tag=ck + f"xte{xc}")
                        prog.op("tensor", lambda e, cB=cB: e.transpose(
                                ptp[0][:, 0:64], xbc[0:64, cB:cB + 128],
                                ctx.ident[0:64, 0:64]),
                                reads=[xbk, "c:ident"], writes=[f"{pfx}ptp0"],
                                tag=ck + "btp")
                        prog.op("vector", lambda e: e.tensor_copy(btok[:], ptp[0][:, 0:64]),
                                reads=[f"{pfx}ptp0"], writes=[pfx + "btok"], tag=ck + "bte")
                        for h in range(32):
                            hk = ck + f"h{h}_"
                            hb = 64 * (h % 2)
                            ycol = (h // 2) * TT + sub * 128
                            lb = h % 2
                            prog.op("tensor", lambda e, h=h, lb=lb: e.matmul(
                                    plr[lb][:, 0:128], sel32[:, h * 128:(h + 1) * 128], lfm[:],
                                    start=True, stop=True),
                                    reads=[pfx + "c_sel32", pfx + "lfm"],
                                    writes=[f"{pfx}plr{lb}"], tag=hk + "lrep")
                            prog.op("vector", lambda e, h=h, lb=lb: e.scalar_tensor_tensor(
                                    dcu[:], plr[lb][:, 0:128], ltok[:, h:h + 1], triu[:],
                                    OP.subtract, OP.mult),
                                    reads=[f"{pfx}plr{lb}", pfx + "ltok", pfx + "c_triu"],
                                    writes=[pfx + "dcu"], tag=hk + "dif")
                            prog.op("scalar", lambda e, lb=lb: e.activation(
                                    epl[:], plr[lb][:, 0:128], AF.Exp),
                                    reads=[f"{pfx}plr{lb}", pfx + "dcu"],
                                    writes=[pfx + "epl"], tag=hk + "eex")
                            prog.op("scalar", lambda e: e.activation(dcu[:], dcu[:], AF.Exp),
                                    reads=[pfx + "dcu"], writes=[pfx + "dcu"], tag=hk + "dex")
                            prog.op("vector", lambda e: e.tensor_mul(dcm[:], dcu[:], tribf[:]),
                                    reads=[pfx + "dcu", pfx + "c_tri_bf"], writes=[pfx + "dcm"],
                                    tag=hk + "dcm")
                            prog.op("vector", lambda e: e.tensor_mul(msb[:], gsb_[:], dcm[:]),
                                    reads=[pfx + "G", pfx + "dcm"], writes=[pfx + "M"],
                                    tag=hk + "m")
                            prog.op("vector", lambda e, cC=cC: e.tensor_mul(
                                    cesb[:], xbc[0:64, cC:cC + 128], epl[0:64, :]),
                                    reads=[xbk, pfx + "epl"], writes=[pfx + "Ce"],
                                    tag=hk + "ce")
                            yb = 0
                            prog.op("vector", lambda e, h=h: e.tensor_copy(
                                    sbf[:], state[:, h * 64:(h + 1) * 64]),
                                    reads=[pfx + "S"], writes=[pfx + "Sbf"], tag=hk + "sbf")
                            def ymm(e, h=h, hb=hb, yb=yb):
                                i1 = e.matmul(pyy[yb][hb:hb + 64, 0:128],
                                              xtok[:, h * 64:(h + 1) * 64], msb[:],
                                              start=True, stop=False)
                                i2 = e.matmul(pyy[yb][hb:hb + 64, 0:128],
                                              sbf[:], cesb[:],
                                              start=False, stop=True)
                                return [i1, i2]
                            prog.op("tensor", ymm, reads=[pfx + "xtok", pfx + "M",
                                                          pfx + "Sbf", pfx + "Ce"],
                                    writes=[f"{pfx}pyy{yb}"], tag=hk + "ymm")
                            def yev(e, h=h, hb=hb, yb=yb, ycol=ycol, sub=sub):
                                return e.scalar_tensor_tensor(
                                    ybuf[hb:hb + 64, ycol:ycol + 128],
                                    xbc[hb:hb + 64, (h // 2) * CS + 3 + sub * 128:
                                        (h // 2) * CS + 3 + sub * 128 + 128],
                                    dpl[hb:hb + 64, h:h + 1], pyy[yb][hb:hb + 64, 0:128],
                                    OP.mult, OP.add)
                            prog.op("vector", yev, reads=[f"{pfx}pyy{yb}", xbk,
                                                          pfx + "dpl"],
                                    writes=[pfx + "y"], tag=hk + "yev")
                            prog.op("vector", lambda e: e.tensor_scalar(
                                    bw[:], btok[:], dcu[:, 127:128], None, OP.mult),
                                    reads=[pfx + "btok", pfx + "dcu"], writes=[pfx + "bw"],
                                    tag=hk + "bw")
                            sb2 = 0
                            prog.op("tensor", lambda e, h=h, sb2=sb2: e.matmul(
                                    pst_[sb2][0:64, 0:64], bw[:],
                                    xtok[:, h * 64:(h + 1) * 64], start=True, stop=True),
                                    reads=[pfx + "bw", pfx + "xtok"],
                                    writes=[f"{pfx}pst{sb2}"], tag=hk + "smm")
                            def sup(e, h=h, sb2=sb2):
                                return e.scalar_tensor_tensor(
                                    state[:, h * 64:(h + 1) * 64],
                                    state[:, h * 64:(h + 1) * 64],
                                    epl[0:64, 127:128],
                                    pst_[sb2][0:64, 0:64], OP.mult, OP.add)
                            prog.op("vector", sup, reads=[f"{pfx}pst{sb2}", pfx + "S",
                                                          pfx + "epl"],
                                    writes=[pfx + "S"], tag=hk + "sup")
                    prog.op("vector", lambda e: [e.tensor_mul(ybuf[:, cc * TT:(cc + 1) * TT],
                                                              ybuf[:, cc * TT:(cc + 1) * TT],
                                                              zts[:, cc * TT:(cc + 1) * TT])
                                                 for cc in range(16)],
                            reads=[pfx + "y", pfx + "zt"], writes=[pfx + "y"], tag=kp + "yg")
                    for cc in range(16):
                        tb = ysqb[cc % 2]
                        tbk = f"{pfx}ysq{cc % 2}"
                        prog.op("vector", lambda e, cc=cc, tb=tb: e.tensor_mul(
                                tb[:], ybuf[:, cc * TT:(cc + 1) * TT],
                                ybuf[:, cc * TT:(cc + 1) * TT]),
                                reads=[pfx + "y"], writes=[tbk], tag=kp + f"ysq{cc}")
                        prog.op("tensor", lambda e, cc=cc, tb=tb: e.matmul(
                                pj[0][:], ctx.ones128[:], tb[:],
                                start=(cc == 0), stop=(cc == 15)),
                                reads=[tbk, "c:ones"], writes=[f"{pfx}pj0"], tag=kp + f"gnmm{cc}")
                    prog.op("vector", lambda e: e.tensor_scalar(gpl2[:], pj[0][:], 1.0 / DI,
                                                                None, OP.mult),
                            reads=[f"{pfx}pj0"], writes=[pfx + "gpl2"], tag=kp + "gnsc")
                    eps2t, eps2k = ctx.constcol(prog, 1e-5)
                    prog.op("scalar", lambda e: e.activation(gpl2[:], gpl2[:], AF.Ln, bias=eps2t[:]),
                            reads=[pfx + "gpl2"], early=[eps2k], writes=[pfx + "gpl2"],
                            tag=kp + "gnln")
                    prog.op("scalar", lambda e: e.activation(gpl2[:], gpl2[:], AF.Exp, scale=-0.5),
                            reads=[pfx + "gpl2"], writes=[pfx + "gpl2"], tag=kp + "gnex")
                    prog.op("vector", lambda e: [e.tensor_mul(ybuf[:, cc * TT:(cc + 1) * TT],
                                                              ybuf[:, cc * TT:(cc + 1) * TT],
                                                              gpl2[:]) for cc in range(16)],
                            reads=[pfx + "y", pfx + "gpl2"], writes=[pfx + "y"], tag=kp + "gnap")
                    prog.op("vector", lambda e: [e.tensor_scalar(ybuf[:, cc * TT:(cc + 1) * TT],
                                                                 ybuf[:, cc * TT:(cc + 1) * TT],
                                                                 gnw[:, cc:cc + 1], None, OP.mult)
                                                 for cc in range(16)],
                            reads=[pfx + "y"], early=[pfx + "mb_gnw"], writes=[pfx + "y"],
                            tag=kp + "gnw")
                    prog.op("sync", lambda e, ti=ti, s=s: e.dma_start(out=cm3(xrt[:], TT),
                                                                      in_=fm3(dst[s, ti], TT)),
                            reads=[f"xr:{s}:{ti}"], writes=[pfx + "xrt"], dma=True,
                            tag=kp + "xrld")
                    for mo in range(KC):
                        bank = mo % 2
                        def mmo(e, mo=mo, bank=bank):
                            return [e.matmul(pj[bank][:],
                                             opw[:, cc * D + mo * 128: cc * D + mo * 128 + 128],
                                             ybuf[:, cc * TT:(cc + 1) * TT],
                                             start=(cc == 0), stop=(cc == 15))
                                    for cc in range(16)]
                        prog.op("tensor", mmo, reads=[pfx + "opw", pfx + "y"],
                                writes=[f"{pfx}pj{bank}"], tag=kp + f"omm{mo}")
                        def oev(e, mo=mo, bank=bank):
                            return e.tensor_scalar(t512[:], pj[bank][:], pmcol[:, 0:1],
                                                   None, OP.mult)
                        prog.op("vector", oev, reads=[f"{pfx}pj{bank}"], early=[pfx + "pmc"],
                                writes=[pfx + "t512"], tag=kp + f"oev{mo}")
                        def oc2(e, mo=mo):
                            return e.scalar_tensor_tensor(
                                t512[:], xnt2[:, mo * TT:(mo + 1) * TT],
                                pfcol[:, 0:1], t512[:], OP.mult, OP.add)
                        prog.op("vector", oc2, reads=[pfx + "xnt2", pfx + "t512"],
                                early=[pfx + "pfc"], writes=[pfx + "t512"], tag=kp + f"oc2{mo}")
                        def oc3(e, mo=mo):
                            return e.tensor_add(xrt[:, mo * TT:(mo + 1) * TT],
                                                xrt[:, mo * TT:(mo + 1) * TT], t512[:])
                        prog.op("vector", oc3, reads=[pfx + "t512", pfx + "xrt"],
                                writes=[pfx + "xrt"], tag=kp + f"oc3{mo}")
                    prog.op("sync", lambda e, ti=ti, s=s: e.dma_start(out=fm3(dst[s, ti], TT),
                                                                      in_=cm3(xrt[:], TT)),
                            reads=[pfx + "xrt"], writes=[f"xr:{s}:{ti}"], dma=True,
                            tag=kp + "xrst")
                prog.fence()
        prog.fence()



def emit_attn_b(ctx, prog, src, dst, W, C, sidx):
    nc, T, NT, n_seq = ctx.nc, ctx.T, ctx.NT, ctx.n_seq
    pfx = f"s{sidx}x_"
    NQ = T // 128
    NKC = T // 512
    NBIS = 16
    abxn = nc.dram_tensor(ctx.name("abxn"), [n_seq, NT, D, TT], BF16,
                          kind="Internal").ap()

    with ExitStack() as sc:
        iqb = sbuf(ctx, sc, "x_iq", [128, 2 * T], F32)
        ikb = sbuf(ctx, sc, "x_ik", [128, T], F32)
        iwt = sbuf(ctx, sc, "x_iw", [128, (T // 128) * 4], F32)
        qb = sbuf(ctx, sc, "x_q", [128, KC * T], BF16)
        kb = sbuf(ctx, sc, "x_k", [128, KC * T], BF16)
        vb = sbuf(ctx, sc, "x_v", [128, (T // 128) * D], BF16)

        for s in range(n_seq):
            spfx = f"{pfx}{s}_"
            with ExitStack() as ph:
                lnb = make_ln_bufs(ctx, ph)
                xn32 = sbuf(ctx, ph, "x_xn32", [128, KC * TT], F32)
                iqw = sbuf(ctx, ph, "x_iqw", [128, KC * 256], F32)
                ikw = sbuf(ctx, ph, "x_ikw", [128, KC * 64], F32)
                iww = sbuf(ctx, ph, "x_iww", [128, KC * 4], F32)
                pj = [psumt(ctx, ph, "x_pj") for _ in range(2)]
                prog.op("sync", lambda e: e.dma_start(out=cm3(iqw[:], 256),
                                                      in_=fm3(W["ab_iqw"], 256)),
                        writes=[pfx + "iqw"], dma=True, tag=spfx + "ldiqw")
                prog.op("sync", lambda e: e.dma_start(out=cm3(ikw[:], 64),
                                                      in_=fm3(W["ab_ikw"], 64)),
                        writes=[pfx + "ikw"], dma=True, tag=spfx + "ldikw")
                prog.op("sync", lambda e: e.dma_start(out=cm3(iww[:], 4),
                                                      in_=fm3(W["ab_iww"], 4)),
                        writes=[pfx + "iww"], dma=True, tag=spfx + "ldiww")
                for ti in range(NT):
                    kp = f"{spfx}t{ti}_"
                    emit_ln_tile(ctx, prog, lnb, src[s, ti], f"xr:{s}:{ti}", 1e-5,
                                 kp, None, 0, None)
                    xnb, kxnb = lnb["xbn"], lnb["k"] + "xbn"
                    xt_, rstd_, KB = lnb["xt"], lnb["rstd"], lnb["k"]
                    prog.op("vector", lambda e: [e.tensor_mul(
                            xn32[:, c * TT:(c + 1) * TT], xt_[:, c * TT:(c + 1) * TT],
                            rstd_[:]) for c in range(KC)],
                            reads=[KB + "xt", KB + "rstd"], writes=[pfx + "xn32"],
                            tag=kp + "xn32")
                    prog.op("sync", lambda e, ti=ti, s=s: e.dma_start(
                            out=fm3(abxn[s, ti], TT), in_=cm3(xnb[:], TT)),
                            reads=[kxnb], writes=[f"{pfx}abxn:{s}:{ti}"], dma=True,
                            tag=kp + "xnst")
                    for mo in range(2):
                        bank = mo % 2
                        def mmiq(e, mo=mo, bank=bank):
                            return [e.matmul(pj[bank][:],
                                             iqw[:, c * 256 + mo * 128: c * 256 + mo * 128 + 128],
                                             xn32[:, c * TT:(c + 1) * TT],
                                             start=(c == 0), stop=(c == KC - 1))
                                    for c in range(KC)]
                        prog.op("tensor", mmiq, reads=[pfx + "iqw", pfx + "xn32"],
                                writes=[f"{pfx}pj{bank}"], tag=kp + f"iqmm{mo}")
                        prog.op("vector", lambda e, mo=mo, bank=bank, ti=ti: e.tensor_copy(
                                iqb[:, mo * T + ti * TT: mo * T + ti * TT + TT], pj[bank][:]),
                                reads=[f"{pfx}pj{bank}"], writes=[pfx + "iq"],
                                tag=kp + f"iqev{mo}")
                    def mmik(e):
                        return [e.matmul(pj[0][0:64, 0:TT],
                                         ikw[:, c * 64:(c + 1) * 64],
                                         xn32[:, c * TT:(c + 1) * TT],
                                         start=(c == 0), stop=(c == KC - 1))
                                for c in range(KC)] + \
                               [e.matmul(pj[0][64:128, 0:TT],
                                         ikw[:, c * 64:(c + 1) * 64],
                                         xn32[:, c * TT:(c + 1) * TT],
                                         start=(c == 0), stop=(c == KC - 1))
                                for c in range(KC)]
                    prog.op("tensor", mmik, reads=[pfx + "ikw", pfx + "xn32"],
                            writes=[f"{pfx}pj0"], tag=kp + "ikmm")
                    prog.op("vector", lambda e, ti=ti: e.tensor_copy(
                            ikb[:, ti * TT:(ti + 1) * TT], pj[0][0:128, 0:TT]),
                            reads=[f"{pfx}pj0"], writes=[pfx + "ik"], tag=kp + "ikev")
                    for st in range(TT // 128):
                        def mmiw(e, st=st):
                            return [e.matmul(pj[1][0:128, 0:4],
                                             xn32[:, c * TT + st * 128: c * TT + st * 128 + 128],
                                             iww[:, c * 4:(c + 1) * 4],
                                             start=(c == 0), stop=(c == KC - 1))
                                    for c in range(KC)]
                        prog.op("tensor", mmiw, reads=[pfx + "iww", pfx + "xn32"],
                                writes=[f"{pfx}pj1"], tag=kp + f"iwmm{st}")
                        tok = ti * (TT // 128) + st
                        prog.op("vector", lambda e, tok=tok: e.tensor_copy(
                                iwt[:, tok * 4:(tok + 1) * 4], pj[1][0:128, 0:4]),
                                reads=[f"{pfx}pj1"], writes=[pfx + "iwt"], tag=kp + f"iwev{st}")
                prog.fence()

            with ExitStack() as ph:
                wqkv = load_weight_bf16(ctx, prog, ph, W["ab_wqkv"], D, 3072,
                                        pfx + "wqkv", pfx + f"wqkv{s}")
                xnt = [sbuf(ctx, ph, "x_xnt0", [128, KC * TT], BF16)] * 2
                cosp = sbuf(ctx, ph, "x_cos", [128, T], BF16)
                sinp = sbuf(ctx, ph, "x_sin", [128, T], BF16)
                swap = sbuf(ctx, ph, "x_swap", [128, 128], BF16)
                qsb = sbuf(ctx, ph, "x_qsb", [128, TT], BF16)
                rsb = sbuf(ctx, ph, "x_rsb", [128, TT], BF16)
                pj = [psumt(ctx, ph, "x_pj2") for _ in range(4)]
                rps = psumt(ctx, ph, "x_rps")
                prog.op("gpsimd", lambda e: e.dma_start(out=cosp[:], in_=C["ab_cos"][:]),
                        writes=[pfx + "cos"], dma=True, tag=spfx + "ldcos")
                prog.op("gpsimd", lambda e: e.dma_start(out=sinp[:], in_=C["ab_sin"][:]),
                        writes=[pfx + "sin"], dma=True, tag=spfx + "ldsin")
                prog.op("gpsimd", lambda e: e.dma_start(out=swap[:], in_=C["c_swap"][:]),
                        writes=[pfx + "swap"], dma=True, tag=spfx + "ldswap")
                for ti in range(NT):
                    kp = f"{spfx}q{ti}_"
                    xb_ = xnt[0]
                    xbk = pfx + "xnt0"
                    prog.op("sync", lambda e, ti=ti, s=s, xb_=xb_: e.dma_start(
                            out=cm3(xb_[:], TT), in_=fm3(abxn[s, ti], TT)),
                            reads=[f"{pfx}abxn:{s}:{ti}"], writes=[xbk], dma=True,
                            tag=kp + "xnld")
                    for which, off, buf, bkey in (("q", 0, qb, pfx + "Q"), ("k", D, kb, pfx + "K")):
                        for mo in range(KC):
                            bank = mo % 4
                            def mm(e, off=off, mo=mo, bank=bank, xb_=xb_):
                                return [e.matmul(pj[bank][:],
                                                 wqkv[:, c * 3072 + off + mo * 128:
                                                      c * 3072 + off + mo * 128 + 128],
                                                 xb_[:, c * TT:(c + 1) * TT],
                                                 start=(c == 0), stop=(c == KC - 1))
                                        for c in range(KC)]
                            prog.op("tensor", mm, reads=[pfx + "wqkv", xbk],
                                    writes=[f"{pfx}pj2{bank}"], tag=kp + which + f"mm{mo}")
                            prog.op("vector", lambda e, bank=bank: e.tensor_copy(
                                    qsb[:], pj[bank][:]),
                                    reads=[f"{pfx}pj2{bank}"], writes=[pfx + "qsb"],
                                    tag=kp + which + f"qe{mo}")
                            prog.op("tensor", lambda e: e.matmul(rps[:], swap[:], qsb[:],
                                                                 start=True, stop=True),
                                    reads=[pfx + "swap", pfx + "qsb"], writes=[pfx + "rps"],
                                    tag=kp + which + f"rot{mo}")
                            prog.op("vector", lambda e, ti=ti: e.tensor_mul(
                                    rsb[:], rps[:], sinp[:, ti * TT:(ti + 1) * TT]),
                                    reads=[pfx + "rps", pfx + "sin"], writes=[pfx + "rsb"],
                                    tag=kp + which + f"rs{mo}")
                            def rc(e, mo=mo, ti=ti, buf=buf):
                                return e.tensor_mul(qsb[:], qsb[:],
                                                    cosp[:, ti * TT:(ti + 1) * TT])
                            prog.op("vector", rc, reads=[pfx + "qsb", pfx + "cos"],
                                    writes=[pfx + "qsb"], tag=kp + which + f"rc{mo}")
                            def rsum(e, mo=mo, ti=ti, buf=buf):
                                return e.tensor_add(
                                    buf[:, mo * T + ti * TT: mo * T + ti * TT + TT],
                                    qsb[:], rsb[:])
                            prog.op("vector", rsum, reads=[pfx + "qsb", pfx + "rsb"],
                                    writes=[bkey], tag=kp + which + f"ra{mo}")
                    for st in range(TT // 128):
                        for half in range(2):
                            bank = (st * 2 + half) % 4
                            def mmv(e, st=st, half=half, bank=bank, xb_=xb_):
                                return [e.matmul(pj[bank][:],
                                                 xb_[:, c * TT + st * 128: c * TT + st * 128 + 128],
                                                 wqkv[:, c * 3072 + 2048 + half * 512:
                                                      c * 3072 + 2048 + half * 512 + 512],
                                                 start=(c == 0), stop=(c == KC - 1))
                                        for c in range(KC)]
                            prog.op("tensor", mmv, reads=[pfx + "wqkv", xbk],
                                    writes=[f"{pfx}pj2{bank}"], tag=kp + f"vmm{st}{half}")
                            tok = ti * (TT // 128) + st
                            def evv(e, tok=tok, half=half, bank=bank):
                                return e.tensor_copy(vb[:, tok * D + half * 512:
                                                        tok * D + half * 512 + 512],
                                                     pj[bank][:])
                            prog.op("vector", evv, reads=[f"{pfx}pj2{bank}"],
                                    writes=[pfx + "V"], tag=kp + f"vev{st}{half}")
                prog.fence()

            with ExitStack() as ph:
                ibuf = sbuf(ctx, ph, "x_I", [128, T], F32)
                itmp = sbuf(ctx, ph, "x_it", [128, 512], F32)
                bias = [sbuf(ctx, ph, f"x_bias{i}", [128, T], F32) for i in range(2)]
                lo = sbuf(ctx, ph, "x_lo", [128, 1], F32)
                hi = sbuf(ctx, ph, "x_hi", [128, 1], F32)
                thr = sbuf(ctx, ph, "x_thr", [128, 1], F32)
                cnt = sbuf(ctx, ph, "x_cnt", [128, 1], F32)
                cmask = sbuf(ctx, ph, "x_cm", [128, 1], U32)
                diagc = sbuf(ctx, ph, "x_diagc", [128, 128], F32)
                nthr = sbuf(ctx, ph, "x_nthr", [128, 1], F32)
                scores = sbuf(ctx, ph, "x_sc", [128, T], F32)
                pbuf = [sbuf(ctx, ph, f"x_p{i}", [128, T], BF16) for i in range(2)]
                ptb = sbuf(ctx, ph, "x_pt", [128, T], BF16)
                mxc = sbuf(ctx, ph, "x_mx", [128, 1], F32)
                negmax = sbuf(ctx, ph, "x_nm", [128, 1], F32)
                lrow = sbuf(ctx, ph, "x_l", [128, 1], F32)
                linv = sbuf(ctx, ph, "x_li", [128, 1], F32)
                attn_out = sbuf(ctx, ph, "x_ao", [128, KC * TT], BF16)
                xrt = sbuf(ctx, ph, "x_xr", [128, KC * TT], F32)
                sc_ps = [psumt(ctx, ph, "x_scps") for _ in range(2)]
                pt_ps = [psumt(ctx, ph, "x_ptps", BF16) for _ in range(2)]
                av_ps = [psumt(ctx, ph, "x_avps") for _ in range(2)]
                oj_ps = [psumt(ctx, ph, "x_ojps") for _ in range(2)]

                prog.op("sync", lambda e: e.dma_start(out=diagc[:], in_=C["c_diagc"][:]),
                        writes=[pfx + "diagc"], dma=True, tag=spfx + "lddiag")

                def idx_scores(qt):
                    nkc = min(NKC, qt // 4 + 1)
                    for h in range(4):
                        hp = 64 * (h % 2)
                        hc = h // 2
                        for j in range(nkc):
                            jb = j % 2
                            def dmm(e, hp=hp, hc=hc, qt=qt, j=j, jb=jb):
                                return e.matmul(oj_ps[jb][:],
                                                iqb[hp:hp + 64,
                                                    hc * T + qt * 128: hc * T + qt * 128 + 128],
                                                ikb[hp:hp + 64, j * 512:(j + 1) * 512],
                                                start=True, stop=True)
                            prog.op("tensor", dmm, reads=[pfx + "iq", pfx + "ik"],
                                    writes=[f"{pfx}ojps{jb}"], tag=f"{spfx}d{qt}h{h}j{j}")
                            def dev(e, jb=jb):
                                return e.activation(itmp[:], oj_ps[jb][:], AF.Relu)
                            prog.op("scalar", dev, reads=[f"{pfx}ojps{jb}"],
                                    writes=[pfx + "itmp"], tag=f"{spfx}de{qt}h{h}j{j}")
                            if h == 0:
                                def dacc(e, j=j, h=h, qt=qt):
                                    return e.tensor_scalar(
                                        ibuf[:, j * 512:(j + 1) * 512], itmp[:],
                                        iwt[:, qt * 4 + h: qt * 4 + h + 1], None, OP.mult)
                                prog.op("vector", dacc, reads=[pfx + "itmp"],
                                        early=[pfx + "iwt"],
                                        writes=[pfx + "I"], tag=f"{spfx}da{qt}h{h}j{j}")
                            else:
                                def dacc(e, j=j, h=h, qt=qt):
                                    return e.scalar_tensor_tensor(
                                        ibuf[:, j * 512:(j + 1) * 512], itmp[:],
                                        iwt[:, qt * 4 + h: qt * 4 + h + 1],
                                        ibuf[:, j * 512:(j + 1) * 512], OP.mult, OP.add)
                                prog.op("vector", dacc, reads=[pfx + "itmp", pfx + "I"],
                                        early=[pfx + "iwt"],
                                        writes=[pfx + "I"], tag=f"{spfx}da{qt}h{h}j{j}")
                    prog.op("vector", lambda e, qt=qt: e.tensor_add(
                            ibuf[:, qt * 128:(qt + 1) * 128],
                            ibuf[:, qt * 128:(qt + 1) * 128], diagc[:]),
                            reads=[pfx + "diagc", pfx + "I"], writes=[pfx + "I"],
                            tag=f"{spfx}imaskd{qt}")
                    if (qt + 1) * 128 < nkc * 512:
                        prog.op("vector", lambda e, qt=qt, nkc=nkc: e.memset(
                                ibuf[:, (qt + 1) * 128: nkc * 512], NEG),
                                writes=[pfx + "I"], tag=f"{spfx}imaskm{qt}")
                    prog.op("vector", lambda e, nkc=nkc: e.reduce_max(
                            hi[:], ibuf[:, 0:nkc * 512], AX),
                            reads=[pfx + "I"], writes=[pfx + "hi"], tag=f"{spfx}bh{qt}")
                    prog.op("vector", lambda e, qt=qt: e.tensor_reduce(
                            lo[:], ibuf[:, 0:qt * 128], AX, OP.min),
                            reads=[pfx + "I"], writes=[pfx + "lo"], tag=f"{spfx}bl{qt}")

                def bisect_iter(qt, it):
                    nkc = min(NKC, qt // 4 + 1)
                    prog.op("vector", lambda e: e.tensor_add(thr[:], lo[:], hi[:]),
                            reads=[pfx + "lo", pfx + "hi"], writes=[pfx + "thr"],
                            tag=f"{spfx}bt{qt}_{it}")
                    prog.op("vector", lambda e: e.tensor_scalar(
                            nthr[:], thr[:], -0.5, None, OP.mult),
                            reads=[pfx + "thr"], writes=[pfx + "nthr"],
                            tag=f"{spfx}bn{qt}_{it}")
                    def bsg(e, nkc=nkc):
                        return e.activation(ptb[:, 0:nkc * 512],
                                            ibuf[:, 0:nkc * 512], AF.Sign,
                                            bias=nthr[:], accum_out=cnt[:])
                    prog.op("scalar", bsg, reads=[pfx + "I"], early=[pfx + "nthr"],
                            writes=[pfx + "Pt", pfx + "cnt"],
                            tag=f"{spfx}bs{qt}_{it}")
                    cthr = float(1024 - nkc * 512)
                    prog.op("vector", lambda e, cthr=cthr: e.tensor_scalar(
                            cmask[:], cnt[:], cthr, None, OP.is_ge),
                            reads=[pfx + "cnt"], writes=[pfx + "cmvec"],
                            tag=f"{spfx}bc{qt}_{it}")
                    prog.op("vector", lambda e: e.tensor_scalar(
                            thr[:], thr[:], 0.5, None, OP.mult),
                            reads=[pfx + "thr"], writes=[pfx + "thr"],
                            tag=f"{spfx}bm{qt}_{it}")
                    prog.op("vector", lambda e: e.select(lo[:], cmask[:], thr[:], lo[:]),
                            reads=[pfx + "cmvec", pfx + "thr", pfx + "lo"],
                            writes=[pfx + "lo"], tag=f"{spfx}bu{qt}_{it}")
                    prog.op("vector", lambda e: e.select(hi[:], cmask[:], hi[:], thr[:]),
                            reads=[pfx + "cmvec", pfx + "thr", pfx + "hi"],
                            writes=[pfx + "hi"], tag=f"{spfx}bv{qt}_{it}")

                def mask_build(qt):
                    nkc = min(NKC, qt // 4 + 1)
                    bb = bias[qt % 2]
                    bbk = f"{pfx}biasb{qt % 2}"
                    prog.op("vector", lambda e, nkc=nkc, bb=bb: e.tensor_scalar(
                            bb[:, 0:nkc * 512], ibuf[:, 0:nkc * 512], lo[:],
                            None, OP.is_gt),
                            reads=[pfx + "I"], early=[pfx + "lo"],
                            writes=[bbk], tag=f"{spfx}bb{qt}")
                    prog.op("vector", lambda e, nkc=nkc, bb=bb: e.tensor_scalar(
                            bb[:, 0:nkc * 512], bb[:, 0:nkc * 512], -8.0 * NEG, 8.0 * NEG,
                            OP.mult, OP.add),
                            reads=[bbk], writes=[bbk], tag=f"{spfx}bf{qt}")

                def stageA(qt, h):
                    hc, hp = h // 2, 64 * (h % 2)
                    nkc = min(NKC, qt // 4 + 1)
                    pb = pbuf[h % 2]
                    pk = f"{pfx}P{h % 2}"
                    bb = bias[qt % 2]
                    bbk = f"{pfx}biasb{qt % 2}"
                    hkp = f"{spfx}q{qt}h{h}_"
                    for j in range(nkc):
                        sb_ = j % 2
                        def smm(e, hc=hc, hp=hp, qt=qt, j=j, sb_=sb_):
                            return e.matmul(sc_ps[sb_][:],
                                            qb[hp:hp + 64, hc * T + qt * 128: hc * T + qt * 128 + 128],
                                            kb[hp:hp + 64, hc * T + j * 512: hc * T + j * 512 + 512],
                                            start=True, stop=True)
                        prog.op("tensor", smm, reads=[pfx + "Q", pfx + "K"],
                                writes=[f"{pfx}scps{sb_}"], tag=hkp + f"smm{j}")
                        if qt >= 4:
                            def sev(e, j=j, sb_=sb_, bb=bb):
                                return e.scalar_tensor_tensor(
                                    scores[:, j * 512:(j + 1) * 512], sc_ps[sb_][:],
                                    0.125, bb[:, j * 512:(j + 1) * 512],
                                    OP.mult, OP.add)
                            prog.op("vector", sev,
                                    reads=[f"{pfx}scps{sb_}", bbk],
                                    writes=[pfx + "scores"], tag=hkp + f"sev{j}")
                        else:
                            def sev(e, j=j, sb_=sb_):
                                return e.tensor_scalar(
                                    scores[:, j * 512:(j + 1) * 512], sc_ps[sb_][:],
                                    0.125, None, OP.mult)
                            prog.op("vector", sev, reads=[f"{pfx}scps{sb_}"],
                                    writes=[pfx + "scores"], tag=hkp + f"sev{j}")
                    prog.op("vector", lambda e, nkc=nkc: e.reduce_max(
                            negmax[:], scores[:, 0:nkc * 512], AX, negate=True),
                            reads=[pfx + "scores"], writes=[pfx + "negmax"], tag=hkp + "neg")
                    prog.op("scalar", lambda e, nkc=nkc, pb=pb: e.activation(
                            pb[:, 0:nkc * 512], scores[:, 0:nkc * 512], AF.Exp,
                            bias=negmax[:], accum_out=lrow[:]),
                            reads=[pfx + "scores"], early=[pfx + "negmax"],
                            writes=[pk, pfx + "l"], tag=hkp + "exp")
                    prog.op("vector", lambda e: e.reciprocal(linv[:], lrow[:]),
                            reads=[pfx + "l"], writes=[pfx + "linv"], tag=hkp + "rcp")
                    prog.op("vector", lambda e, nkc=nkc, pb=pb: e.tensor_scalar(
                            pb[:, 0:nkc * 512], pb[:, 0:nkc * 512], linv[:], None, OP.mult),
                            reads=[pk], early=[pfx + "linv"], writes=[pk],
                            tag=hkp + "pscale")

                def stageB(qt, h):
                    hc, hp = h // 2, 64 * (h % 2)
                    qq = qt % 4
                    nkt = min(NKC, qt // 4 + 1) * 4
                    pb = pbuf[h % 2]
                    pk = f"{pfx}P{h % 2}"
                    hkp = f"{spfx}q{qt}h{h}_"
                    nh_half = (nkt + 1) // 2
                    for half in range(2):
                        j0 = half * nh_half
                        j1 = min(nkt, j0 + nh_half)
                        if j0 >= j1:
                            continue
                        def tp(e, j0=j0, j1=j1, half=half, pb=pb):
                            return [e.transpose(pt_ps[half][:, (j - j0) * 128:(j - j0 + 1) * 128],
                                                pb[:, j * 128:(j + 1) * 128],
                                                ctx.ident[:])
                                    for j in range(j0, j1)]
                        prog.op("tensor", tp, reads=[pk, "c:ident"],
                                writes=[f"{pfx}ptps{half}"], tag=hkp + f"tp{half}")
                        def tpe(e, j0=j0, j1=j1, half=half):
                            return e.copy(ptb[:, j0 * 128:j1 * 128],
                                          pt_ps[half][:, 0:(j1 - j0) * 128])
                        prog.op("scalar", tpe, reads=[f"{pfx}ptps{half}"],
                                writes=[pfx + "Pt"], tag=hkp + f"tpe{half}")
                    ab = h % 2
                    def av(e, h=h, ab=ab, nkt=nkt):
                        return [e.matmul(av_ps[ab][0:64, 0:128],
                                         vb[:, tc * D + h * 64: tc * D + h * 64 + 64],
                                         ptb[:, tc * 128:(tc + 1) * 128],
                                         start=(tc == 0), stop=(tc == nkt - 1))
                                for tc in range(nkt)]
                    prog.op("tensor", av, reads=[pfx + "Pt", pfx + "V"],
                            writes=[f"{pfx}avps{ab}"], tag=hkp + "av")
                    def ave(e, hc=hc, hp=hp, ab=ab, qq=qq):
                        return e.copy(
                            attn_out[hp:hp + 64, hc * TT + qq * 128: hc * TT + qq * 128 + 128],
                            av_ps[ab][0:64, 0:128])
                    prog.op("scalar", ave, reads=[f"{pfx}avps{ab}"],
                            writes=[pfx + "ao"], tag=hkp + "ave")

                def phaseC(g):
                    with ExitStack() as phg:
                        wo = load_weight_bf16(ctx, prog, phg, W["ab_wo"], D, D,
                                              pfx + "wo", pfx + f"wo{s}_{g}")
                        prog.op("sync", lambda e, g=g, s=s: e.dma_start(out=cm3(xrt[:], TT),
                                                                        in_=fm3(src[s, g], TT)),
                                reads=[f"xr:{s}:{g}"], writes=[pfx + "xrt"], dma=True,
                                tag=f"{spfx}xr{g}")
                        for mo in range(KC):
                            ob = mo % 2
                            def omm(e, mo=mo, ob=ob, wo=wo):
                                return [e.matmul(oj_ps[ob][:],
                                                 wo[:, c * D + mo * 128: c * D + mo * 128 + 128],
                                                 attn_out[:, c * TT:(c + 1) * TT],
                                                 start=(c == 0), stop=(c == KC - 1))
                                        for c in range(KC)]
                            prog.op("tensor", omm, reads=[pfx + "wo", pfx + "ao"],
                                    writes=[f"{pfx}ojps{ob}"], tag=f"{spfx}omm{g}_{mo}")
                            def oev(e, mo=mo, ob=ob):
                                return e.tensor_add(xrt[:, mo * TT:(mo + 1) * TT],
                                                    xrt[:, mo * TT:(mo + 1) * TT], oj_ps[ob][:])
                            prog.op("vector", oev, reads=[f"{pfx}ojps{ob}", pfx + "xrt"],
                                    writes=[pfx + "xrt"], tag=f"{spfx}oev{g}_{mo}")
                        prog.op("sync", lambda e, g=g, s=s: e.dma_start(out=fm3(dst[s, g], TT),
                                                                        in_=cm3(xrt[:], TT)),
                                reads=[pfx + "xrt"], writes=[f"xr:{s}:{g}"],
                                dma=True, tag=f"{spfx}st{g}")
                        prog.fence()

                def idx_work(qt, h):
                    if qt >= len(idx_sched):
                        return
                    sched = idx_sched[qt]
                    for fn in sched.get(h, ()):
                        fn()

                NQ_ = T // 128
                idx_sched = {}
                for tq in range(4, NQ_):
                    sl = {}
                    sl[0] = [lambda tq=tq: idx_scores(tq)]
                    for it in range(NBIS):
                        slot = 1 + (it * 14) // NBIS
                        sl.setdefault(slot, []).append(lambda tq=tq, it=it: bisect_iter(tq, it))
                    sl[15] = [lambda tq=tq: mask_build(tq)]
                    idx_sched[tq] = sl

                prev = None
                for qt in range(NQ_):
                    for h in range(16):
                        stageA(qt, h)
                        if qt + 1 >= 4 and (qt + 1) in idx_sched:
                            for fn in idx_sched[qt + 1].get(h, ()):
                                fn()
                        if prev is not None:
                            stageB(*prev)
                            if prev[1] == 15 and prev[0] % 4 == 3:
                                phaseC(prev[0] // 4)
                        prev = (qt, h)
                stageB(*prev)
                phaseC(NT - 1)
                prog.fence()
        prog.fence()



def emit_moe_precast(ctx, prog, W):
    nc = ctx.nc
    out = {}
    pcscope = ExitStack()
    stg = [pcscope.enter_context(
        nc.sbuf_tensor(ctx.name("pcst"), [128, 2048], BF16))
        for _ in range(2)]
    pieces = []
    for nm in ("w1x", "w3x", "w2x"):
        d = nc.dram_tensor(ctx.name("moebf_" + nm), [10, 1024, 1024], BF16,
                           kind="Internal").ap()
        out[nm] = d
        for e_ in range(10):
            for q in range(4):
                pieces.append((nm, d, e_, q))
    for i, (nm, d, e_, q) in enumerate(pieces):
        b = i % 2
        key = f"pc{b}"
        def ldp(e, nm=nm, e_=e_, q=q, b=b):
            return e.dma_start(out=stg[b][:].rearrange("p (c t) -> p c t", t=1024),
                               in_=W[f"moe_{nm}"][e_].rearrange("(c p) t -> p c t", p=128)
                               [:, 2 * q:2 * q + 2, :])
        prog.op("gpsimd", ldp, writes=[key], dma=True, tag=f"pcl{i}")
        def stp(e, d=d, e_=e_, q=q, b=b):
            return e.dma_start(out=d[e_].rearrange("(c p) t -> p c t", p=128)
                               [:, 2 * q:2 * q + 2, :],
                               in_=stg[b][:].rearrange("p (c t) -> p c t", t=1024))
        prog.op("sync", stp, reads=[key], writes=[f"moebf:{nm}:{e_}"], dma=True,
                tag=f"pcs{i}")
    prog.fence()
    pcscope.close()
    return out



SUB_EMITTERS = {}


def build_program(T=2048, n_seq=2, subs=("attn_a0",), wnames=(), cnames=()):
    nc = bass.Bass("TRN2", target_bir_lowering=False, detect_race_conditions=False)
    NT = T // TT
    stack = ExitStack()
    ctx = Ctx(nc, stack, T, n_seq)

    xin = nc.dram_tensor("xin", [n_seq, NT, D, TT], F32, kind="ExternalInput").ap()
    xout = nc.dram_tensor("xout", [n_seq, NT, D, TT], F32, kind="ExternalOutput").ap()
    W = {n: nc.dram_tensor(n, list(s), F32, kind="ExternalInput").ap() for n, s in wnames}
    C = {n: nc.dram_tensor(n, list(s), F32, kind="ExternalInput").ap() for n, s in cnames}
    if len(subs) > 1:
        xr = nc.dram_tensor("xr", [n_seq, NT, D, TT], F32, kind="Internal").ap()
    else:
        xr = None

    with stack:
        ctx.ones128 = sbuf(ctx, stack, "g_ones", [128, 128], BF16)
        ctx.ident = sbuf(ctx, stack, "g_ident", [128, 128], BF16)
        prog = Prog(ctx)
        prog.op("gpsimd", lambda e: e.dma_start(out=ctx.ones128[:], in_=C["c_ones"][:]),
                writes=["c:ones"], dma=True, tag="ld_ones")
        prog.op("gpsimd", lambda e: e.dma_start(out=ctx.ident[:], in_=C["c_ident"][:]),
                writes=["c:ident"], dma=True, tag="ld_ident")

        if "moe" in subs and USE_PRECAST:
            ctx.moe_bf = emit_moe_precast(ctx, prog, W)
        for i, sub in enumerate(subs):
            src = xin if i == 0 else xr
            dst = xout if i == len(subs) - 1 else xr
            SUB_EMITTERS[sub](ctx, prog, src, dst, W, C, i)

        prog.final_waits()
        with nc.Block() as blk:
            prog.emit(blk)
    return nc


SUB_EMITTERS["attn_a0"] = emit_attn_a
SUB_EMITTERS["moe"] = emit_moe
SUB_EMITTERS["ffn1"] = emit_l1ffn
SUB_EMITTERS["attn_a1"] = emit_l1aa
SUB_EMITTERS["attn_b1"] = emit_mamba
SUB_EMITTERS["attn_b0"] = emit_attn_b



def _to_fm(x_seqs, T):
    S = x_seqs.shape[0]
    NT = T // TT
    xf = np.ascontiguousarray(x_seqs.transpose(0, 2, 1))
    xf = xf.reshape(S, D, NT, TT).transpose(0, 2, 1, 3)
    return np.ascontiguousarray(xf)


def _from_fm(xf):
    S, NT = xf.shape[0], xf.shape[1]
    x = xf.transpose(0, 2, 1, 3).reshape(S, D, NT * TT)
    return np.ascontiguousarray(x.transpose(0, 2, 1))


def sparse_mask_bias(T):
    r = np.arange(T)
    causal = r[None, :] <= r[:, None]
    local = np.abs(r[:, None] - r[None, :]) <= 128
    strided = (r[None, :] % 64 == 0) & np.ones((T, 1), bool)
    mask = causal & (local | strided)
    return np.where(mask, 0.0, 8.0 * NEG).astype(np.float32)


def host_prep_attn_a(inputs, T):
    W = {
        "aa_wqkv": np.ascontiguousarray(inputs["L0_attn_a_w_qkv"].T),
        "aa_wout": np.ascontiguousarray(inputs["L0_attn_a_w_out"].T),
    }
    p = np.arange(128)[:, None]
    j = np.arange(128)[None, :]
    prevm = (p <= j) | (j == 0) | (j == 64)
    diagm = p >= j
    wb = np.concatenate([np.where(prevm, 0.0, 8.0 * NEG),
                         np.where(diagm, 0.0, 8.0 * NEG)], axis=1).astype(np.float32)
    nstr = T // 64
    selstr = np.zeros((T, nstr), np.float32)
    selstr[np.arange(nstr) * 64, np.arange(nstr)] = 1.0
    C = {"aa_wb": wb, "c_selstr": selstr}
    return W, C


def host_prep_moe(inputs):
    w1 = np.asarray(inputs["L0_ffn_w1"], np.float32)
    w3 = np.asarray(inputs["L0_ffn_w3"], np.float32)
    w2 = np.asarray(inputs["L0_ffn_w2"], np.float32)
    sw1 = np.asarray(inputs["L0_ffn_sw1"], np.float32)
    sw3 = np.asarray(inputs["L0_ffn_sw3"], np.float32)
    sw2 = np.asarray(inputs["L0_ffn_sw2"], np.float32)
    w1x = np.concatenate([w1.transpose(0, 2, 1),
                          sw1.T[None, :, 0:1024], sw1.T[None, :, 1024:2048]], 0)
    w3x = np.concatenate([w3.transpose(0, 2, 1),
                          sw3.T[None, :, 0:1024], sw3.T[None, :, 1024:2048]], 0)
    w2x = np.concatenate([w2.transpose(0, 2, 1),
                          sw2.T[None, 0:1024, :], sw2.T[None, 1024:2048, :]], 0)
    return {
        "moe_w1x": np.ascontiguousarray(w1x),
        "moe_w3x": np.ascontiguousarray(w3x),
        "moe_w2x": np.ascontiguousarray(w2x),
        "moe_gw": np.ascontiguousarray(inputs["L0_ffn_gate_w"].T),
        "moe_gbias": np.ascontiguousarray(inputs["L0_ffn_gate_bias"].reshape(1, 8)),
    }


def host_prep_l1ffn(inputs):
    return {
        "f1_wr": np.ascontiguousarray(inputs["L1_ffn_W_r"].T),
        "f1_wk": np.ascontiguousarray(inputs["L1_ffn_W_k"].T),
        "f1_wv": np.ascontiguousarray(inputs["L1_ffn_W_v"].T),
        "f1_mur": np.ascontiguousarray(inputs["L1_ffn_mu_r"].reshape(8, 128).T),
        "f1_muk": np.ascontiguousarray(inputs["L1_ffn_mu_k"].reshape(8, 128).T),
    }


def rope_factor_a1(T, d_head=64):
    i = np.arange(d_head)
    theta = np.power(10000.0, -2.0 * (i // 2).astype(np.float64) / d_head)
    pos = np.arange(T, dtype=np.float64)
    ang = pos[:, None] * theta[None, :]
    sign = np.where(np.arange(T) % 2 == 0, -1.0, 1.0)[:, None]
    rf = np.cos(ang) + np.sin(ang) * sign
    return rf.astype(np.float32)


def host_prep_l1aa(inputs, T):
    rf = rope_factor_a1(T)
    W = {
        "a1_wq": np.ascontiguousarray(inputs["L1_attn_a_w_q"], np.float32),
        "a1_wk": np.ascontiguousarray(inputs["L1_attn_a_w_k"], np.float32),
        "a1_wv": np.ascontiguousarray(inputs["L1_attn_a_w_v"], np.float32),
        "a1_wo": np.ascontiguousarray(inputs["L1_attn_a_w_o"], np.float32),
        "a1_normw": np.ascontiguousarray(inputs["L1_attn_a_norm_w"].reshape(8, 128).T),
    }
    r = np.arange(T)
    causal = r[None, :] <= r[:, None]
    C = {
        "a1_bias": np.where(causal, 0.0, 8.0 * NEG).astype(np.float32),
        "c_ropea1": np.ascontiguousarray(np.vstack([rf.T, rf.T])),
    }
    return W, C


def host_prep_attn_b(inputs, T):
    dh = 64
    inv_freq = 1.0 / (10000.0 ** (np.arange(0, dh, 2) / dh))
    t = np.arange(T)
    fr = np.outer(t, inv_freq)
    cos = np.cos(fr).astype(np.float32)
    sin = np.sin(fr).astype(np.float32)
    cosd = np.empty((dh, T), np.float32)
    sind = np.empty((dh, T), np.float32)
    cosd[0::2] = cos.T
    cosd[1::2] = cos.T
    sind[0::2] = -sin.T
    sind[1::2] = sin.T
    swap = np.zeros((128, 128), np.float32)
    for i in range(64):
        swap[2 * i, 2 * i + 1] = 1.0
        swap[2 * i + 1, 2 * i] = 1.0
    W = {
        "ab_wqkv": np.ascontiguousarray(inputs["L0_attn_b_qkv_w"].T),
        "ab_wo": np.ascontiguousarray(inputs["L0_attn_b_wo_w"].T),
        "ab_iqw": np.ascontiguousarray(inputs["L0_attn_b_iq_w"].T),
        "ab_ikw": np.ascontiguousarray(inputs["L0_attn_b_ik_w"].T),
        "ab_iww": np.ascontiguousarray(inputs["L0_attn_b_iw_w"].T),
    }
    C = {
        "ab_cos": np.ascontiguousarray(np.vstack([cosd, cosd])),
        "ab_sin": np.ascontiguousarray(np.vstack([sind, sind])),
        "c_swap": swap,
    }
    return W, C


def host_prep_mamba(inputs):
    ipw = np.asarray(inputs["L1_attn_b_in_proj_w"], np.float32).T
    z64 = np.zeros((1024, 64), np.float32)
    ipw_z = ipw[:, 0:2048]
    ipw_r = np.concatenate([ipw[:, 2048:4096], ipw[:, 4096:4160], z64,
                            ipw[:, 4160:4224], z64, ipw[:, 4224:4256]], axis=1)
    cw = np.asarray(inputs["L1_attn_b_conv_w"], np.float32)
    cb = np.asarray(inputs["L1_attn_b_conv_b"], np.float32)
    cw_new = np.zeros((18 * 128, 4), np.float32)
    cb_new = np.zeros(18 * 128, np.float32)
    cw_new[0:2048] = cw[0:2048]
    cw_new[16 * 128:16 * 128 + 64] = cw[2048:2112]
    cw_new[17 * 128:17 * 128 + 64] = cw[2112:2176]
    cb_new[0:2048] = cb[0:2048]
    cb_new[16 * 128:16 * 128 + 64] = cb[2048:2112]
    cb_new[17 * 128:17 * 128 + 64] = cb[2112:2176]
    return {
        "mb_ipwz": np.ascontiguousarray(ipw_z),
        "mb_ipwr": np.ascontiguousarray(ipw_r),
        "mb_opw": np.ascontiguousarray(inputs["L1_attn_b_out_proj_w"].T),
        "mb_convw": np.ascontiguousarray(cw_new.reshape(18, 128, 4).transpose(1, 0, 2)
                                         .reshape(128, 72)),
        "mb_convb": np.ascontiguousarray(cb_new.reshape(18, 128).T),
        "mb_dtb": np.ascontiguousarray(inputs["L1_attn_b_dt_bias"].reshape(1, 32)),
        "mb_alog": np.ascontiguousarray(inputs["L1_attn_b_A_log"].reshape(1, 32)),
        "mb_dpar": np.ascontiguousarray(inputs["L1_attn_b_D_param"].reshape(1, 32)),
        "mb_gnw": np.ascontiguousarray(inputs["L1_attn_b_gnorm_w"].reshape(16, 128).T),
        "mb_nw": np.ascontiguousarray(inputs["L1_attn_b_norm_w"].reshape(8, 128).T),
        "mb_rw1": np.ascontiguousarray(inputs["L1_attn_b_r_w1"].T),
        "mb_rb1": np.ascontiguousarray(inputs["L1_attn_b_r_b1"].reshape(32, 1)),
        "mb_rw2": np.ascontiguousarray(inputs["L1_attn_b_r_w2"].T),
        "mb_rb2": np.ascontiguousarray(inputs["L1_attn_b_r_b2"].reshape(1, 3)),
    }


def base_consts():
    return {
        "c_ones": np.ones((128, 128), np.float32),
        "c_ident": np.eye(128, dtype=np.float32),
        "c_ones1r": np.ones((1, 128), np.float32),
        "c_identf": np.eye(128, dtype=np.float32),
        "c_sel8": np.kron(np.eye(8, dtype=np.float32), np.ones((1, 128), np.float32)),
        "c_triu": np.triu(np.ones((128, 128), np.float32)),
        "c_diagc": np.where(np.arange(128)[None, :] <= np.arange(128)[:, None],
                            0.0, NEG).astype(np.float32),
        "c_sel32": np.kron(np.eye(32, dtype=np.float32), np.ones((1, 128), np.float32)),
    }


CHAIN = ("attn_a0", "moe", "attn_b0", "moe", "attn_a1", "ffn1", "attn_b1", "ffn1")


def host_prep_all(inputs, T, subs):
    W, C = {}, base_consts()
    if any(s.startswith("attn_a0") for s in subs):
        w, c = host_prep_attn_a(inputs, T)
        W.update(w)
        C.update(c)
    if "moe" in subs:
        W.update(host_prep_moe(inputs))
    if "ffn1" in subs:
        W.update(host_prep_l1ffn(inputs))
    if "attn_a1" in subs:
        w, c = host_prep_l1aa(inputs, T)
        W.update(w)
        C.update(c)
    if "attn_b1" in subs:
        W.update(host_prep_mamba(inputs))
    if "attn_b0" in subs:
        w, c = host_prep_attn_b(inputs, T)
        W.update(w)
        C.update(c)
        if "a1_bias" not in C:
            r = np.arange(T)
            causal = r[None, :] <= r[:, None]
            C["a1_bias"] = np.where(causal, 0.0, 8.0 * NEG).astype(np.float32)
    return W, C


def kernel(**inputs):
    T = 2048
    x = np.asarray(inputs["L0_attn_a_x"], np.float32)
    B = x.shape[0]
    n_cores = 8
    spc = B // n_cores

    subs = CHAIN
    W, C = host_prep_all(inputs, T, subs)
    wnames = [(k, v.shape) for k, v in W.items()]
    cnames = [(k, v.shape) for k, v in C.items()]
    nc = build_program(T=T, n_seq=spc, subs=subs, wnames=wnames, cnames=cnames)

    in_maps = []
    for c in range(n_cores):
        m = {"xin": _to_fm(x[c * spc:(c + 1) * spc], T)}
        m.update(W)
        m.update(C)
        in_maps.append(m)
    res = run_bass_kernel_spmd(nc, in_maps, core_ids=list(range(n_cores)))
    outs = [_from_fm(r["xout"]) for r in res.results]
    return np.concatenate(outs, axis=0)


# version 14
# speedup vs baseline: 1.1867x; 1.1867x over previous
import numpy as np
from contextlib import ExitStack

import concourse.bass as bass
import concourse.mybir as mybir
from concourse.alu_op_type import AluOpType as OP
from concourse.bass_utils import run_bass_kernel_spmd

F32 = mybir.dt.float32
BF16 = mybir.dt.bfloat16
U32 = mybir.dt.uint32
AF = mybir.ActivationFunctionType
AX = mybir.AxisListType.X

D = 1024
KC = D // 128
TT = 512
NEG = -1e30
DEBUG_TAPS = False
USE_PRECAST = False


class _Op:
    __slots__ = ("idx", "eng", "fn", "deps", "need_inc", "inc_val", "dma", "tag")

    def __init__(self, idx, eng, fn, dma, tag):
        self.idx, self.eng, self.fn, self.dma, self.tag = idx, eng, fn, dma, tag
        self.deps = []
        self.need_inc = False
        self.inc_val = None

    @property
    def semname(self):
        return self.eng + ("_dma" if self.dma else "_c")


class Prog:
    ENG_ORDER = ("sync", "gpsimd", "tensor", "vector", "scalar")

    def __init__(self, ctx):
        self.ctx = ctx
        self.ops = []
        self.last_writer = {}
        self.readers = {}
        self.fence_tails = []

    def op(self, eng, fn, reads=(), writes=(), early=(), dma=False, tag=""):
        o = _Op(len(self.ops), eng, fn, dma, tag)
        deps = {}
        for k in tuple(reads) + tuple(early):
            w = self.last_writer.get(k)
            if w is not None:
                deps[w.idx] = w
        for k in writes:
            w = self.last_writer.get(k)
            if w is not None:
                deps[w.idx] = w
            for r in self.readers.get(k, ()):
                deps[r.idx] = r
        for t in self.fence_tails:
            deps[t.idx] = t
        o.deps = list(deps.values())
        for k in tuple(reads) + tuple(early):
            self.readers.setdefault(k, []).append(o)
        for k in writes:
            self.last_writer[k] = o
            self.readers[k] = []
        self.ops.append(o)
        return o

    def fence(self):
        tails = {}
        for o in self.ops:
            tails[o.semname] = o
        self.fence_tails = list(tails.values())

    def emit(self, block):
        for o in self.ops:
            for d in o.deps:
                d.need_inc = True
        counters = {}
        for o in self.ops:
            if o.dma:
                o.need_inc = True
            if o.need_inc:
                k = 16 if o.dma else 1
                counters[o.semname] = counters.get(o.semname, 0) + k
                o.inc_val = counters[o.semname]
        for eng in self.ENG_ORDER:
            ops_e = [o for o in self.ops if o.eng == eng]
            if not ops_e:
                continue

            def make_section(ops_e):
                def section(e):
                    water = {}
                    for o in ops_e:
                        waits = {}
                        for d in o.deps:
                            if d.inc_val is None:
                                raise RuntimeError(f"dep without inc: {d.tag}")
                            s = d.semname
                            waits[s] = max(waits.get(s, 0), d.inc_val)
                        for s, v in sorted(waits.items()):
                            if water.get(s, 0) >= v:
                                continue
                            e.wait_ge(self.ctx.sem(s).h, v)
                            water[s] = v
                        insts = o.fn(e)
                        if o.need_inc:
                            last = insts[-1] if isinstance(insts, (list, tuple)) else insts
                            if last is None:
                                raise RuntimeError(f"op fn returned None: {o.tag}")
                            last.then_inc(self.ctx.sem(o.semname).h, 16 if o.dma else 1)
                return section

            getattr(block, eng)(make_section(ops_e))

    def final_waits(self):
        tails = {}
        for o in self.ops:
            tails[o.semname] = o
        def fn(e):
            return e.nop()
        o = self.op("gpsimd", fn, tag="final")
        o.deps = [t for t in tails.values() if t is not o]


class Ctx:
    def __init__(self, nc, stack, T, n_seq):
        self.nc = nc
        self.stack = stack
        self.T = T
        self.NT = T // TT
        self.n_seq = n_seq
        self.sems = {}
        self.uid = 0

    def sem(self, name):
        if name not in self.sems:
            self.sems[name] = Sem(self, name)
        return self.sems[name]

    def name(self, base):
        self.uid += 1
        return f"{base}_{self.uid}"

    def tap(self, prog, name, sb, readkeys, shape=None):
        if not DEBUG_TAPS:
            return
        shape = shape or list(sb.shape)
        d = self.nc.dram_tensor("tap_" + name, list(shape), sb.ap().dtype,
                                kind="ExternalOutput").ap()
        prog.op("gpsimd", lambda e: e.dma_start(out=d[:], in_=sb[:]),
                reads=list(readkeys), dma=True, tag="tap_" + name)

    def constcol(self, prog, value):
        if not hasattr(self, "_ccols"):
            self._ccols = {}
        v = float(value)
        if v not in self._ccols:
            t = self.stack.enter_context(
                self.nc.sbuf_tensor(self.name("cc"), [128, 1], F32, side="right"))
            key = f"cc:{v}"
            prog.op("vector", lambda e: e.memset(t[:], v), writes=[key], tag=key)
            self._ccols[v] = (t, key)
        return self._ccols[v]


class Sem:
    def __init__(self, ctx, name):
        self.h = ctx.stack.enter_context(ctx.nc.semaphore(name))
        self.n = 0


def sbuf(ctx, scope, base, shape, dtype):
    return scope.enter_context(ctx.nc.sbuf_tensor(ctx.name(base), shape, dtype))


def psumt(ctx, scope, base, dtype=F32):
    shape = [128, 512] if dtype == F32 else [128, 1024]
    return scope.enter_context(ctx.nc.psum_tensor(ctx.name(base), shape, dtype))


def fm3(ap_2d, inner):
    return ap_2d.rearrange("(c p) t -> p c t", p=128)


def cm3(sb_ap, tcols):
    return sb_ap.rearrange("p (c t) -> p c t", t=tcols)



def emit_ln_tile(ctx, prog, bufs, src_tile_ap, src_key, eps, tagpfx, out_sb,
                 out_cols, out_key, T_out=None):
    xt, xbn, xsq = bufs["xt"], bufs["xbn"], bufs["xsq"]
    meanpl, varpl, rstdpl = bufs["mean"], bufs["var"], bufs["rstd"]
    ps1, ps2 = bufs["ps1"], bufs["ps2"]
    K = bufs["k"]
    kxt, kxbn, kxsq = K + "xt", K + "xbn", K + "xsq"
    T_out = T_out or ctx.T

    if src_tile_ap is not None:
        prog.op("sync", lambda e: e.dma_start(out=cm3(xt[:], TT), in_=fm3(src_tile_ap, TT)),
                reads=[src_key], writes=[kxt], dma=True, tag=tagpfx + "ld")
    prog.op("scalar", lambda e: [e.copy(xbn[:, c * TT:(c + 1) * TT], xt[:, c * TT:(c + 1) * TT])
                                 for c in range(KC)],
            reads=[kxt], writes=[kxbn], tag=tagpfx + "cast")
    prog.op("vector", lambda e: [e.tensor_mul(xsq[:, c * TT:(c + 1) * TT],
                                              xt[:, c * TT:(c + 1) * TT],
                                              xt[:, c * TT:(c + 1) * TT]) for c in range(KC)],
            reads=[kxt], writes=[kxsq], tag=tagpfx + "sq")
    ones = ctx.ones128
    prog.op("tensor", lambda e: [e.matmul(ps1[:], ones[:], xbn[:, c * TT:(c + 1) * TT],
                                          start=(c == 0), stop=(c == KC - 1)) for c in range(KC)],
            reads=[kxbn, "c:ones"], writes=[K + "ps1"], tag=tagpfx + "mm1")
    prog.op("tensor", lambda e: [e.matmul(ps2[:], ones[:], xsq[:, c * TT:(c + 1) * TT],
                                          start=(c == 0), stop=(c == KC - 1)) for c in range(KC)],
            reads=[kxsq, "c:ones"], writes=[K + "ps2"], tag=tagpfx + "mm2")
    kmean, kvar, krstd = K + "mean", K + "var", K + "rstd"

    prog.op("vector", lambda e: e.tensor_scalar(meanpl[:], ps1[:], 1.0 / D, None, OP.mult),
            reads=[K + "ps1"], writes=[kmean], tag=tagpfx + "st1")
    prog.op("vector", lambda e: e.tensor_scalar(varpl[:], ps2[:], 1.0 / D, None, OP.mult),
            reads=[K + "ps2"], writes=[kvar], tag=tagpfx + "st2")
    prog.op("vector", lambda e: e.tensor_mul(ps1[:], meanpl[:], meanpl[:]),
            reads=[kmean], writes=[K + "ps1"], tag=tagpfx + "st3")
    prog.op("vector", lambda e: e.tensor_sub(varpl[:], varpl[:], ps1[:]),
            reads=[K + "ps1", kvar], writes=[kvar], tag=tagpfx + "st4")

    epst, epsk = ctx.constcol(prog, float(eps))
    prog.op("scalar", lambda e: e.activation(varpl[:], varpl[:], AF.Ln, bias=epst[:]),
            reads=[kvar], early=[epsk], writes=[kvar], tag=tagpfx + "ln")
    prog.op("scalar", lambda e: e.activation(rstdpl[:], varpl[:], AF.Exp, scale=-0.5),
            reads=[kvar], writes=[krstd], tag=tagpfx + "exp")

    dst = out_sb if out_sb is not None else xbn
    dkey = out_key if out_sb is not None else kxbn
    oc = out_cols if out_sb is not None else 0
    stride = T_out if out_sb is not None else TT

    prog.op("vector", lambda e: [e.tensor_sub(xt[:, c * TT:(c + 1) * TT],
                                              xt[:, c * TT:(c + 1) * TT], meanpl[:])
                                 for c in range(KC)],
            reads=[kmean, kxt], writes=[kxt], tag=tagpfx + "sub")
    prog.op("vector", lambda e: [e.tensor_mul(dst[:, c * stride + oc:c * stride + oc + TT],
                                              xt[:, c * TT:(c + 1) * TT], rstdpl[:])
                                 for c in range(KC)],
            reads=[krstd, kxt], writes=[dkey], tag=tagpfx + "mul")
    return kxbn


def make_ln_bufs(ctx, scope):
    k = ctx.name("lnb")
    return {
        "k": k,
        "xt": sbuf(ctx, scope, "ln_xt", [128, KC * TT], F32),
        "xbn": sbuf(ctx, scope, "ln_xbn", [128, KC * TT], BF16),
        "xsq": sbuf(ctx, scope, "ln_xsq", [128, KC * TT], BF16),
        "mean": sbuf(ctx, scope, "ln_mean", [128, TT], F32),
        "var": sbuf(ctx, scope, "ln_var", [128, TT], F32),
        "rstd": sbuf(ctx, scope, "ln_rstd", [128, TT], F32),
        "ps1": psumt(ctx, scope, "ln_ps1"),
        "ps2": psumt(ctx, scope, "ln_ps2"),
    }


def load_weight_bf16(ctx, prog, scope, dram_ap, kdim, mdim, key, tag):
    sb = sbuf(ctx, scope, tag, [128, (kdim // 128) * mdim], BF16)
    prog.op("gpsimd", lambda e: e.dma_start(out=cm3(sb[:], mdim), in_=fm3(dram_ap, mdim)),
            writes=[key], dma=True, tag="ldw_" + tag)
    return sb



def emit_attn_a(ctx, prog, src, dst, W, C, sidx):
    nc, T, NT, n_seq = ctx.nc, ctx.T, ctx.NT, ctx.n_seq
    NQ = T // 128
    NS_ALL = T // 64
    SKM = min(32, NS_ALL)
    pfx = f"s{sidx}a_"

    with ExitStack() as sc:
        qb = sbuf(ctx, sc, "aa_q", [128, KC * T], BF16)
        kb = sbuf(ctx, sc, "aa_k", [128, KC * T], BF16)
        vb = sbuf(ctx, sc, "aa_v", [128, (T // 128) * D], BF16)
        kstrb = sbuf(ctx, sc, "aa_kst", [128, KC * NS_ALL], BF16)
        vstr = sbuf(ctx, sc, "aa_vst", [32, D], BF16)
        selstr = sbuf(ctx, sc, "aa_sel", [128, (T // 128) * NS_ALL], BF16)
        wbt = sbuf(ctx, sc, "aa_wbt", [128, 256], F32)
        attn_out = sbuf(ctx, sc, "aa_ao", [128, KC * TT], BF16)
        prog.op("gpsimd", lambda e: e.dma_start(
                out=selstr[:].rearrange("p (c n) -> p c n", n=NS_ALL),
                in_=C["c_selstr"].rearrange("(c p) n -> p c n", p=128)),
                writes=[pfx + "sel"], dma=True, tag=pfx + "ldsel")
        prog.op("sync", lambda e: e.dma_start(out=wbt[:], in_=C["aa_wb"][:]),
                writes=[pfx + "wb"], dma=True, tag=pfx + "ldwb")

        for s in range(ctx.n_seq):
            spfx = f"{pfx}{s}_"
            with ExitStack() as ph:
                lnb = make_ln_bufs(ctx, ph)
                wqkv = load_weight_bf16(ctx, prog, ph, W["aa_wqkv"], D, 3072,
                                        pfx + "wqkv", pfx + f"wqkv{s}")
                pj = [psumt(ctx, ph, "aa_pj") for _ in range(4)]
                for ti in range(NT):
                    kp = f"{spfx}t{ti}_"
                    kxn = emit_ln_tile(ctx, prog, lnb, src[s, ti], f"xr:{s}:{ti}",
                                       1e-5, kp, None, 0, None)
                    xn = lnb["xbn"]
                    for which, off, buf, bkey in (("q", 0, qb, pfx + "Q"), ("k", D, kb, pfx + "K")):
                        for mo in range(KC):
                            bank = (mo % 4)
                            def mm(e, off=off, mo=mo, bank=bank, xn=xn):
                                return [e.matmul(pj[bank][:],
                                                 wqkv[:, c * 3072 + off + mo * 128:
                                                      c * 3072 + off + mo * 128 + 128],
                                                 xn[:, c * TT:(c + 1) * TT],
                                                 start=(c == 0), stop=(c == KC - 1))
                                        for c in range(KC)]
                            prog.op("tensor", mm, reads=[pfx + "wqkv", kxn], writes=[f"{pfx}pj{bank}"],
                                    tag=kp + which + f"mm{mo}")
                            def ev(e, mo=mo, bank=bank, buf=buf, ti=ti):
                                return e.tensor_copy(buf[:, mo * T + ti * TT: mo * T + ti * TT + TT],
                                                     pj[bank][:])
                            prog.op("vector", ev, reads=[f"{pfx}pj{bank}"], writes=[bkey],
                                    tag=kp + which + f"ev{mo}")
                    for st in range(TT // 128):
                        for half in range(2):
                            bank = (st * 2 + half) % 4
                            def mmv(e, st=st, half=half, bank=bank, xn=xn):
                                return [e.matmul(pj[bank][:],
                                                 xn[:, c * TT + st * 128: c * TT + st * 128 + 128],
                                                 wqkv[:, c * 3072 + 2048 + half * 512:
                                                      c * 3072 + 2048 + half * 512 + 512],
                                                 start=(c == 0), stop=(c == KC - 1))
                                        for c in range(KC)]
                            prog.op("tensor", mmv, reads=[pfx + "wqkv", kxn], writes=[f"{pfx}pj{bank}"],
                                    tag=kp + f"vmm{st}{half}")
                            tok = ti * (TT // 128) + st
                            def evv(e, tok=tok, half=half, bank=bank):
                                return e.tensor_copy(vb[:, tok * D + half * 512:
                                                        tok * D + half * 512 + 512], pj[bank][:])
                            prog.op("vector", evv, reads=[f"{pfx}pj{bank}"], writes=[pfx + "V"],
                                    tag=kp + f"vev{st}{half}")
                prog.op("vector", lambda e: [e.tensor_copy(
                        cm3(kstrb[:], NS_ALL)[:, c, :],
                        kb[:].rearrange("p (c t) -> p c t", t=T)[:, c, 0:T:64])
                        for c in range(KC)],
                        reads=[pfx + "K"], writes=[pfx + "kst"], tag=spfx + "kstr")
                for half in range(2):
                    def vsmm(e, half=half):
                        return [e.matmul(pj[half][0:NS_ALL, 0:512],
                                         selstr[:, tc * NS_ALL: (tc + 1) * NS_ALL],
                                         vb[:, tc * D + half * 512: tc * D + half * 512 + 512],
                                         start=(tc == 0), stop=(tc == T // 128 - 1))
                                for tc in range(T // 128)]
                    prog.op("tensor", vsmm, reads=[pfx + "sel", pfx + "V"],
                            writes=[f"{pfx}pj{half}"], tag=spfx + f"vsmm{half}")
                    prog.op("vector", lambda e, half=half: e.tensor_copy(
                            vstr[0:NS_ALL, half * 512:(half + 1) * 512],
                            pj[half][0:NS_ALL, 0:512]),
                            reads=[f"{pfx}pj{half}"], writes=[pfx + "vstr"],
                            tag=spfx + f"vsev{half}")
                prog.fence()

            with ExitStack() as ph:
                wout = load_weight_bf16(ctx, prog, ph, W["aa_wout"], D, D,
                                        pfx + "wout", pfx + f"wout{s}")
                scores = sbuf(ctx, ph, "aa_sc", [128, 320], F32)
                pbuf = [sbuf(ctx, ph, f"aa_p{i}", [128, 320], BF16) for i in range(2)]
                ptb = sbuf(ctx, ph, "aa_pt", [128, 512], BF16)
                negmax = sbuf(ctx, ph, "aa_nm", [128, 1], F32)
                lrow = sbuf(ctx, ph, "aa_l", [128, 1], F32)
                linv = sbuf(ctx, ph, "aa_li", [128, 1], F32)
                xrt = sbuf(ctx, ph, "aa_xr", [128, KC * TT], F32)
                sc_ps = [psumt(ctx, ph, "aa_scps") for _ in range(2)]
                pt_ps = [psumt(ctx, ph, "aa_ptps", BF16) for _ in range(2)]
                av_ps = [psumt(ctx, ph, "aa_avps") for _ in range(2)]
                oj_ps = [psumt(ctx, ph, "aa_ojps") for _ in range(2)]

                def geom(qt):
                    wk = 128 if qt == 0 else 256
                    woff = 0 if qt == 0 else (qt - 1) * 128
                    ns = max(0, 2 * (qt - 1))
                    return wk, woff, ns

                def stageA(qt, h):
                    hc, hp = h // 2, 64 * (h % 2)
                    wk, woff, ns = geom(qt)
                    tot = wk + ns
                    pb = pbuf[h % 2]
                    pk = f"{pfx}P{h % 2}"
                    hkp = f"{spfx}q{qt}h{h}_"
                    def smm(e, hc=hc, hp=hp, qt=qt, wk=wk, woff=woff, ns=ns):
                        insts = [e.matmul(sc_ps[0][:, 0:wk],
                                          qb[hp:hp + 64, hc * T + qt * 128: hc * T + qt * 128 + 128],
                                          kb[hp:hp + 64, hc * T + woff: hc * T + woff + wk],
                                          start=True, stop=True)]
                        if ns > 0:
                            insts.append(e.matmul(sc_ps[1][:, 0:ns],
                                                  qb[hp:hp + 64, hc * T + qt * 128: hc * T + qt * 128 + 128],
                                                  kstrb[hp:hp + 64, hc * NS_ALL: hc * NS_ALL + ns],
                                                  start=True, stop=True))
                        return insts
                    prog.op("tensor", smm, reads=[pfx + "Q", pfx + "K", pfx + "kst"],
                            writes=[f"{pfx}scps0", f"{pfx}scps1"], tag=hkp + "smm")
                    def sev(e, qt=qt, wk=wk, ns=ns):
                        insts = [e.scalar_tensor_tensor(
                            scores[:, 0:wk], sc_ps[0][:, 0:wk], 0.125,
                            wbt[:, 256 - wk:256], OP.mult, OP.add)]
                        if ns > 0:
                            insts.append(e.tensor_scalar(
                                scores[:, wk:wk + ns], sc_ps[1][:, 0:ns],
                                0.125, None, OP.mult))
                        return insts
                    prog.op("vector", sev, reads=[f"{pfx}scps0", f"{pfx}scps1", pfx + "wb"],
                            writes=[pfx + "scores"], tag=hkp + "sev")
                    prog.op("vector", lambda e, tot=tot: e.reduce_max(
                            negmax[:], scores[:, 0:tot], AX, negate=True),
                            reads=[pfx + "scores"], writes=[pfx + "negmax"], tag=hkp + "neg")
                    if 0 < ns < SKM:
                        prog.op("vector", lambda e, wk=wk, tot=tot, pb=pb: e.memset(
                                pb[:, tot:wk + SKM], 0.0),
                                writes=[pk], tag=hkp + "pz")
                    prog.op("scalar", lambda e, tot=tot, pb=pb: e.activation(
                            pb[:, 0:tot], scores[:, 0:tot], AF.Exp,
                            bias=negmax[:], accum_out=lrow[:]),
                            reads=[pfx + "scores"], early=[pfx + "negmax"],
                            writes=[pk, pfx + "l"], tag=hkp + "exp")
                    prog.op("vector", lambda e: e.reciprocal(linv[:], lrow[:]),
                            reads=[pfx + "l"], writes=[pfx + "linv"], tag=hkp + "rcp")
                    prog.op("vector", lambda e, tot=tot, pb=pb: e.tensor_scalar(
                            pb[:, 0:tot], pb[:, 0:tot], linv[:], None, OP.mult),
                            reads=[pk], early=[pfx + "linv"], writes=[pk],
                            tag=hkp + "pscale")

                def stageB(qt, h):
                    hc, hp = h // 2, 64 * (h % 2)
                    qq = qt % 4
                    wk, woff, ns = geom(qt)
                    nwc = wk // 128
                    pb = pbuf[h % 2]
                    pk = f"{pfx}P{h % 2}"
                    hkp = f"{spfx}q{qt}h{h}_"
                    def tp(e, nwc=nwc, ns=ns, pb=pb):
                        insts = [e.transpose(pt_ps[0][:, j * 128:(j + 1) * 128],
                                             pb[:, j * 128:(j + 1) * 128], ctx.ident[:])
                                 for j in range(nwc)]
                        if ns > 0:
                            insts.append(e.transpose(pt_ps[1][0:SKM, 0:128],
                                                     pb[:, nwc * 128:nwc * 128 + SKM],
                                                     ctx.ident[:]))
                        return insts
                    prog.op("tensor", tp, reads=[pk, "c:ident"],
                            writes=[f"{pfx}ptps0", f"{pfx}ptps1"], tag=hkp + "tp")
                    def tpe(e, nwc=nwc, ns=ns):
                        insts = [e.copy(ptb[:, 0:nwc * 128], pt_ps[0][:, 0:nwc * 128])]
                        if ns > 0:
                            insts.append(e.copy(ptb[0:SKM, 384:512], pt_ps[1][0:SKM, 0:128]))
                        return insts
                    prog.op("scalar", tpe, reads=[f"{pfx}ptps0", f"{pfx}ptps1"],
                            writes=[pfx + "Pt"], tag=hkp + "tpe")
                    ab = h % 2
                    def av(e, h=h, ab=ab, qt=qt, nwc=nwc, ns=ns, woff=woff):
                        insts = []
                        for j in range(nwc):
                            tc = woff // 128 + j
                            insts.append(e.matmul(av_ps[ab][0:64, 0:128],
                                                  vb[:, tc * D + h * 64: tc * D + h * 64 + 64],
                                                  ptb[:, j * 128:(j + 1) * 128],
                                                  start=(j == 0), stop=(j == nwc - 1 and ns == 0)))
                        if ns > 0:
                            insts.append(e.matmul(av_ps[ab][0:64, 0:128],
                                                  vstr[0:SKM, h * 64: h * 64 + 64],
                                                  ptb[0:SKM, 384:512],
                                                  start=False, stop=True))
                        return insts
                    prog.op("tensor", av, reads=[pfx + "Pt", pfx + "V", pfx + "vstr"],
                            writes=[f"{pfx}avps{ab}"], tag=hkp + "av")
                    def ave(e, hc=hc, hp=hp, ab=ab, qq=qq):
                        return e.tensor_copy(
                            attn_out[hp:hp + 64, hc * TT + qq * 128: hc * TT + qq * 128 + 128],
                            av_ps[ab][0:64, 0:128])
                    prog.op("vector", ave, reads=[f"{pfx}avps{ab}"],
                            writes=[pfx + "ao"], tag=hkp + "ave")

                def phaseC(g):
                    prog.op("sync", lambda e, g=g, s=s: e.dma_start(out=cm3(xrt[:], TT),
                                                                    in_=fm3(src[s, g], TT)),
                            reads=[f"xr:{s}:{g}"], writes=[pfx + "xrt"], dma=True,
                            tag=f"{spfx}xr{g}")
                    for mo in range(KC):
                        ob = mo % 2
                        def omm(e, mo=mo, ob=ob):
                            return [e.matmul(oj_ps[ob][:],
                                             wout[:, c * D + mo * 128: c * D + mo * 128 + 128],
                                             attn_out[:, c * TT:(c + 1) * TT],
                                             start=(c == 0), stop=(c == KC - 1))
                                    for c in range(KC)]
                        prog.op("tensor", omm, reads=[pfx + "wout", pfx + "ao"],
                                writes=[f"{pfx}ojps{ob}"], tag=f"{spfx}omm{g}_{mo}")
                        def oev(e, mo=mo, ob=ob):
                            return e.tensor_add(xrt[:, mo * TT:(mo + 1) * TT],
                                                xrt[:, mo * TT:(mo + 1) * TT], oj_ps[ob][:])
                        prog.op("vector", oev, reads=[f"{pfx}ojps{ob}", pfx + "xrt"],
                                writes=[pfx + "xrt"], tag=f"{spfx}oev{g}_{mo}")
                    prog.op("sync", lambda e, g=g, s=s: e.dma_start(out=fm3(dst[s, g], TT),
                                                                    in_=cm3(xrt[:], TT)),
                            reads=[pfx + "xrt"], writes=[f"xr:{s}:{g}"],
                            dma=True, tag=f"{spfx}st{g}")

                items = [(qt, h) for qt in range(NQ) for h in range(16)]
                prev = None
                for it in items + [None]:
                    if it is not None:
                        stageA(*it)
                    if prev is not None:
                        stageB(*prev)
                        if prev[1] == 15 and prev[0] % 4 == 3:
                            phaseC(prev[0] // 4)
                    prev = it
                prog.fence()
        prog.fence()



def emit_moe(ctx, prog, src, dst, W, C, sidx):
    nc, T, NT, n_seq = ctx.nc, ctx.T, ctx.NT, ctx.n_seq
    pfx = f"s{sidx}m_"
    NS = TT // 128

    with ExitStack() as sc:
        xnorm = sbuf(ctx, sc, "m_xn", [128, KC * T], BF16)
        acc = sbuf(ctx, sc, "m_acc", [128, KC * T], F32)
        gfm = sbuf(ctx, sc, "m_gfm", [8, T], F32)
        ones1r = sbuf(ctx, sc, "m_o1r", [1, 128], F32)
        identf = sbuf(ctx, sc, "m_idf", [128, 128], F32)
        gwt = sbuf(ctx, sc, "m_gw", [128, KC * 8], BF16)
        gbias_sb = sbuf(ctx, sc, "m_gb", [1, 8], F32)
        gbrep = sbuf(ctx, sc, "m_gbr", [128, 8], F32)
        glog = sbuf(ctx, sc, "m_glog", [128, NS * 8], F32)
        gsb = sbuf(ctx, sc, "m_gsb", [128, 8], F32)
        gmax8 = sbuf(ctx, sc, "m_gm8", [128, 8], F32)
        gmask = sbuf(ctx, sc, "m_gmk", [128, 8], F32)
        gnm = sbuf(ctx, sc, "m_gnm", [128, 1], F32)
        gl = sbuf(ctx, sc, "m_gl", [128, 1], F32)
        gli = sbuf(ctx, sc, "m_gli", [128, 1], F32)
        gps = psumt(ctx, sc, "m_gps")
        gtps = psumt(ctx, sc, "m_gtps")

        sel8 = sbuf(ctx, sc, "m_sel", [8, 8 * 128], F32)
        prog.op("sync", lambda e: e.dma_start(out=sel8[:], in_=C["c_sel8"][:]),
                writes=[pfx + "sel8"], dma=True, tag=pfx + "ldsel")
        prog.op("sync", lambda e: e.dma_start(out=ones1r[:], in_=C["c_ones1r"][:]),
                writes=[pfx + "o1r"], dma=True, tag=pfx + "ldo1r")
        prog.op("sync", lambda e: e.dma_start(out=identf[:], in_=C["c_identf"][:]),
                writes=[pfx + "idf"], dma=True, tag=pfx + "ldidf")
        prog.op("sync", lambda e: e.dma_start(out=gbias_sb[:], in_=W["moe_gbias"][:]),
                writes=[pfx + "gb"], dma=True, tag=pfx + "ldgb")
        prog.op("gpsimd", lambda e: e.dma_start(out=cm3(gwt[:], 8), in_=fm3(W["moe_gw"], 8)),
                writes=[pfx + "gw"], dma=True, tag=pfx + "ldgw")
        prog.op("tensor", lambda e: e.matmul(gps[0:128, 0:8], ones1r[:], gbias_sb[:],
                                             start=True, stop=True),
                reads=[pfx + "o1r", pfx + "gb"], writes=[pfx + "gps"], tag=pfx + "gbrep_mm")
        prog.op("vector", lambda e: e.tensor_copy(gbrep[:], gps[0:128, 0:8]),
                reads=[pfx + "gps"], writes=[pfx + "gbrep"], tag=pfx + "gbrep_ev")

        for s in range(n_seq):
            spfx = f"{pfx}{s}_"
            prog.op("vector", lambda e: [e.memset(acc[:, c * T:(c + 1) * T], 0.0)
                                         for c in range(KC)],
                    writes=[pfx + "acc"], tag=spfx + "accz")
            with ExitStack() as ph:
                lnb = make_ln_bufs(ctx, ph)
                for ti in range(NT):
                    kp = f"{spfx}t{ti}_"
                    emit_ln_tile(ctx, prog, lnb, src[s, ti], f"xr:{s}:{ti}", 1e-5,
                                 kp, xnorm, ti * TT, pfx + "xn")
                    for st in range(NS):
                        def gmm(e, ti=ti, st=st):
                            return [e.matmul(gps[:, st * 8:(st + 1) * 8],
                                             xnorm[:, c * T + ti * TT + st * 128:
                                                   c * T + ti * TT + st * 128 + 128],
                                             gwt[:, c * 8:(c + 1) * 8],
                                             start=(c == 0), stop=(c == KC - 1))
                                    for c in range(KC)]
                        prog.op("tensor", gmm, reads=[pfx + "xn", pfx + "gw"],
                                writes=[pfx + "gps"], tag=kp + f"gmm{st}")
                    for st in range(NS):
                        sl = slice(st * 8, st * 8 + 8)
                        prog.op("vector", lambda e, sl=sl: e.tensor_copy(glog[:, sl], gps[:, sl]),
                                reads=[pfx + "gps"], writes=[pfx + "glog"], tag=kp + f"glev{st}")
                        tok0 = ti * TT + st * 128
                        prog.op("vector", lambda e, sl=sl: e.reduce_max(gnm[:], glog[:, sl], AX, negate=True),
                                reads=[pfx + "glog"], writes=[pfx + "gnm"], tag=kp + f"gmax{st}")
                        prog.op("scalar", lambda e, sl=sl: e.activation(glog[:, sl], glog[:, sl], AF.Exp,
                                                                        bias=gnm[:], accum_out=gl[:]),
                                reads=[pfx + "glog"], early=[pfx + "gnm"],
                                writes=[pfx + "glog", pfx + "gl"], tag=kp + f"gexp{st}")
                        prog.op("vector", lambda e: e.reciprocal(gli[:], gl[:]),
                                reads=[pfx + "gl"], writes=[pfx + "gli"], tag=kp + f"grcp{st}")
                        prog.op("vector", lambda e, sl=sl: e.tensor_scalar(glog[:, sl], glog[:, sl],
                                                                           gli[:], None, OP.mult),
                                reads=[pfx + "glog"], early=[pfx + "gli"],
                                writes=[pfx + "glog"], tag=kp + f"gsc{st}")
                        prog.op("vector", lambda e, sl=sl: e.tensor_add(gsb[:], glog[:, sl], gbrep[:]),
                                reads=[pfx + "glog", pfx + "gbrep"], writes=[pfx + "gsb"],
                                tag=kp + f"gsb{st}")
                        prog.op("vector", lambda e: e.max(out=gmax8[:], in_=gsb[:]),
                                reads=[pfx + "gsb"], writes=[pfx + "gm8"], tag=kp + f"gm8{st}")
                        prog.op("vector", lambda e: e.tensor_scalar(gmask[:], gsb[:], gmax8[:, 1:2],
                                                                    None, OP.is_ge),
                                reads=[pfx + "gsb"], early=[pfx + "gm8"],
                                writes=[pfx + "gmask"], tag=kp + f"gge{st}")
                        prog.op("vector", lambda e, sl=sl: e.tensor_mul(gmask[:], glog[:, sl], gmask[:]),
                                reads=[pfx + "glog", pfx + "gmask"], writes=[pfx + "gmask"],
                                tag=kp + f"ggate{st}")
                        prog.op("tensor", lambda e: e.transpose(gtps[0:8, 0:128], gmask[:, 0:8], identf[:]),
                                reads=[pfx + "gmask", pfx + "idf"], writes=[pfx + "gtps"],
                                tag=kp + f"gtp{st}")
                        prog.op("vector", lambda e, tok0=tok0: e.tensor_copy(
                                gfm[0:8, tok0:tok0 + 128], gtps[0:8, 0:128]),
                                reads=[pfx + "gtps"], writes=[pfx + "gfm"], tag=kp + f"gfev{st}")
                prog.fence()
            if s == 0:
                ctx.tap(prog, "xnorm", xnorm, [pfx + "xn"])
                ctx.tap(prog, "gfm", gfm, [pfx + "gfm"])
                ctx.tap(prog, "glog", glog, [pfx + "glog"])
                ctx.tap(prog, "gsb", gsb, [pfx + "gsb"])
                ctx.tap(prog, "gmask", gmask, [pfx + "gmask"])
                ctx.tap(prog, "gm8", gmax8, [pfx + "gm8"])

            with ExitStack() as ph:
                w1b = sbuf(ctx, ph, "m_w1", [128, KC * 1024], BF16)
                w3b = sbuf(ctx, ph, "m_w3", [128, KC * 1024], BF16)
                w2b = sbuf(ctx, ph, "m_w2", [128, KC * 1024], BF16)
                s1 = sbuf(ctx, ph, "m_s1", [128, KC * TT], BF16)
                h3g = sbuf(ctx, ph, "m_h3g", [128, KC * TT], BF16)
                hh = [sbuf(ctx, ph, f"m_hh{i}", [128, KC * TT], BF16) for i in range(2)]
                gp_sb = sbuf(ctx, ph, "m_gp", [128, TT], F32)
                pj = [psumt(ctx, ph, "m_pj") for _ in range(4)]
                gpl = psumt(ctx, ph, "m_gpl")

                def h_stage(e_, ti):
                    tk = f"{spfx}e{e_}t{ti}_"
                    if e_ < 8:
                        prog.op("tensor", lambda e, e_=e_, ti=ti: e.matmul(
                                gpl[:], sel8[:, e_ * 128:(e_ + 1) * 128],
                                gfm[0:8, ti * TT:(ti + 1) * TT],
                                start=True, stop=True),
                                reads=[pfx + "sel8", pfx + "gfm"], writes=[pfx + "gpl"],
                                tag=tk + "gplmm")
                        prog.op("vector", lambda e: e.tensor_copy(gp_sb[:], gpl[:]),
                                reads=[pfx + "gpl"], writes=[pfx + "gp"], tag=tk + "gplev")
                    for which, wb in (("h1", w1b), ("h3", w3b)):
                        for mo in range(KC):
                            bank = mo % 4
                            def mm(e, wb=wb, mo=mo, bank=bank, ti=ti):
                                return [e.matmul(pj[bank][:],
                                                 wb[:, c * 1024 + mo * 128: c * 1024 + mo * 128 + 128],
                                                 xnorm[:, c * T + ti * TT: c * T + ti * TT + TT],
                                                 start=(c == 0), stop=(c == KC - 1))
                                        for c in range(KC)]
                            prog.op("tensor", mm,
                                    reads=[pfx + ("w1" if which == "h1" else "w3"), pfx + "xn"],
                                    writes=[f"{pfx}pj{bank}"], tag=tk + which + f"mm{mo}")
                            if which == "h1":
                                def ev(e, mo=mo, bank=bank):
                                    return e.activation(s1[:, mo * TT:(mo + 1) * TT],
                                                        pj[bank][:], AF.Silu)
                                prog.op("scalar", ev, reads=[f"{pfx}pj{bank}"],
                                        writes=[pfx + "s1"], tag=tk + which + f"ev{mo}")
                            elif e_ < 8:
                                def ev(e, mo=mo, bank=bank):
                                    return e.tensor_mul(h3g[:, mo * TT:(mo + 1) * TT],
                                                        pj[bank][:], gp_sb[:])
                                prog.op("vector", ev, reads=[f"{pfx}pj{bank}", pfx + "gp"],
                                        writes=[pfx + "h3g"], tag=tk + which + f"ev{mo}")
                            else:
                                def ev(e, mo=mo, bank=bank):
                                    return e.tensor_copy(h3g[:, mo * TT:(mo + 1) * TT],
                                                         pj[bank][:])
                                prog.op("vector", ev, reads=[f"{pfx}pj{bank}"],
                                        writes=[pfx + "h3g"], tag=tk + which + f"ev{mo}")
                    hb_ = hh[ti % 2]
                    hbk = f"{pfx}hh{ti % 2}"
                    prog.op("vector", lambda e, hb_=hb_: [e.tensor_mul(
                            hb_[:, c * TT:(c + 1) * TT], s1[:, c * TT:(c + 1) * TT],
                            h3g[:, c * TT:(c + 1) * TT]) for c in range(KC)],
                            reads=[pfx + "s1", pfx + "h3g"], writes=[hbk],
                            tag=tk + "comb")

                def w2_stage(e_, ti):
                    tk = f"{spfx}e{e_}t{ti}_"
                    hb_ = hh[ti % 2]
                    hbk = f"{pfx}hh{ti % 2}"
                    for mo in range(KC):
                        bank = mo % 4
                        def mm2(e, mo=mo, bank=bank, hb_=hb_):
                            return [e.matmul(pj[bank][:],
                                             w2b[:, c * 1024 + mo * 128: c * 1024 + mo * 128 + 128],
                                             hb_[:, c * TT:(c + 1) * TT],
                                             start=(c == 0), stop=(c == KC - 1))
                                    for c in range(KC)]
                        prog.op("tensor", mm2, reads=[pfx + "w2", hbk],
                                writes=[f"{pfx}pj{bank}"], tag=tk + f"w2mm{mo}")
                        def ev2(e, mo=mo, bank=bank, ti=ti):
                            col = mo * T + ti * TT
                            return e.tensor_add(acc[:, col:col + TT], acc[:, col:col + TT],
                                                pj[bank][:])
                        prog.op("vector", ev2, reads=[f"{pfx}pj{bank}", pfx + "acc"],
                                writes=[pfx + "acc"], tag=tk + f"w2ev{mo}")

                def ld(nm, buf, e_):
                    if USE_PRECAST:
                        def ldw(e, nm=nm, buf=buf, e_=e_):
                            return e.dma_start(out=cm3(buf[:], 1024),
                                               in_=fm3(ctx.moe_bf[nm + "x"][e_], 1024))
                        prog.op("gpsimd", ldw, reads=[f"moebf:{nm}x:{e_}"],
                                writes=[pfx + nm], dma=True,
                                tag=f"{spfx}e{e_}ld{nm}")
                    else:
                        def ldw(e, nm=nm, buf=buf, e_=e_):
                            return e.dma_start(out=cm3(buf[:], 1024),
                                               in_=fm3(W[f"moe_{nm}x"][e_], 1024))
                        prog.op("gpsimd", ldw, writes=[pfx + nm], dma=True,
                                tag=f"{spfx}e{e_}ld{nm}")

                prev = None
                for e_ in range(10):
                    ld("w1", w1b, e_)
                    ld("w3", w3b, e_)
                    if NT == 1:
                        ld("w2", w2b, e_)
                        h_stage(e_, 0)
                        w2_stage(e_, 0)
                        continue
                    for ti in range(NT):
                        h_stage(e_, ti)
                        if prev is not None:
                            w2_stage(*prev)
                        if ti == 0:
                            ld("w2", w2b, e_)
                        prev = (e_, ti)
                if prev is not None:
                    w2_stage(*prev)
                prog.fence()

            with ExitStack() as ph:
                xrta = [sbuf(ctx, ph, f"m_xr{i}", [128, KC * TT], F32) for i in range(2)]
                for ti in range(NT):
                    xrt = xrta[ti % 2]
                    xk = pfx + "xrt" + str(ti % 2)
                    prog.op("sync", lambda e, ti=ti, s=s, xrt=xrt: e.dma_start(
                            out=cm3(xrt[:], TT), in_=fm3(src[s, ti], TT)),
                            reads=[f"xr:{s}:{ti}"], writes=[xk], dma=True,
                            tag=f"{spfx}rld{ti}")
                    def radd(e, ti=ti, xrt=xrt):
                        return [e.tensor_add(xrt[:, c * TT:(c + 1) * TT],
                                             xrt[:, c * TT:(c + 1) * TT],
                                             acc[:, c * T + ti * TT: c * T + ti * TT + TT])
                                for c in range(KC)]
                    prog.op("vector", radd, reads=[pfx + "acc", xk],
                            writes=[xk], tag=f"{spfx}radd{ti}")
                    prog.op("sync", lambda e, ti=ti, s=s, xrt=xrt: e.dma_start(
                            out=fm3(dst[s, ti], TT), in_=cm3(xrt[:], TT)),
                            reads=[xk], writes=[f"xr:{s}:{ti}"], dma=True,
                            tag=f"{spfx}rst{ti}")
                prog.fence()
        prog.fence()



def emit_l1ffn(ctx, prog, src, dst, W, C, sidx):
    nc, T, NT, n_seq = ctx.nc, ctx.T, ctx.NT, ctx.n_seq
    pfx = f"s{sidx}f_"
    TP = T + 1

    with ExitStack() as sc:
        xn = sbuf(ctx, sc, "f_xn", [128, KC * TP], BF16)
        mur = sbuf(ctx, sc, "f_mur", [128, KC], F32)
        muk = sbuf(ctx, sc, "f_muk", [128, KC], F32)
        wr = load_weight_bf16(ctx, prog, sc, W["f1_wr"], D, D, pfx + "wr", pfx + "wr")
        wk = load_weight_bf16(ctx, prog, sc, W["f1_wk"], D, D, pfx + "wk", pfx + "wk")
        wv = load_weight_bf16(ctx, prog, sc, W["f1_wv"], D, D, pfx + "wv", pfx + "wv")
        prog.op("sync", lambda e: e.dma_start(out=mur[:], in_=W["f1_mur"][:]),
                writes=[pfx + "mur"], dma=True, tag=pfx + "ldmur")
        prog.op("sync", lambda e: e.dma_start(out=muk[:], in_=W["f1_muk"][:]),
                writes=[pfx + "muk"], dma=True, tag=pfx + "ldmuk")
        prog.op("vector", lambda e: [e.memset(xn[:, c * TP:c * TP + 1], 0.0)
                                     for c in range(KC)],
                writes=[pfx + "z0"], tag=pfx + "zcol")

        for s in range(ctx.n_seq):
            spfx = f"{pfx}{s}_"
            with ExitStack() as ph:
                with ExitStack() as lnph:
                    lnb = make_ln_bufs(ctx, lnph)
                    for ti in range(NT):
                        kp = f"{spfx}t{ti}_"
                        emit_ln_tile(ctx, prog, lnb, src[s, ti], f"xr:{s}:{ti}", 1e-5,
                                     kp, xn, ti * TT + 1, pfx + "xn", T_out=TP)
                    prog.fence()
                t1a = [sbuf(ctx, ph, "f_t1", [128, KC * TT], BF16)] * 2
                xmra = [sbuf(ctx, ph, f"f_xmr{i}", [128, KC * TT], BF16) for i in range(2)]
                xmka = [sbuf(ctx, ph, f"f_xmk{i}", [128, KC * TT], BF16) for i in range(2)]
                rbufa = [sbuf(ctx, ph, f"f_r{i}", [128, KC * TT], BF16) for i in range(2)]
                kka = [sbuf(ctx, ph, f"f_kk{i}", [128, KC * TT], BF16) for i in range(2)]
                tka = [sbuf(ctx, ph, f"f_tk{i}", [128, KC * TT], BF16) for i in range(2)]
                xrta = [sbuf(ctx, ph, f"f_xrt{i}", [128, KC * TT], F32) for i in range(2)]
                pj = [psumt(ctx, ph, "f_pj") for _ in range(4)]

                for ti in range(NT):
                    kp = f"{spfx}t{ti}_"
                    par = ti % 2
                    t1, xmr, xmk = t1a[par], xmra[par], xmka[par]
                    rbuf, kk, tk_, xrt = rbufa[par], kka[par], tka[par], xrta[par]
                    P_ = str(par)
                    zdep = [pfx + "z0"] if ti == 0 else []
                    def shift_ops(e, ti=ti, t1=t1):
                        return [e.tensor_sub(t1[:, c * TT:(c + 1) * TT],
                                             xn[:, c * TP + 1 + ti * TT: c * TP + 1 + ti * TT + TT],
                                             xn[:, c * TP + ti * TT: c * TP + ti * TT + TT])
                                for c in range(KC)]
                    prog.op("vector", shift_ops, reads=[pfx + "xn"] + zdep,
                            writes=[pfx + "t1"], tag=kp + "sh")
                    for which, mu, dstb, dkey in (("r", mur, xmr, pfx + "xmr" + P_),
                                                  ("k", muk, xmk, pfx + "xmk" + P_)):
                        def mix(e, ti=ti, mu=mu, dstb=dstb, t1=t1):
                            return [e.scalar_tensor_tensor(
                                dstb[:, c * TT:(c + 1) * TT], t1[:, c * TT:(c + 1) * TT],
                                mu[:, c:c + 1],
                                xn[:, c * TP + ti * TT: c * TP + ti * TT + TT],
                                OP.mult, OP.add) for c in range(KC)]
                        prog.op("vector", mix,
                                reads=[pfx + "t1", pfx + "xn", pfx + "mu" + which] + zdep,
                                writes=[dkey], tag=kp + "mix" + which)
                    for mo in range(KC):
                        bank = mo % 4
                        def mmr(e, mo=mo, bank=bank, xmr=xmr):
                            return [e.matmul(pj[bank][:],
                                             wr[:, c * D + mo * 128: c * D + mo * 128 + 128],
                                             xmr[:, c * TT:(c + 1) * TT],
                                             start=(c == 0), stop=(c == KC - 1))
                                    for c in range(KC)]
                        prog.op("tensor", mmr, reads=[pfx + "wr", pfx + "xmr" + P_],
                                writes=[f"{pfx}pj{bank}"], tag=kp + f"rmm{mo}")
                        prog.op("scalar", lambda e, mo=mo, bank=bank, rbuf=rbuf:
                                e.activation(rbuf[:, mo * TT:(mo + 1) * TT], pj[bank][:],
                                             AF.Sigmoid),
                                reads=[f"{pfx}pj{bank}"], writes=[pfx + "r" + P_],
                                tag=kp + f"rev{mo}")
                    for mo in range(KC):
                        bank = mo % 4
                        def mmk(e, mo=mo, bank=bank, xmk=xmk):
                            return [e.matmul(pj[bank][:],
                                             wk[:, c * D + mo * 128: c * D + mo * 128 + 128],
                                             xmk[:, c * TT:(c + 1) * TT],
                                             start=(c == 0), stop=(c == KC - 1))
                                    for c in range(KC)]
                        prog.op("tensor", mmk, reads=[pfx + "wk", pfx + "xmk" + P_],
                                writes=[f"{pfx}pj{bank}"], tag=kp + f"kmm{mo}")
                        prog.op("vector", lambda e, mo=mo, bank=bank, tk_=tk_:
                                e.tensor_scalar(tk_[:, mo * TT:(mo + 1) * TT], pj[bank][:],
                                                0.0, None, OP.max),
                                reads=[f"{pfx}pj{bank}"], writes=[pfx + "tk" + P_],
                                tag=kp + f"krelu{mo}")
                    prog.op("vector", lambda e, kk=kk, tk_=tk_: [e.tensor_mul(
                            kk[:, c * TT:(c + 1) * TT],
                            tk_[:, c * TT:(c + 1) * TT],
                            tk_[:, c * TT:(c + 1) * TT]) for c in range(KC)],
                            reads=[pfx + "tk" + P_], writes=[pfx + "kk" + P_], tag=kp + "ksq")
                    prog.op("sync", lambda e, ti=ti, s=s, xrt=xrt: e.dma_start(
                            out=cm3(xrt[:], TT), in_=fm3(src[s, ti], TT)),
                            reads=[f"xr:{s}:{ti}"], writes=[pfx + "xrt" + P_], dma=True,
                            tag=kp + "xrld")
                    for mo in range(KC):
                        bank = mo % 4
                        def mmv(e, mo=mo, bank=bank, kk=kk):
                            return [e.matmul(pj[bank][:],
                                             wv[:, c * D + mo * 128: c * D + mo * 128 + 128],
                                             kk[:, c * TT:(c + 1) * TT],
                                             start=(c == 0), stop=(c == KC - 1))
                                    for c in range(KC)]
                        prog.op("tensor", mmv, reads=[pfx + "wv", pfx + "kk" + P_],
                                writes=[f"{pfx}pj{bank}"], tag=kp + f"vmm{mo}")
                        prog.op("vector", lambda e, mo=mo, bank=bank, tk_=tk_, rbuf=rbuf:
                                e.tensor_mul(tk_[:, mo * TT:(mo + 1) * TT], pj[bank][:],
                                             rbuf[:, mo * TT:(mo + 1) * TT]),
                                reads=[f"{pfx}pj{bank}", pfx + "r" + P_], writes=[pfx + "tk" + P_],
                                tag=kp + f"vev{mo}")
                    prog.op("vector", lambda e, xrt=xrt, tk_=tk_: [e.tensor_add(
                            xrt[:, c * TT:(c + 1) * TT],
                            xrt[:, c * TT:(c + 1) * TT],
                            tk_[:, c * TT:(c + 1) * TT]) for c in range(KC)],
                            reads=[pfx + "tk" + P_, pfx + "xrt" + P_], writes=[pfx + "xrt" + P_],
                            tag=kp + "radd")
                    prog.op("sync", lambda e, ti=ti, s=s, xrt=xrt: e.dma_start(
                            out=fm3(dst[s, ti], TT), in_=cm3(xrt[:], TT)),
                            reads=[pfx + "xrt" + P_], writes=[f"xr:{s}:{ti}"], dma=True,
                            tag=kp + "xrst")
                prog.fence()
        prog.fence()



def emit_l1aa(ctx, prog, src, dst, W, C, sidx):
    nc, T, NT, n_seq = ctx.nc, ctx.T, ctx.NT, ctx.n_seq
    pfx = f"s{sidx}g_"
    RMS_EPS = 1.1920929e-07

    with ExitStack() as sc:
        xnf = sbuf(ctx, sc, "g_xn", [128, KC * T], BF16)
        qb = sbuf(ctx, sc, "g_q", [128, KC * T], BF16)
        kb = sbuf(ctx, sc, "g_k", [128, 2 * T], BF16)
        vb = sbuf(ctx, sc, "g_v", [128, (T // 128) * 256], BF16)
        ropepl = sbuf(ctx, sc, "g_rope", [128, T], F32)
        normw = sbuf(ctx, sc, "g_nw", [128, KC], F32)
        attn_out = sbuf(ctx, sc, "g_ao", [128, KC * TT], BF16)
        prog.op("sync", lambda e: e.dma_start(out=ropepl[:], in_=C["c_ropea1"][:]),
                writes=[pfx + "rope"], dma=True, tag=pfx + "ldrope")
        prog.op("sync", lambda e: e.dma_start(out=normw[:], in_=W["a1_normw"][:]),
                writes=[pfx + "nw"], dma=True, tag=pfx + "ldnw")

        for s in range(n_seq):
            spfx = f"{pfx}{s}_"
            with ExitStack() as ph:
                lnb = make_ln_bufs(ctx, ph)
                wq = load_weight_bf16(ctx, prog, ph, W["a1_wq"], D, D, pfx + "wq",
                                      pfx + f"wq{s}")
                wk = load_weight_bf16(ctx, prog, ph, W["a1_wk"], D, 256, pfx + "wk",
                                      pfx + f"wk{s}")
                wv = load_weight_bf16(ctx, prog, ph, W["a1_wv"], D, 256, pfx + "wv",
                                      pfx + f"wv{s}")
                xn2 = sbuf(ctx, ph, "g_xn2", [128, KC * TT], BF16)
                sqs = sbuf(ctx, ph, "g_sqs", [128, KC * TT], BF16)
                rstd2 = sbuf(ctx, ph, "g_rstd2", [128, TT], F32)
                tpl = sbuf(ctx, ph, "g_tpl", [128, TT], F32)
                ps3 = psumt(ctx, ph, "g_ps3")
                pj = [psumt(ctx, ph, "g_pj") for _ in range(4)]
                epst, epsk = ctx.constcol(prog, RMS_EPS)

                for ti in range(NT):
                    kp = f"{spfx}t{ti}_"
                    emit_ln_tile(ctx, prog, lnb, src[s, ti], f"xr:{s}:{ti}", 1e-5,
                                 kp, xnf, ti * TT, pfx + "xn")
                    prog.op("vector", lambda e, ti=ti: [e.tensor_mul(
                            sqs[:, c * TT:(c + 1) * TT],
                            xnf[:, c * T + ti * TT: c * T + ti * TT + TT],
                            xnf[:, c * T + ti * TT: c * T + ti * TT + TT]) for c in range(KC)],
                            reads=[pfx + "xn"], writes=[pfx + "sqs"], tag=kp + "rsq")
                    prog.op("tensor", lambda e: [e.matmul(ps3[:], ctx.ones128[:],
                                                          sqs[:, c * TT:(c + 1) * TT],
                                                          start=(c == 0), stop=(c == KC - 1))
                                                 for c in range(KC)],
                            reads=[pfx + "sqs", "c:ones"], writes=[pfx + "ps3"], tag=kp + "rmm")
                    prog.op("vector", lambda e: e.tensor_scalar(tpl[:], ps3[:], 1.0 / D, None, OP.mult),
                            reads=[pfx + "ps3"], writes=[pfx + "tpl"], tag=kp + "rsc")
                    prog.op("scalar", lambda e: e.activation(tpl[:], tpl[:], AF.Ln, bias=epst[:]),
                            reads=[pfx + "tpl"], early=[epsk], writes=[pfx + "tpl"], tag=kp + "rln")
                    prog.op("scalar", lambda e: e.activation(rstd2[:], tpl[:], AF.Exp, scale=-0.5),
                            reads=[pfx + "tpl"], writes=[pfx + "rstd2"], tag=kp + "rexp")
                    prog.op("vector", lambda e, ti=ti: [e.tensor_mul(
                            xn2[:, c * TT:(c + 1) * TT],
                            xnf[:, c * T + ti * TT: c * T + ti * TT + TT], rstd2[:])
                            for c in range(KC)],
                            reads=[pfx + "xn", pfx + "rstd2"], writes=[pfx + "xn2"],
                            tag=kp + "rmul")
                    prog.op("vector", lambda e: [e.tensor_scalar(
                            xn2[:, c * TT:(c + 1) * TT], xn2[:, c * TT:(c + 1) * TT],
                            normw[:, c:c + 1], None, OP.mult) for c in range(KC)],
                            reads=[pfx + "xn2"], early=[pfx + "nw"], writes=[pfx + "xn2"],
                            tag=kp + "rnw")
                    for which, wbuf, wkey, nmo, obuf, okey in (
                            ("q", wq, pfx + "wq", KC, qb, pfx + "Q"),
                            ("k", wk, pfx + "wk", 2, kb, pfx + "K")):
                        md = D if which == "q" else 256
                        for mo in range(nmo):
                            bank = mo % 4
                            def mm(e, wbuf=wbuf, mo=mo, bank=bank, md=md):
                                return [e.matmul(pj[bank][:],
                                                 wbuf[:, c * md + mo * 128: c * md + mo * 128 + 128],
                                                 xn2[:, c * TT:(c + 1) * TT],
                                                 start=(c == 0), stop=(c == KC - 1))
                                        for c in range(KC)]
                            prog.op("tensor", mm, reads=[wkey, pfx + "xn2"],
                                    writes=[f"{pfx}pj{bank}"], tag=kp + which + f"mm{mo}")
                            def ev(e, mo=mo, bank=bank, obuf=obuf, ti=ti):
                                return e.tensor_mul(obuf[:, mo * T + ti * TT: mo * T + ti * TT + TT],
                                                    pj[bank][:], ropepl[:, ti * TT:(ti + 1) * TT])
                            prog.op("vector", ev, reads=[f"{pfx}pj{bank}", pfx + "rope"],
                                    writes=[okey], tag=kp + which + f"ev{mo}")
                    for st in range(TT // 128):
                        bank = st % 4
                        def mmv(e, st=st, bank=bank, ti=ti):
                            return [e.matmul(pj[bank][0:128, 0:256],
                                             xn2[:, c * TT + st * 128: c * TT + st * 128 + 128],
                                             wv[:, c * 256:(c + 1) * 256],
                                             start=(c == 0), stop=(c == KC - 1))
                                    for c in range(KC)]
                        prog.op("tensor", mmv, reads=[pfx + "wv", pfx + "xn2"],
                                writes=[f"{pfx}pj{bank}"], tag=kp + f"vmm{st}")
                        tok = ti * (TT // 128) + st
                        prog.op("vector", lambda e, tok=tok, bank=bank: e.tensor_copy(
                                vb[:, tok * 256:(tok + 1) * 256], pj[bank][0:128, 0:256]),
                                reads=[f"{pfx}pj{bank}"], writes=[pfx + "V"],
                                tag=kp + f"vev{st}")
                prog.fence()

            with ExitStack() as ph:
                wo = load_weight_bf16(ctx, prog, ph, W["a1_wo"], D, D, pfx + "wo",
                                      pfx + f"wo{s}")
                scores = sbuf(ctx, ph, "g_sc", [128, T], F32)
                pbuf = [sbuf(ctx, ph, f"g_p{i}", [128, T], BF16) for i in range(2)]
                ptb = sbuf(ctx, ph, "g_pt", [128, T], BF16)
                mxc = sbuf(ctx, ph, "g_mx", [128, 1], F32)
                negmax = sbuf(ctx, ph, "g_nm", [128, 1], F32)
                lrow = sbuf(ctx, ph, "g_l", [128, 1], F32)
                linv = sbuf(ctx, ph, "g_li", [128, 1], F32)
                btile = [sbuf(ctx, ph, "g_bias", [128, T], F32) for _ in range(2)]
                xrt = sbuf(ctx, ph, "g_xr", [128, KC * TT], F32)
                sc_ps = [psumt(ctx, ph, "g_scps") for _ in range(2)]
                pt_ps = [psumt(ctx, ph, "g_ptps", BF16) for _ in range(2)]
                av_ps = [psumt(ctx, ph, "g_avps") for _ in range(2)]
                oj_ps = [psumt(ctx, ph, "g_ojps") for _ in range(2)]

                def stageA(qt, h):
                    hc, hp = h // 2, 64 * (h % 2)
                    kvh = h % 4
                    kc_, kp_ = kvh // 2, 64 * (kvh % 2)
                    bk = qt % 2
                    nkc = qt // 4 + 1
                    pb = pbuf[h % 2]
                    pk = f"{pfx}P{h % 2}"
                    hkp = f"{spfx}q{qt}h{h}_"
                    if h == 0:
                        prog.op("sync", lambda e, qt=qt, bk=bk, nkc=nkc: e.dma_start(
                                out=btile[bk][:, 0:nkc * 512],
                                in_=C["a1_bias"][qt * 128:(qt + 1) * 128, 0:nkc * 512]),
                                writes=[f"{pfx}bias{bk}"], dma=True, tag=f"{spfx}b{qt}")
                    for j in range(nkc):
                        sb_ = j % 2
                        def smm(e, hc=hc, hp=hp, qt=qt, j=j, sb_=sb_, kc_=kc_, kp_=kp_):
                            return e.matmul(sc_ps[sb_][:],
                                            qb[hp:hp + 64, hc * T + qt * 128: hc * T + qt * 128 + 128],
                                            kb[kp_:kp_ + 64, kc_ * T + j * 512: kc_ * T + j * 512 + 512],
                                            start=True, stop=True)
                        prog.op("tensor", smm, reads=[pfx + "Q", pfx + "K"],
                                writes=[f"{pfx}scps{sb_}"], tag=hkp + f"smm{j}")
                        def sev(e, j=j, sb_=sb_, bk=bk):
                            return e.scalar_tensor_tensor(
                                scores[:, j * 512:(j + 1) * 512], sc_ps[sb_][:],
                                0.125, btile[bk][:, j * 512:(j + 1) * 512],
                                OP.mult, OP.add)
                        prog.op("vector", sev,
                                reads=[f"{pfx}scps{sb_}", f"{pfx}bias{bk}"],
                                writes=[pfx + "scores"], tag=hkp + f"sev{j}")
                    prog.op("vector", lambda e, nkc=nkc: e.reduce_max(
                            negmax[:], scores[:, 0:nkc * 512], AX, negate=True),
                            reads=[pfx + "scores"], writes=[pfx + "negmax"], tag=hkp + "neg")
                    prog.op("scalar", lambda e, nkc=nkc, pb=pb: e.activation(
                            pb[:, 0:nkc * 512], scores[:, 0:nkc * 512], AF.Exp,
                            bias=negmax[:], accum_out=lrow[:]),
                            reads=[pfx + "scores"], early=[pfx + "negmax"],
                            writes=[pk, pfx + "l"], tag=hkp + "exp")
                    prog.op("vector", lambda e: e.reciprocal(linv[:], lrow[:]),
                            reads=[pfx + "l"], writes=[pfx + "linv"], tag=hkp + "rcp")
                    prog.op("vector", lambda e, nkc=nkc, pb=pb: e.tensor_scalar(
                            pb[:, 0:nkc * 512], pb[:, 0:nkc * 512], linv[:], None, OP.mult),
                            reads=[pk], early=[pfx + "linv"], writes=[pk],
                            tag=hkp + "pscale")

                def stageB(qt, h):
                    hc, hp = h // 2, 64 * (h % 2)
                    kvh = h % 4
                    qq = qt % 4
                    nkt = (qt // 4 + 1) * 4
                    pb = pbuf[h % 2]
                    pk = f"{pfx}P{h % 2}"
                    hkp = f"{spfx}q{qt}h{h}_"
                    nh_half = (nkt + 1) // 2
                    for half in range(2):
                        j0 = half * nh_half
                        j1 = min(nkt, j0 + nh_half)
                        if j0 >= j1:
                            continue
                        def tp(e, j0=j0, j1=j1, half=half, pb=pb):
                            return [e.transpose(pt_ps[half][:, (j - j0) * 128:(j - j0 + 1) * 128],
                                                pb[:, j * 128:(j + 1) * 128],
                                                ctx.ident[:])
                                    for j in range(j0, j1)]
                        prog.op("tensor", tp, reads=[pk, "c:ident"],
                                writes=[f"{pfx}ptps{half}"], tag=hkp + f"tp{half}")
                        def tpe(e, j0=j0, j1=j1, half=half):
                            return e.copy(ptb[:, j0 * 128:j1 * 128],
                                          pt_ps[half][:, 0:(j1 - j0) * 128])
                        prog.op("scalar", tpe, reads=[f"{pfx}ptps{half}"],
                                writes=[pfx + "Pt"], tag=hkp + f"tpe{half}")
                    ab = h % 2
                    def av(e, kvh=kvh, ab=ab, nkt=nkt):
                        return [e.matmul(av_ps[ab][0:64, 0:128],
                                         vb[:, tc * 256 + kvh * 64: tc * 256 + kvh * 64 + 64],
                                         ptb[:, tc * 128:(tc + 1) * 128],
                                         start=(tc == 0), stop=(tc == nkt - 1))
                                for tc in range(nkt)]
                    prog.op("tensor", av, reads=[pfx + "Pt", pfx + "V"],
                            writes=[f"{pfx}avps{ab}"], tag=hkp + "av")
                    def ave(e, hc=hc, hp=hp, ab=ab, qq=qq):
                        return e.copy(
                            attn_out[hp:hp + 64, hc * TT + qq * 128: hc * TT + qq * 128 + 128],
                            av_ps[ab][0:64, 0:128])
                    prog.op("scalar", ave, reads=[f"{pfx}avps{ab}"],
                            writes=[pfx + "ao"], tag=hkp + "ave")

                def phaseC(g):
                    prog.op("sync", lambda e, g=g, s=s: e.dma_start(out=cm3(xrt[:], TT),
                                                                    in_=fm3(src[s, g], TT)),
                            reads=[f"xr:{s}:{g}"], writes=[pfx + "xrt"], dma=True,
                            tag=f"{spfx}xr{g}")
                    prog.op("vector", lambda e, g=g: [e.tensor_add(
                            xrt[:, c * TT:(c + 1) * TT], xrt[:, c * TT:(c + 1) * TT],
                            xnf[:, c * T + g * TT: c * T + g * TT + TT]) for c in range(KC)],
                            reads=[pfx + "xn", pfx + "xrt"], writes=[pfx + "xrt"],
                            tag=f"{spfx}lnadd{g}")
                    for mo in range(KC):
                        ob = mo % 2
                        def omm(e, mo=mo, ob=ob):
                            return [e.matmul(oj_ps[ob][:],
                                             wo[:, c * D + mo * 128: c * D + mo * 128 + 128],
                                             attn_out[:, c * TT:(c + 1) * TT],
                                             start=(c == 0), stop=(c == KC - 1))
                                    for c in range(KC)]
                        prog.op("tensor", omm, reads=[pfx + "wo", pfx + "ao"],
                                writes=[f"{pfx}ojps{ob}"], tag=f"{spfx}omm{g}_{mo}")
                        def oev(e, mo=mo, ob=ob):
                            return e.tensor_add(xrt[:, mo * TT:(mo + 1) * TT],
                                                xrt[:, mo * TT:(mo + 1) * TT], oj_ps[ob][:])
                        prog.op("vector", oev, reads=[f"{pfx}ojps{ob}", pfx + "xrt"],
                                writes=[pfx + "xrt"], tag=f"{spfx}oev{g}_{mo}")
                    prog.op("sync", lambda e, g=g, s=s: e.dma_start(out=fm3(dst[s, g], TT),
                                                                    in_=cm3(xrt[:], TT)),
                            reads=[pfx + "xrt"], writes=[f"xr:{s}:{g}"],
                            dma=True, tag=f"{spfx}st{g}")

                NQ = T // 128
                items = [(qt, h) for qt in range(NQ) for h in range(16)]
                prev = None
                for it in items + [None]:
                    if it is not None:
                        stageA(*it)
                    if prev is not None:
                        stageB(*prev)
                        if prev[1] == 15 and prev[0] % 4 == 3:
                            phaseC(prev[0] // 4)
                    prev = it
                prog.fence()
        prog.fence()



def emit_mamba(ctx, prog, src, dst, W, C, sidx):
    nc, T, NT, n_seq = ctx.nc, ctx.T, ctx.NT, ctx.n_seq
    pfx = f"s{sidx}b_"
    NCH = 18
    DI = 2048
    IPW = 2336
    CS = 515
    RMS_EPS = 1.1920929e-07
    mbxn = nc.dram_tensor(ctx.name("mbxn"), [n_seq, NT, D, TT], BF16,
                          kind="Internal").ap()
    mbz = nc.dram_tensor(ctx.name("mbz"), [n_seq, NT, DI, TT], BF16,
                         kind="Internal").ap()

    with ExitStack() as sc:
        ipw = load_weight_bf16(ctx, prog, sc, W["mb_ipwr"], D, IPW, pfx + "ipw", pfx + "ipw")
        triu = sbuf(ctx, sc, "b_triu", [128, 128], F32)
        tribf = sbuf(ctx, sc, "b_tribf", [128, 128], BF16)
        sel32 = sbuf(ctx, sc, "b_sel", [32, 32 * 128], F32)
        convw = sbuf(ctx, sc, "b_cw", [128, NCH * 4], F32)
        convb = sbuf(ctx, sc, "b_cb", [128, NCH], F32)
        dtbpl = sbuf(ctx, sc, "b_dtb", [128, 32], F32)
        apl = sbuf(ctx, sc, "b_apl", [128, 32], F32)
        dpl = sbuf(ctx, sc, "b_dpl", [128, 32], F32)
        gnw = sbuf(ctx, sc, "b_gnw", [128, 16], F32)
        nw = sbuf(ctx, sc, "b_nw", [128, KC], F32)
        ones1r = sbuf(ctx, sc, "b_o1r", [1, 128], F32)
        row32 = sbuf(ctx, sc, "b_r32", [1, 32], F32)
        pfcol = sbuf(ctx, sc, "b_pf", [128, 1], F32)
        pmcol = sbuf(ctx, sc, "b_pm", [128, 1], F32)
        state = sbuf(ctx, sc, "b_S", [64, 32 * 64], F32)
        sbf = sbuf(ctx, sc, "b_Sbf", [64, 64], BF16)
        xmsum = sbuf(ctx, sc, "b_xms", [128, KC], F32)
        tred = sbuf(ctx, sc, "b_tred", [128, 1], F32)
        initsc = ExitStack()
        gps = psumt(ctx, initsc, "b_gps")

        for nm, buf in (("c_triu", triu), ("c_sel32", sel32)):
            prog.op("sync", lambda e, buf=buf, nm=nm: e.dma_start(out=buf[:], in_=C[nm][:]),
                    writes=[pfx + nm], dma=True, tag=pfx + "ld" + nm)
        prog.op("gpsimd", lambda e: e.dma_start(out=tribf[:], in_=C["c_triu"][:]),
                writes=[pfx + "c_tri_bf"], dma=True, tag=pfx + "ldtribf")
        for nm, buf in (("mb_convw", convw), ("mb_convb", convb), ("mb_gnw", gnw),
                        ("mb_nw", nw)):
            prog.op("sync", lambda e, buf=buf, nm=nm: e.dma_start(out=buf[:], in_=W[nm][:]),
                    writes=[pfx + nm], dma=True, tag=pfx + "ld" + nm)
        prog.op("sync", lambda e: e.dma_start(out=ones1r[:], in_=C["c_ones1r"][:]),
                writes=[pfx + "o1r"], dma=True, tag=pfx + "ldo1r")

        def repl32(wname, dstbuf, dkey, transform=None):
            prog.op("sync", lambda e: e.dma_start(out=row32[:], in_=W[wname][:]),
                    writes=[pfx + "r32"], dma=True, tag=pfx + "ldr_" + wname)
            if transform == "negexp":
                prog.op("scalar", lambda e: e.activation(row32[:], row32[:], AF.Exp),
                        reads=[pfx + "r32"], writes=[pfx + "r32"], tag=pfx + "nx1" + wname)
                prog.op("vector", lambda e: e.tensor_scalar(row32[:], row32[:], -1.0, None, OP.mult),
                        reads=[pfx + "r32"], writes=[pfx + "r32"], tag=pfx + "nx2" + wname)
            prog.op("tensor", lambda e: e.matmul(gps[0:128, 0:32], ones1r[:], row32[:],
                                                 start=True, stop=True),
                    reads=[pfx + "o1r", pfx + "r32"], writes=[pfx + "gps"], tag=pfx + "rp_" + wname)
            prog.op("vector", lambda e: e.tensor_copy(dstbuf[:], gps[0:128, 0:32]),
                    reads=[pfx + "gps"], writes=[dkey], tag=pfx + "rpe_" + wname)
        repl32("mb_dtb", dtbpl, pfx + "dtb")
        repl32("mb_alog", apl, pfx + "apl", transform="negexp")
        repl32("mb_dpar", dpl, pfx + "dpl")
        prog.fence()
        initsc.close()

        for s in range(n_seq):
            spfx = f"{pfx}{s}_"
            prog.op("vector", lambda e: [e.memset(state[:, i * 512:(i + 1) * 512], 0.0)
                                         for i in range(4)],
                    writes=[pfx + "S"], tag=spfx + "Sz")
            prog.op("vector", lambda e: e.memset(xmsum[:], 0.0),
                    writes=[pfx + "xms"], tag=spfx + "xmz")
            with ExitStack() as ph:
                lnb = make_ln_bufs(ctx, ph)
                xnt = sbuf(ctx, ph, "b_xnt", [128, KC * TT], BF16)
                sqs = sbuf(ctx, ph, "b_sqs", [128, KC * TT], BF16)
                rstd2 = sbuf(ctx, ph, "b_rstd2", [128, TT], F32)
                tpl = sbuf(ctx, ph, "b_tpl", [128, TT], F32)
                xr0 = sbuf(ctx, ph, "b_xr0", [128, KC * TT], F32)
                ipwz = load_weight_bf16(ctx, prog, ph, W["mb_ipwz"], D, DI,
                                        pfx + "ipwz", pfx + f"ipwz{s}")
                zt1 = sbuf(ctx, ph, "b_zt1", [128, 16 * TT], BF16)
                ps3 = psumt(ctx, ph, "b_ps3")
                pjz = [psumt(ctx, ph, "b_pjz") for _ in range(2)]
                epst, epsk = ctx.constcol(prog, RMS_EPS)
                for ti in range(NT):
                    kp = f"{spfx}t{ti}_"
                    emit_ln_tile(ctx, prog, lnb, src[s, ti], f"xr:{s}:{ti}", 1e-5,
                                 kp, xnt, 0, pfx + "xnt", T_out=TT)
                    KB = lnb["k"]
                    xt_, meanpl_ = lnb["xt"], lnb["mean"]
                    prog.op("vector", lambda e: [e.tensor_add(
                            xt_[:, c * TT:(c + 1) * TT], xt_[:, c * TT:(c + 1) * TT],
                            meanpl_[:]) for c in range(KC)],
                            reads=[KB + "xt", KB + "mean"], writes=[KB + "xt"],
                            tag=kp + "unmean")
                    prog.op("vector", lambda e: [e.tensor_add(
                            xr0[:, c * TT:(c + 1) * TT], xt_[:, c * TT:(c + 1) * TT],
                            xnt[:, c * TT:(c + 1) * TT]) for c in range(KC)],
                            reads=[KB + "xt", pfx + "xnt"], writes=[pfx + "xr0"],
                            tag=kp + "lnadd")
                    prog.op("sync", lambda e, ti=ti, s=s: e.dma_start(
                            out=fm3(dst[s, ti], TT), in_=cm3(xr0[:], TT)),
                            reads=[pfx + "xr0"], writes=[f"xr:{s}:{ti}"], dma=True,
                            tag=kp + "st0")
                    prog.op("vector", lambda e: [e.tensor_mul(
                            sqs[:, c * TT:(c + 1) * TT], xnt[:, c * TT:(c + 1) * TT],
                            xnt[:, c * TT:(c + 1) * TT]) for c in range(KC)],
                            reads=[pfx + "xnt"], writes=[pfx + "sqs"], tag=kp + "rsq")
                    prog.op("tensor", lambda e: [e.matmul(ps3[:], ctx.ones128[:],
                                                          sqs[:, c * TT:(c + 1) * TT],
                                                          start=(c == 0), stop=(c == KC - 1))
                                                 for c in range(KC)],
                            reads=[pfx + "sqs", "c:ones"], writes=[pfx + "ps3"], tag=kp + "rmm")
                    prog.op("vector", lambda e: e.tensor_scalar(tpl[:], ps3[:], 1.0 / D, None, OP.mult),
                            reads=[pfx + "ps3"], writes=[pfx + "tpl"], tag=kp + "rsc")
                    prog.op("scalar", lambda e: e.activation(tpl[:], tpl[:], AF.Ln, bias=epst[:]),
                            reads=[pfx + "tpl"], early=[epsk], writes=[pfx + "tpl"], tag=kp + "rln")
                    prog.op("scalar", lambda e: e.activation(rstd2[:], tpl[:], AF.Exp, scale=-0.5),
                            reads=[pfx + "tpl"], writes=[pfx + "rstd2"], tag=kp + "rexp")
                    prog.op("vector", lambda e: [e.tensor_mul(
                            xnt[:, c * TT:(c + 1) * TT], xnt[:, c * TT:(c + 1) * TT],
                            rstd2[:]) for c in range(KC)],
                            reads=[pfx + "xnt", pfx + "rstd2"], writes=[pfx + "xnt"],
                            tag=kp + "rmul")
                    prog.op("vector", lambda e: [e.tensor_scalar(
                            xnt[:, c * TT:(c + 1) * TT], xnt[:, c * TT:(c + 1) * TT],
                            nw[:, c:c + 1], None, OP.mult) for c in range(KC)],
                            reads=[pfx + "xnt"], early=[pfx + "mb_nw"], writes=[pfx + "xnt"],
                            tag=kp + "rnw")
                    prog.op("sync", lambda e, ti=ti, s=s: e.dma_start(
                            out=fm3(mbxn[s, ti], TT), in_=cm3(xnt[:], TT)),
                            reads=[pfx + "xnt"], writes=[f"{pfx}mbxn:{s}:{ti}"], dma=True,
                            tag=kp + "xnst")
                    prog.op("vector", lambda e: [e.reduce_sum(tred[:] if c == 0 else tred[:],
                                                              xnt[:, c * TT:(c + 1) * TT], AX)
                                                 for c in range(0)],
                            tag=kp + "nop") if False else None
                    for c in range(KC):
                        prog.op("vector", lambda e, c=c: e.reduce_sum(
                                tred[:], xnt[:, c * TT:(c + 1) * TT], AX),
                                reads=[pfx + "xnt"], writes=[pfx + "tred"], tag=kp + f"mr{c}")
                        prog.op("vector", lambda e, c=c: e.tensor_add(
                                xmsum[:, c:c + 1], xmsum[:, c:c + 1], tred[:]),
                                reads=[pfx + "tred", pfx + "xms"], writes=[pfx + "xms"],
                                tag=kp + f"ma{c}")
                    for mo in range(16):
                        bank = mo % 2
                        def mmz(e, mo=mo, bank=bank):
                            return [e.matmul(pjz[bank][:],
                                             ipwz[:, c * DI + mo * 128: c * DI + mo * 128 + 128],
                                             xnt[:, c * TT:(c + 1) * TT],
                                             start=(c == 0), stop=(c == KC - 1))
                                    for c in range(KC)]
                        prog.op("tensor", mmz, reads=[pfx + "ipwz", pfx + "xnt"],
                                writes=[f"{pfx}pjz{bank}"], tag=kp + f"zmm{mo}")
                        prog.op("scalar", lambda e, mo=mo, bank=bank: e.activation(
                                zt1[:, mo * TT:(mo + 1) * TT], pjz[bank][:], AF.Silu),
                                reads=[f"{pfx}pjz{bank}"], writes=[pfx + "zt1"],
                                tag=kp + f"zev{mo}")
                    prog.op("sync", lambda e, ti=ti, s=s: e.dma_start(
                            out=fm3(mbz[s, ti], TT), in_=cm3(zt1[:], TT)),
                            reads=[pfx + "zt1"], writes=[f"{pfx}mbz:{s}:{ti}"], dma=True,
                            tag=kp + "zst")
                prog.fence()

            with ExitStack() as ph:
                xmb = sbuf(ctx, ph, "b_xmb", [128, KC], BF16)
                rw1 = load_weight_bf16(ctx, prog, ph, W["mb_rw1"], D, 32, pfx + "rw1",
                                       pfx + f"rw1{s}")
                rw2 = sbuf(ctx, ph, "b_rw2", [32, 3], F32)
                rb1 = sbuf(ctx, ph, "b_rb1", [32, 1], F32)
                rb2 = sbuf(ctx, ph, "b_rb2", [1, 3], F32)
                hrow = sbuf(ctx, ph, "b_h", [32, 1], F32)
                probs = sbuf(ctx, ph, "b_probs", [1, 3], F32)
                pnm = sbuf(ctx, ph, "b_pnm", [1, 1], F32)
                pl_ = sbuf(ctx, ph, "b_pl", [1, 1], F32)
                pli = sbuf(ctx, ph, "b_pli", [1, 1], F32)
                pm1 = sbuf(ctx, ph, "b_pm1", [1, 1], F32)
                rps = psumt(ctx, ph, "b_rps")
                prog.op("sync", lambda e: e.dma_start(out=rw2[:], in_=W["mb_rw2"][:]),
                        writes=[pfx + "rw2"], dma=True, tag=spfx + "ldrw2")
                prog.op("sync", lambda e: e.dma_start(out=rb1[:], in_=W["mb_rb1"][:]),
                        writes=[pfx + "rb1"], dma=True, tag=spfx + "ldrb1")
                prog.op("sync", lambda e: e.dma_start(out=rb2[:], in_=W["mb_rb2"][:]),
                        writes=[pfx + "rb2"], dma=True, tag=spfx + "ldrb2")
                prog.op("vector", lambda e: e.tensor_scalar(xmb[:], xmsum[:], 1.0 / T, None, OP.mult),
                        reads=[pfx + "xms"], writes=[pfx + "xmb"], tag=spfx + "xmsc")
                prog.op("tensor", lambda e: [e.matmul(rps[0:32, 0:1],
                                                      rw1[:, c * 32:(c + 1) * 32],
                                                      xmb[:, c:c + 1],
                                                      start=(c == 0), stop=(c == KC - 1))
                                             for c in range(KC)],
                        reads=[pfx + "rw1", pfx + "xmb"], writes=[pfx + "rps"], tag=spfx + "r1mm")
                prog.op("vector", lambda e: e.tensor_add(hrow[:], rps[0:32, 0:1], rb1[:]),
                        reads=[pfx + "rps", pfx + "rb1"], writes=[pfx + "h"], tag=spfx + "r1b")
                prog.op("scalar", lambda e: e.activation(hrow[:], hrow[:], AF.Silu),
                        reads=[pfx + "h"], writes=[pfx + "h"], tag=spfx + "r1silu")
                prog.op("tensor", lambda e: e.matmul(rps[0:1, 0:3], hrow[:], rw2[:],
                                                     start=True, stop=True),
                        reads=[pfx + "h", pfx + "rw2"], writes=[pfx + "rps"], tag=spfx + "r2mm")
                prog.op("vector", lambda e: e.tensor_add(probs[:], rps[0:1, 0:3], rb2[:]),
                        reads=[pfx + "rps", pfx + "rb2"], writes=[pfx + "probs"], tag=spfx + "r2b")
                prog.op("vector", lambda e: e.reduce_max(pnm[:], probs[:], AX, negate=True),
                        reads=[pfx + "probs"], writes=[pfx + "pnm"], tag=spfx + "pmax")
                prog.op("scalar", lambda e: e.activation(probs[:], probs[:], AF.Exp,
                                                         bias=pnm[:], accum_out=pl_[:]),
                        reads=[pfx + "probs"], early=[pfx + "pnm"],
                        writes=[pfx + "probs", pfx + "plsum"], tag=spfx + "pexp")
                prog.op("vector", lambda e: e.reciprocal(pli[:], pl_[:]),
                        reads=[pfx + "plsum"], writes=[pfx + "pli"], tag=spfx + "prcp")
                prog.op("vector", lambda e: e.tensor_scalar(probs[:], probs[:], pli[:], None, OP.mult),
                        reads=[pfx + "probs"], early=[pfx + "pli"], writes=[pfx + "probs"],
                        tag=spfx + "pscale")
                prog.op("vector", lambda e: e.reduce_sum(pm1[:], probs[:, 1:3], AX),
                        reads=[pfx + "probs"], writes=[pfx + "pm1"], tag=spfx + "pmsum")
                prog.op("tensor", lambda e: e.matmul(rps[0:128, 0:1], ones1r[:], probs[0:1, 0:1],
                                                     start=True, stop=True),
                        reads=[pfx + "o1r", pfx + "probs"], writes=[pfx + "rps"], tag=spfx + "pfmm")
                prog.op("vector", lambda e: e.tensor_copy(pfcol[:], rps[0:128, 0:1]),
                        reads=[pfx + "rps"], writes=[pfx + "pfc"], tag=spfx + "pfe")
                prog.op("tensor", lambda e: e.matmul(rps[0:128, 0:1], ones1r[:], pm1[:],
                                                     start=True, stop=True),
                        reads=[pfx + "o1r", pfx + "pm1"], writes=[pfx + "rps"], tag=spfx + "pmmm")
                prog.op("vector", lambda e: e.tensor_copy(pmcol[:], rps[0:128, 0:1]),
                        reads=[pfx + "rps"], writes=[pfx + "pmc"], tag=spfx + "pme")
                prog.fence()

            with ExitStack() as ph:
                xnt2 = sbuf(ctx, ph, "b_xnt2", [128, KC * TT], BF16)
                zts = sbuf(ctx, ph, "b_zt", [128, 16 * TT], BF16)
                xbc = sbuf(ctx, ph, "b_xbc", [128, NCH * CS], BF16)
                hsv = sbuf(ctx, ph, "b_hsv", [128, NCH * 3], BF16)
                dtt = sbuf(ctx, ph, "b_dt", [128, 4 * 32], F32)
                dtp = sbuf(ctx, ph, "b_dtp", [128, 32], F32)
                dta = sbuf(ctx, ph, "b_dta", [128, 32], F32)
                ltok = sbuf(ctx, ph, "b_ltok", [128, 32], F32)
                lfm = sbuf(ctx, ph, "b_lfm", [32, 128], F32)
                gsb_ = sbuf(ctx, ph, "b_G", [128, 128], BF16)
                dcu = sbuf(ctx, ph, "b_dcu", [128, 128], F32)
                epl = sbuf(ctx, ph, "b_epl", [128, 128], F32)
                dcm = sbuf(ctx, ph, "b_dcm", [128, 128], BF16)
                msb = sbuf(ctx, ph, "b_M", [128, 128], BF16)
                cesb = sbuf(ctx, ph, "b_Ce", [64, 128], BF16)
                xtok = sbuf(ctx, ph, "b_xtok", [128, 16 * 128], BF16)
                btok = sbuf(ctx, ph, "b_btok", [128, 64], BF16)
                bw = sbuf(ctx, ph, "b_bw", [128, 64], BF16)
                ybuf = sbuf(ctx, ph, "b_y", [128, 16 * TT], BF16)
                gpl2 = sbuf(ctx, ph, "b_gpl2", [128, TT], F32)
                xrt = sbuf(ctx, ph, "b_xrt", [128, KC * TT], F32)
                t512 = sbuf(ctx, ph, "b_t512", [128, TT], F32)
                t512b = sbuf(ctx, ph, "b_t512b", [128, TT], F32)
                ysqb = [sbuf(ctx, ph, f"b_ysq{i}", [128, TT], BF16) for i in range(2)]
                opw = load_weight_bf16(ctx, prog, ph, W["mb_opw"], DI, D, pfx + "opw",
                                       pfx + f"opw{s}")
                pj = [psumt(ctx, ph, "b_pj") for _ in range(2)]
                ptp = [psumt(ctx, ph, "b_ptp", BF16) for _ in range(2)]
                pyy = [psumt(ctx, ph, "b_pyy")]
                pst_ = [psumt(ctx, ph, "b_pst")]
                plr = [psumt(ctx, ph, "b_plr") for _ in range(2)]
                xbk = pfx + "xbc"

                for ti in range(NT):
                    kp = f"{spfx}t{ti}_"
                    prog.op("sync", lambda e, ti=ti, s=s: e.dma_start(
                            out=cm3(xnt2[:], TT), in_=fm3(mbxn[s, ti], TT)),
                            reads=[f"{pfx}mbxn:{s}:{ti}"], writes=[pfx + "xnt2"], dma=True,
                            tag=kp + "xnld")
                    if ti == 0:
                        prog.op("vector", lambda e: [e.memset(
                                xbc[:, c * CS:c * CS + 3], 0.0) for c in range(NCH)],
                                writes=[xbk + "h"], tag=kp + "halo0")
                    else:
                        prog.op("vector", lambda e: [e.tensor_copy(
                                xbc[:, c * CS:c * CS + 3], hsv[:, c * 3:(c + 1) * 3])
                                for c in range(NCH)],
                                reads=[pfx + "hsv"], writes=[xbk + "h"], tag=kp + "halo")
                    prog.op("sync", lambda e, ti=ti, s=s: e.dma_start(
                            out=cm3(zts[:], TT), in_=fm3(mbz[s, ti], TT)),
                            reads=[f"{pfx}mbz:{s}:{ti}"], writes=[pfx + "zt"], dma=True,
                            tag=kp + "zld")
                    for mo in range(18):
                        bank = mo % 2
                        def mmip(e, mo=mo, bank=bank):
                            return [e.matmul(pj[bank][:],
                                             ipw[:, c * IPW + mo * 128: c * IPW + mo * 128 + 128],
                                             xnt2[:, c * TT:(c + 1) * TT],
                                             start=(c == 0), stop=(c == KC - 1))
                                    for c in range(KC)]
                        prog.op("tensor", mmip, reads=[pfx + "ipw", pfx + "xnt2"],
                                writes=[f"{pfx}pj{bank}"], tag=kp + f"ipmm{mo}")
                        cch = mo
                        def evx(e, cch=cch, bank=bank):
                            return e.tensor_copy(xbc[:, cch * CS + 3:cch * CS + 3 + TT],
                                                 pj[bank][:])
                        prog.op("vector", evx, reads=[f"{pfx}pj{bank}"],
                                writes=[xbk], tag=kp + f"ipxe{cch}")
                    for st in range(4):
                        bank = st % 2
                        def mmdt(e, st=st, bank=bank):
                            return [e.matmul(pj[bank][0:128, 0:32],
                                             xnt2[:, c * TT + st * 128: c * TT + st * 128 + 128],
                                             ipw[:, c * IPW + 2304: c * IPW + 2336],
                                             start=(c == 0), stop=(c == KC - 1))
                                    for c in range(KC)]
                        prog.op("tensor", mmdt, reads=[pfx + "ipw", pfx + "xnt2"],
                                writes=[f"{pfx}pj{bank}"], tag=kp + f"dtmm{st}")
                        prog.op("vector", lambda e, st=st, bank=bank: e.tensor_copy(
                                dtt[:, st * 32:(st + 1) * 32], pj[bank][0:128, 0:32]),
                                reads=[f"{pfx}pj{bank}"], writes=[pfx + "dtt"],
                                tag=kp + f"dte{st}")
                    prog.op("vector", lambda e: [e.tensor_copy(
                            hsv[:, c * 3:(c + 1) * 3], xbc[:, c * CS + 512:c * CS + 515])
                            for c in range(NCH)],
                            reads=[xbk], writes=[pfx + "hsv"], tag=kp + "hsave")
                    for cch in range(NCH):
                        tb = t512 if cch % 2 == 0 else t512b
                        tbk = pfx + ("t512" if cch % 2 == 0 else "t512b")
                        def cv0(e, cch=cch, tb=tb):
                            return e.tensor_scalar(tb[:], xbc[:, cch * CS:cch * CS + TT],
                                                   convw[:, cch * 4:cch * 4 + 1], None, OP.mult)
                        prog.op("vector", cv0, reads=[xbk, xbk + "h"],
                                early=[pfx + "mb_convw"],
                                writes=[tbk], tag=kp + f"cv0_{cch}")
                        for kk_ in range(1, 4):
                            def cvk(e, cch=cch, kk_=kk_, tb=tb):
                                return e.scalar_tensor_tensor(
                                    tb[:], xbc[:, cch * CS + kk_:cch * CS + kk_ + TT],
                                    convw[:, cch * 4 + kk_:cch * 4 + kk_ + 1], tb[:],
                                    OP.mult, OP.add)
                            prog.op("vector", cvk, reads=[xbk, xbk + "h", tbk],
                                    early=[pfx + "mb_convw"],
                                    writes=[tbk], tag=kp + f"cv{kk_}_{cch}")
                        def cvs(e, cch=cch, tb=tb):
                            return e.activation(xbc[:, cch * CS + 3:cch * CS + 3 + TT], tb[:],
                                                AF.Silu, bias=convb[:, cch:cch + 1])
                        prog.op("scalar", cvs, reads=[tbk, pfx + "hsv"],
                                early=[pfx + "mb_convb"],
                                writes=[xbk], tag=kp + f"cvs{cch}")
                    for sub in range(4):
                        ck = kp + f"c{sub}_"
                        dsl = slice(sub * 32, sub * 32 + 32)
                        cB = 16 * CS + 3 + sub * 128
                        cC = 17 * CS + 3 + sub * 128
                        prog.op("vector", lambda e, dsl=dsl: e.tensor_add(dtp[:], dtt[:, dsl], dtbpl[:]),
                                reads=[pfx + "dtt", pfx + "dtb"], writes=[pfx + "dtp"], tag=ck + "d1")
                        prog.op("scalar", lambda e: e.activation(dta[:], dtp[:], AF.Abs),
                                reads=[pfx + "dtp"], writes=[pfx + "dta"], tag=ck + "d2")
                        prog.op("scalar", lambda e: e.activation(dta[:], dta[:], AF.Exp, scale=-1.0),
                                reads=[pfx + "dta"], writes=[pfx + "dta"], tag=ck + "d3")
                        one_c, one_k = ctx.constcol(prog, 1.0)
                        prog.op("scalar", lambda e: e.activation(dta[:], dta[:], AF.Ln, bias=one_c[:]),
                                reads=[pfx + "dta"], early=[one_k], writes=[pfx + "dta"], tag=ck + "d4")
                        prog.op("vector", lambda e: e.tensor_scalar(dtp[:], dtp[:], 0.0, None, OP.max),
                                reads=[pfx + "dtp"], writes=[pfx + "dtp"], tag=ck + "d5")
                        prog.op("vector", lambda e: e.tensor_add(dtp[:], dtp[:], dta[:]),
                                reads=[pfx + "dtp", pfx + "dta"], writes=[pfx + "dtp"], tag=ck + "d6")
                        prog.op("vector", lambda e: e.tensor_mul(dta[:], dtp[:], apl[:]),
                                reads=[pfx + "dtp", pfx + "apl"], writes=[pfx + "dta"], tag=ck + "d7")
                        prog.op("tensor", lambda e: e.matmul(pj[0][0:128, 0:32], triu[:], dta[:],
                                                             start=True, stop=True),
                                reads=[pfx + "c_triu", pfx + "dta"], writes=[f"{pfx}pj0"],
                                tag=ck + "lmm")
                        prog.op("vector", lambda e: e.tensor_copy(ltok[:], pj[0][0:128, 0:32]),
                                reads=[f"{pfx}pj0"], writes=[pfx + "ltok"], tag=ck + "le")
                        prog.op("tensor", lambda e: e.matmul(pj[1][0:32, 0:128], dta[:], triu[:],
                                                             start=True, stop=True),
                                reads=[pfx + "c_triu", pfx + "dta"], writes=[f"{pfx}pj1"],
                                tag=ck + "lfmm")
                        prog.op("vector", lambda e: e.tensor_copy(lfm[:], pj[1][0:32, 0:128]),
                                reads=[f"{pfx}pj1"], writes=[pfx + "lfm"], tag=ck + "lfe")
                        prog.op("tensor", lambda e, cB=cB, cC=cC: e.matmul(
                                pj[0][:, 0:128], xbc[0:64, cB:cB + 128],
                                xbc[0:64, cC:cC + 128], start=True, stop=True),
                                reads=[xbk], writes=[f"{pfx}pj0"], tag=ck + "gmm")
                        prog.op("vector", lambda e: e.tensor_copy(gsb_[:], pj[0][:, 0:128]),
                                reads=[f"{pfx}pj0"], writes=[pfx + "G"], tag=ck + "ge")
                        for xc in range(8):
                            bnk = xc % 2
                            def xtp(e, xc=xc, sub=sub, bnk=bnk):
                                return [e.transpose(ptp[bnk][:, j * 128:(j + 1) * 128],
                                                    xbc[:, (xc * 2 + j) * CS + 3 + sub * 128:
                                                        (xc * 2 + j) * CS + 3 + sub * 128 + 128],
                                                    ctx.ident[:]) for j in range(2)]
                            prog.op("tensor", xtp, reads=[xbk, "c:ident"],
                                    writes=[f"{pfx}ptp{bnk}"], tag=ck + f"xtp{xc}")
                            def xte(e, xc=xc, bnk=bnk):
                                insts = []
                                for j in range(2):
                                    for hh in range(2):
                                        h = (xc * 2 + j) * 2 + hh
                                        insts.append(e.tensor_scalar(
                                            xtok[:, h * 64:(h + 1) * 64],
                                            ptp[bnk][:, j * 128 + hh * 64: j * 128 + hh * 64 + 64],
                                            dtp[:, h:h + 1], None, OP.mult))
                                return insts
                            prog.op("vector", xte, reads=[f"{pfx}ptp{bnk}", pfx + "dtp"],
                                    writes=[pfx + "xtok"], tag=ck + f"xte{xc}")
                        prog.op("tensor", lambda e, cB=cB: e.transpose(
                                ptp[0][:, 0:64], xbc[0:64, cB:cB + 128],
                                ctx.ident[0:64, 0:64]),
                                reads=[xbk, "c:ident"], writes=[f"{pfx}ptp0"],
                                tag=ck + "btp")
                        prog.op("vector", lambda e: e.tensor_copy(btok[:], ptp[0][:, 0:64]),
                                reads=[f"{pfx}ptp0"], writes=[pfx + "btok"], tag=ck + "bte")
                        for h in range(32):
                            hk = ck + f"h{h}_"
                            hb = 64 * (h % 2)
                            ycol = (h // 2) * TT + sub * 128
                            lb = h % 2
                            prog.op("tensor", lambda e, h=h, lb=lb: e.matmul(
                                    plr[lb][:, 0:128], sel32[:, h * 128:(h + 1) * 128], lfm[:],
                                    start=True, stop=True),
                                    reads=[pfx + "c_sel32", pfx + "lfm"],
                                    writes=[f"{pfx}plr{lb}"], tag=hk + "lrep")
                            prog.op("vector", lambda e, h=h, lb=lb: e.scalar_tensor_tensor(
                                    dcu[:], plr[lb][:, 0:128], ltok[:, h:h + 1], triu[:],
                                    OP.subtract, OP.mult),
                                    reads=[f"{pfx}plr{lb}", pfx + "ltok", pfx + "c_triu"],
                                    writes=[pfx + "dcu"], tag=hk + "dif")
                            prog.op("scalar", lambda e, lb=lb: e.activation(
                                    epl[:], plr[lb][:, 0:128], AF.Exp),
                                    reads=[f"{pfx}plr{lb}", pfx + "dcu"],
                                    writes=[pfx + "epl"], tag=hk + "eex")
                            prog.op("scalar", lambda e: e.activation(dcu[:], dcu[:], AF.Exp),
                                    reads=[pfx + "dcu"], writes=[pfx + "dcu"], tag=hk + "dex")
                            prog.op("vector", lambda e: e.tensor_mul(dcm[:], dcu[:], tribf[:]),
                                    reads=[pfx + "dcu", pfx + "c_tri_bf"], writes=[pfx + "dcm"],
                                    tag=hk + "dcm")
                            prog.op("vector", lambda e: e.tensor_mul(msb[:], gsb_[:], dcm[:]),
                                    reads=[pfx + "G", pfx + "dcm"], writes=[pfx + "M"],
                                    tag=hk + "m")
                            prog.op("vector", lambda e, cC=cC: e.tensor_mul(
                                    cesb[:], xbc[0:64, cC:cC + 128], epl[0:64, :]),
                                    reads=[xbk, pfx + "epl"], writes=[pfx + "Ce"],
                                    tag=hk + "ce")
                            yb = 0
                            prog.op("vector", lambda e, h=h: e.tensor_copy(
                                    sbf[:], state[:, h * 64:(h + 1) * 64]),
                                    reads=[pfx + "S"], writes=[pfx + "Sbf"], tag=hk + "sbf")
                            def ymm(e, h=h, hb=hb, yb=yb):
                                i1 = e.matmul(pyy[yb][hb:hb + 64, 0:128],
                                              xtok[:, h * 64:(h + 1) * 64], msb[:],
                                              start=True, stop=False)
                                i2 = e.matmul(pyy[yb][hb:hb + 64, 0:128],
                                              sbf[:], cesb[:],
                                              start=False, stop=True)
                                return [i1, i2]
                            prog.op("tensor", ymm, reads=[pfx + "xtok", pfx + "M",
                                                          pfx + "Sbf", pfx + "Ce"],
                                    writes=[f"{pfx}pyy{yb}"], tag=hk + "ymm")
                            def yev(e, h=h, hb=hb, yb=yb, ycol=ycol, sub=sub):
                                return e.scalar_tensor_tensor(
                                    ybuf[hb:hb + 64, ycol:ycol + 128],
                                    xbc[hb:hb + 64, (h // 2) * CS + 3 + sub * 128:
                                        (h // 2) * CS + 3 + sub * 128 + 128],
                                    dpl[hb:hb + 64, h:h + 1], pyy[yb][hb:hb + 64, 0:128],
                                    OP.mult, OP.add)
                            prog.op("vector", yev, reads=[f"{pfx}pyy{yb}", xbk,
                                                          pfx + "dpl"],
                                    writes=[pfx + "y"], tag=hk + "yev")
                            prog.op("vector", lambda e: e.tensor_scalar(
                                    bw[:], btok[:], dcu[:, 127:128], None, OP.mult),
                                    reads=[pfx + "btok", pfx + "dcu"], writes=[pfx + "bw"],
                                    tag=hk + "bw")
                            sb2 = 0
                            prog.op("tensor", lambda e, h=h, sb2=sb2: e.matmul(
                                    pst_[sb2][0:64, 0:64], bw[:],
                                    xtok[:, h * 64:(h + 1) * 64], start=True, stop=True),
                                    reads=[pfx + "bw", pfx + "xtok"],
                                    writes=[f"{pfx}pst{sb2}"], tag=hk + "smm")
                            def sup(e, h=h, sb2=sb2):
                                return e.scalar_tensor_tensor(
                                    state[:, h * 64:(h + 1) * 64],
                                    state[:, h * 64:(h + 1) * 64],
                                    epl[0:64, 127:128],
                                    pst_[sb2][0:64, 0:64], OP.mult, OP.add)
                            prog.op("vector", sup, reads=[f"{pfx}pst{sb2}", pfx + "S",
                                                          pfx + "epl"],
                                    writes=[pfx + "S"], tag=hk + "sup")
                    prog.op("vector", lambda e: [e.tensor_mul(ybuf[:, cc * TT:(cc + 1) * TT],
                                                              ybuf[:, cc * TT:(cc + 1) * TT],
                                                              zts[:, cc * TT:(cc + 1) * TT])
                                                 for cc in range(16)],
                            reads=[pfx + "y", pfx + "zt"], writes=[pfx + "y"], tag=kp + "yg")
                    for cc in range(16):
                        tb = ysqb[cc % 2]
                        tbk = f"{pfx}ysq{cc % 2}"
                        prog.op("vector", lambda e, cc=cc, tb=tb: e.tensor_mul(
                                tb[:], ybuf[:, cc * TT:(cc + 1) * TT],
                                ybuf[:, cc * TT:(cc + 1) * TT]),
                                reads=[pfx + "y"], writes=[tbk], tag=kp + f"ysq{cc}")
                        prog.op("tensor", lambda e, cc=cc, tb=tb: e.matmul(
                                pj[0][:], ctx.ones128[:], tb[:],
                                start=(cc == 0), stop=(cc == 15)),
                                reads=[tbk, "c:ones"], writes=[f"{pfx}pj0"], tag=kp + f"gnmm{cc}")
                    prog.op("vector", lambda e: e.tensor_scalar(gpl2[:], pj[0][:], 1.0 / DI,
                                                                None, OP.mult),
                            reads=[f"{pfx}pj0"], writes=[pfx + "gpl2"], tag=kp + "gnsc")
                    eps2t, eps2k = ctx.constcol(prog, 1e-5)
                    prog.op("scalar", lambda e: e.activation(gpl2[:], gpl2[:], AF.Ln, bias=eps2t[:]),
                            reads=[pfx + "gpl2"], early=[eps2k], writes=[pfx + "gpl2"],
                            tag=kp + "gnln")
                    prog.op("scalar", lambda e: e.activation(gpl2[:], gpl2[:], AF.Exp, scale=-0.5),
                            reads=[pfx + "gpl2"], writes=[pfx + "gpl2"], tag=kp + "gnex")
                    prog.op("vector", lambda e: [e.tensor_mul(ybuf[:, cc * TT:(cc + 1) * TT],
                                                              ybuf[:, cc * TT:(cc + 1) * TT],
                                                              gpl2[:]) for cc in range(16)],
                            reads=[pfx + "y", pfx + "gpl2"], writes=[pfx + "y"], tag=kp + "gnap")
                    prog.op("vector", lambda e: [e.tensor_scalar(ybuf[:, cc * TT:(cc + 1) * TT],
                                                                 ybuf[:, cc * TT:(cc + 1) * TT],
                                                                 gnw[:, cc:cc + 1], None, OP.mult)
                                                 for cc in range(16)],
                            reads=[pfx + "y"], early=[pfx + "mb_gnw"], writes=[pfx + "y"],
                            tag=kp + "gnw")
                    prog.op("sync", lambda e, ti=ti, s=s: e.dma_start(out=cm3(xrt[:], TT),
                                                                      in_=fm3(dst[s, ti], TT)),
                            reads=[f"xr:{s}:{ti}"], writes=[pfx + "xrt"], dma=True,
                            tag=kp + "xrld")
                    for mo in range(KC):
                        bank = mo % 2
                        def mmo(e, mo=mo, bank=bank):
                            return [e.matmul(pj[bank][:],
                                             opw[:, cc * D + mo * 128: cc * D + mo * 128 + 128],
                                             ybuf[:, cc * TT:(cc + 1) * TT],
                                             start=(cc == 0), stop=(cc == 15))
                                    for cc in range(16)]
                        prog.op("tensor", mmo, reads=[pfx + "opw", pfx + "y"],
                                writes=[f"{pfx}pj{bank}"], tag=kp + f"omm{mo}")
                        def oev(e, mo=mo, bank=bank):
                            return e.tensor_scalar(t512[:], pj[bank][:], pmcol[:, 0:1],
                                                   None, OP.mult)
                        prog.op("vector", oev, reads=[f"{pfx}pj{bank}"], early=[pfx + "pmc"],
                                writes=[pfx + "t512"], tag=kp + f"oev{mo}")
                        def oc2(e, mo=mo):
                            return e.scalar_tensor_tensor(
                                t512[:], xnt2[:, mo * TT:(mo + 1) * TT],
                                pfcol[:, 0:1], t512[:], OP.mult, OP.add)
                        prog.op("vector", oc2, reads=[pfx + "xnt2", pfx + "t512"],
                                early=[pfx + "pfc"], writes=[pfx + "t512"], tag=kp + f"oc2{mo}")
                        def oc3(e, mo=mo):
                            return e.tensor_add(xrt[:, mo * TT:(mo + 1) * TT],
                                                xrt[:, mo * TT:(mo + 1) * TT], t512[:])
                        prog.op("vector", oc3, reads=[pfx + "t512", pfx + "xrt"],
                                writes=[pfx + "xrt"], tag=kp + f"oc3{mo}")
                    prog.op("sync", lambda e, ti=ti, s=s: e.dma_start(out=fm3(dst[s, ti], TT),
                                                                      in_=cm3(xrt[:], TT)),
                            reads=[pfx + "xrt"], writes=[f"xr:{s}:{ti}"], dma=True,
                            tag=kp + "xrst")
                prog.fence()
        prog.fence()



def emit_attn_b(ctx, prog, src, dst, W, C, sidx):
    nc, T, NT, n_seq = ctx.nc, ctx.T, ctx.NT, ctx.n_seq
    pfx = f"s{sidx}x_"
    NQ = T // 128
    NKC = T // 512
    NBIS = 16
    abxn = nc.dram_tensor(ctx.name("abxn"), [n_seq, NT, D, TT], BF16,
                          kind="Internal").ap()

    with ExitStack() as sc:
        iqb = sbuf(ctx, sc, "x_iq", [128, 2 * T], F32)
        ikb = sbuf(ctx, sc, "x_ik", [128, T], F32)
        iwt = sbuf(ctx, sc, "x_iw", [128, (T // 128) * 4], F32)
        qb = sbuf(ctx, sc, "x_q", [128, KC * T], BF16)
        kb = sbuf(ctx, sc, "x_k", [128, KC * T], BF16)
        vb = sbuf(ctx, sc, "x_v", [128, (T // 128) * D], BF16)

        for s in range(n_seq):
            spfx = f"{pfx}{s}_"
            with ExitStack() as ph:
                lnb = make_ln_bufs(ctx, ph)
                xn32 = sbuf(ctx, ph, "x_xn32", [128, KC * TT], F32)
                iqw = sbuf(ctx, ph, "x_iqw", [128, KC * 256], F32)
                ikw = sbuf(ctx, ph, "x_ikw", [128, KC * 64], F32)
                iww = sbuf(ctx, ph, "x_iww", [128, KC * 4], F32)
                pj = [psumt(ctx, ph, "x_pj") for _ in range(2)]
                prog.op("sync", lambda e: e.dma_start(out=cm3(iqw[:], 256),
                                                      in_=fm3(W["ab_iqw"], 256)),
                        writes=[pfx + "iqw"], dma=True, tag=spfx + "ldiqw")
                prog.op("sync", lambda e: e.dma_start(out=cm3(ikw[:], 64),
                                                      in_=fm3(W["ab_ikw"], 64)),
                        writes=[pfx + "ikw"], dma=True, tag=spfx + "ldikw")
                prog.op("sync", lambda e: e.dma_start(out=cm3(iww[:], 4),
                                                      in_=fm3(W["ab_iww"], 4)),
                        writes=[pfx + "iww"], dma=True, tag=spfx + "ldiww")
                for ti in range(NT):
                    kp = f"{spfx}t{ti}_"
                    emit_ln_tile(ctx, prog, lnb, src[s, ti], f"xr:{s}:{ti}", 1e-5,
                                 kp, None, 0, None)
                    xnb, kxnb = lnb["xbn"], lnb["k"] + "xbn"
                    xt_, rstd_, KB = lnb["xt"], lnb["rstd"], lnb["k"]
                    prog.op("vector", lambda e: [e.tensor_mul(
                            xn32[:, c * TT:(c + 1) * TT], xt_[:, c * TT:(c + 1) * TT],
                            rstd_[:]) for c in range(KC)],
                            reads=[KB + "xt", KB + "rstd"], writes=[pfx + "xn32"],
                            tag=kp + "xn32")
                    prog.op("sync", lambda e, ti=ti, s=s: e.dma_start(
                            out=fm3(abxn[s, ti], TT), in_=cm3(xnb[:], TT)),
                            reads=[kxnb], writes=[f"{pfx}abxn:{s}:{ti}"], dma=True,
                            tag=kp + "xnst")
                    for mo in range(2):
                        bank = mo % 2
                        def mmiq(e, mo=mo, bank=bank):
                            return [e.matmul(pj[bank][:],
                                             iqw[:, c * 256 + mo * 128: c * 256 + mo * 128 + 128],
                                             xn32[:, c * TT:(c + 1) * TT],
                                             start=(c == 0), stop=(c == KC - 1))
                                    for c in range(KC)]
                        prog.op("tensor", mmiq, reads=[pfx + "iqw", pfx + "xn32"],
                                writes=[f"{pfx}pj{bank}"], tag=kp + f"iqmm{mo}")
                        prog.op("vector", lambda e, mo=mo, bank=bank, ti=ti: e.tensor_copy(
                                iqb[:, mo * T + ti * TT: mo * T + ti * TT + TT], pj[bank][:]),
                                reads=[f"{pfx}pj{bank}"], writes=[pfx + "iq"],
                                tag=kp + f"iqev{mo}")
                    def mmik(e):
                        return [e.matmul(pj[0][0:64, 0:TT],
                                         ikw[:, c * 64:(c + 1) * 64],
                                         xn32[:, c * TT:(c + 1) * TT],
                                         start=(c == 0), stop=(c == KC - 1))
                                for c in range(KC)] + \
                               [e.matmul(pj[0][64:128, 0:TT],
                                         ikw[:, c * 64:(c + 1) * 64],
                                         xn32[:, c * TT:(c + 1) * TT],
                                         start=(c == 0), stop=(c == KC - 1))
                                for c in range(KC)]
                    prog.op("tensor", mmik, reads=[pfx + "ikw", pfx + "xn32"],
                            writes=[f"{pfx}pj0"], tag=kp + "ikmm")
                    prog.op("vector", lambda e, ti=ti: e.tensor_copy(
                            ikb[:, ti * TT:(ti + 1) * TT], pj[0][0:128, 0:TT]),
                            reads=[f"{pfx}pj0"], writes=[pfx + "ik"], tag=kp + "ikev")
                    for st in range(TT // 128):
                        def mmiw(e, st=st):
                            return [e.matmul(pj[1][0:128, 0:4],
                                             xn32[:, c * TT + st * 128: c * TT + st * 128 + 128],
                                             iww[:, c * 4:(c + 1) * 4],
                                             start=(c == 0), stop=(c == KC - 1))
                                    for c in range(KC)]
                        prog.op("tensor", mmiw, reads=[pfx + "iww", pfx + "xn32"],
                                writes=[f"{pfx}pj1"], tag=kp + f"iwmm{st}")
                        tok = ti * (TT // 128) + st
                        prog.op("vector", lambda e, tok=tok: e.tensor_copy(
                                iwt[:, tok * 4:(tok + 1) * 4], pj[1][0:128, 0:4]),
                                reads=[f"{pfx}pj1"], writes=[pfx + "iwt"], tag=kp + f"iwev{st}")
                prog.fence()

            with ExitStack() as ph:
                wqkv = load_weight_bf16(ctx, prog, ph, W["ab_wqkv"], D, 3072,
                                        pfx + "wqkv", pfx + f"wqkv{s}")
                xnt = [sbuf(ctx, ph, "x_xnt0", [128, KC * TT], BF16)] * 2
                cosp = sbuf(ctx, ph, "x_cos", [128, T], BF16)
                sinp = sbuf(ctx, ph, "x_sin", [128, T], BF16)
                swap = sbuf(ctx, ph, "x_swap", [128, 128], BF16)
                qsb = sbuf(ctx, ph, "x_qsb", [128, TT], BF16)
                rsb = sbuf(ctx, ph, "x_rsb", [128, TT], BF16)
                pj = [psumt(ctx, ph, "x_pj2") for _ in range(4)]
                rps = psumt(ctx, ph, "x_rps")
                prog.op("gpsimd", lambda e: e.dma_start(out=cosp[:], in_=C["ab_cos"][:]),
                        writes=[pfx + "cos"], dma=True, tag=spfx + "ldcos")
                prog.op("gpsimd", lambda e: e.dma_start(out=sinp[:], in_=C["ab_sin"][:]),
                        writes=[pfx + "sin"], dma=True, tag=spfx + "ldsin")
                prog.op("gpsimd", lambda e: e.dma_start(out=swap[:], in_=C["c_swap"][:]),
                        writes=[pfx + "swap"], dma=True, tag=spfx + "ldswap")
                for ti in range(NT):
                    kp = f"{spfx}q{ti}_"
                    xb_ = xnt[0]
                    xbk = pfx + "xnt0"
                    prog.op("sync", lambda e, ti=ti, s=s, xb_=xb_: e.dma_start(
                            out=cm3(xb_[:], TT), in_=fm3(abxn[s, ti], TT)),
                            reads=[f"{pfx}abxn:{s}:{ti}"], writes=[xbk], dma=True,
                            tag=kp + "xnld")
                    for which, off, buf, bkey in (("q", 0, qb, pfx + "Q"), ("k", D, kb, pfx + "K")):
                        for mo in range(KC):
                            bank = mo % 4
                            def mm(e, off=off, mo=mo, bank=bank, xb_=xb_):
                                return [e.matmul(pj[bank][:],
                                                 wqkv[:, c * 3072 + off + mo * 128:
                                                      c * 3072 + off + mo * 128 + 128],
                                                 xb_[:, c * TT:(c + 1) * TT],
                                                 start=(c == 0), stop=(c == KC - 1))
                                        for c in range(KC)]
                            prog.op("tensor", mm, reads=[pfx + "wqkv", xbk],
                                    writes=[f"{pfx}pj2{bank}"], tag=kp + which + f"mm{mo}")
                            prog.op("vector", lambda e, bank=bank: e.tensor_copy(
                                    qsb[:], pj[bank][:]),
                                    reads=[f"{pfx}pj2{bank}"], writes=[pfx + "qsb"],
                                    tag=kp + which + f"qe{mo}")
                            prog.op("tensor", lambda e: e.matmul(rps[:], swap[:], qsb[:],
                                                                 start=True, stop=True),
                                    reads=[pfx + "swap", pfx + "qsb"], writes=[pfx + "rps"],
                                    tag=kp + which + f"rot{mo}")
                            prog.op("vector", lambda e, ti=ti: e.tensor_mul(
                                    rsb[:], rps[:], sinp[:, ti * TT:(ti + 1) * TT]),
                                    reads=[pfx + "rps", pfx + "sin"], writes=[pfx + "rsb"],
                                    tag=kp + which + f"rs{mo}")
                            def rc(e, mo=mo, ti=ti, buf=buf):
                                return e.tensor_mul(qsb[:], qsb[:],
                                                    cosp[:, ti * TT:(ti + 1) * TT])
                            prog.op("vector", rc, reads=[pfx + "qsb", pfx + "cos"],
                                    writes=[pfx + "qsb"], tag=kp + which + f"rc{mo}")
                            def rsum(e, mo=mo, ti=ti, buf=buf):
                                return e.tensor_add(
                                    buf[:, mo * T + ti * TT: mo * T + ti * TT + TT],
                                    qsb[:], rsb[:])
                            prog.op("vector", rsum, reads=[pfx + "qsb", pfx + "rsb"],
                                    writes=[bkey], tag=kp + which + f"ra{mo}")
                    for st in range(TT // 128):
                        for half in range(2):
                            bank = (st * 2 + half) % 4
                            def mmv(e, st=st, half=half, bank=bank, xb_=xb_):
                                return [e.matmul(pj[bank][:],
                                                 xb_[:, c * TT + st * 128: c * TT + st * 128 + 128],
                                                 wqkv[:, c * 3072 + 2048 + half * 512:
                                                      c * 3072 + 2048 + half * 512 + 512],
                                                 start=(c == 0), stop=(c == KC - 1))
                                        for c in range(KC)]
                            prog.op("tensor", mmv, reads=[pfx + "wqkv", xbk],
                                    writes=[f"{pfx}pj2{bank}"], tag=kp + f"vmm{st}{half}")
                            tok = ti * (TT // 128) + st
                            def evv(e, tok=tok, half=half, bank=bank):
                                return e.tensor_copy(vb[:, tok * D + half * 512:
                                                        tok * D + half * 512 + 512],
                                                     pj[bank][:])
                            prog.op("vector", evv, reads=[f"{pfx}pj2{bank}"],
                                    writes=[pfx + "V"], tag=kp + f"vev{st}{half}")
                prog.fence()

            with ExitStack() as ph:
                ibuf = sbuf(ctx, ph, "x_I", [128, T], F32)
                itmp = sbuf(ctx, ph, "x_it", [128, 512], F32)
                bias = [sbuf(ctx, ph, f"x_bias{i}", [128, T], F32) for i in range(2)]
                lo = sbuf(ctx, ph, "x_lo", [128, 1], F32)
                hi = sbuf(ctx, ph, "x_hi", [128, 1], F32)
                thr = sbuf(ctx, ph, "x_thr", [128, 1], F32)
                cnt = sbuf(ctx, ph, "x_cnt", [128, 1], F32)
                cmask = sbuf(ctx, ph, "x_cm", [128, 1], U32)
                diagc = sbuf(ctx, ph, "x_diagc", [128, 128], F32)
                nthr = sbuf(ctx, ph, "x_nthr", [128, 1], F32)
                scores = sbuf(ctx, ph, "x_sc", [128, T], F32)
                pbuf = [sbuf(ctx, ph, f"x_p{i}", [128, T], BF16) for i in range(2)]
                ptb = sbuf(ctx, ph, "x_pt", [128, T], BF16)
                mxc = sbuf(ctx, ph, "x_mx", [128, 1], F32)
                negmax = sbuf(ctx, ph, "x_nm", [128, 1], F32)
                lrow = sbuf(ctx, ph, "x_l", [128, 1], F32)
                linv = sbuf(ctx, ph, "x_li", [128, 1], F32)
                attn_out = sbuf(ctx, ph, "x_ao", [128, KC * TT], BF16)
                xrt = sbuf(ctx, ph, "x_xr", [128, KC * TT], F32)
                sc_ps = [psumt(ctx, ph, "x_scps") for _ in range(2)]
                pt_ps = [psumt(ctx, ph, "x_ptps", BF16) for _ in range(2)]
                av_ps = [psumt(ctx, ph, "x_avps") for _ in range(2)]
                oj_ps = [psumt(ctx, ph, "x_ojps") for _ in range(2)]

                prog.op("sync", lambda e: e.dma_start(out=diagc[:], in_=C["c_diagc"][:]),
                        writes=[pfx + "diagc"], dma=True, tag=spfx + "lddiag")

                def idx_scores(qt):
                    nkc = min(NKC, qt // 4 + 1)
                    for h in range(4):
                        hp = 64 * (h % 2)
                        hc = h // 2
                        for j in range(nkc):
                            jb = j % 2
                            def dmm(e, hp=hp, hc=hc, qt=qt, j=j, jb=jb):
                                return e.matmul(oj_ps[jb][:],
                                                iqb[hp:hp + 64,
                                                    hc * T + qt * 128: hc * T + qt * 128 + 128],
                                                ikb[hp:hp + 64, j * 512:(j + 1) * 512],
                                                start=True, stop=True)
                            prog.op("tensor", dmm, reads=[pfx + "iq", pfx + "ik"],
                                    writes=[f"{pfx}ojps{jb}"], tag=f"{spfx}d{qt}h{h}j{j}")
                            def dev(e, jb=jb):
                                return e.activation(itmp[:], oj_ps[jb][:], AF.Relu)
                            prog.op("scalar", dev, reads=[f"{pfx}ojps{jb}"],
                                    writes=[pfx + "itmp"], tag=f"{spfx}de{qt}h{h}j{j}")
                            if h == 0:
                                def dacc(e, j=j, h=h, qt=qt):
                                    return e.tensor_scalar(
                                        ibuf[:, j * 512:(j + 1) * 512], itmp[:],
                                        iwt[:, qt * 4 + h: qt * 4 + h + 1], None, OP.mult)
                                prog.op("vector", dacc, reads=[pfx + "itmp"],
                                        early=[pfx + "iwt"],
                                        writes=[pfx + "I"], tag=f"{spfx}da{qt}h{h}j{j}")
                            else:
                                def dacc(e, j=j, h=h, qt=qt):
                                    return e.scalar_tensor_tensor(
                                        ibuf[:, j * 512:(j + 1) * 512], itmp[:],
                                        iwt[:, qt * 4 + h: qt * 4 + h + 1],
                                        ibuf[:, j * 512:(j + 1) * 512], OP.mult, OP.add)
                                prog.op("vector", dacc, reads=[pfx + "itmp", pfx + "I"],
                                        early=[pfx + "iwt"],
                                        writes=[pfx + "I"], tag=f"{spfx}da{qt}h{h}j{j}")
                    prog.op("vector", lambda e, qt=qt: e.tensor_add(
                            ibuf[:, qt * 128:(qt + 1) * 128],
                            ibuf[:, qt * 128:(qt + 1) * 128], diagc[:]),
                            reads=[pfx + "diagc", pfx + "I"], writes=[pfx + "I"],
                            tag=f"{spfx}imaskd{qt}")
                    if (qt + 1) * 128 < nkc * 512:
                        prog.op("vector", lambda e, qt=qt, nkc=nkc: e.memset(
                                ibuf[:, (qt + 1) * 128: nkc * 512], NEG),
                                writes=[pfx + "I"], tag=f"{spfx}imaskm{qt}")
                    prog.op("vector", lambda e, nkc=nkc: e.reduce_max(
                            hi[:], ibuf[:, 0:nkc * 512], AX),
                            reads=[pfx + "I"], writes=[pfx + "hi"], tag=f"{spfx}bh{qt}")
                    prog.op("vector", lambda e, qt=qt: e.tensor_reduce(
                            lo[:], ibuf[:, 0:qt * 128], AX, OP.min),
                            reads=[pfx + "I"], writes=[pfx + "lo"], tag=f"{spfx}bl{qt}")

                def bisect_iter(qt, it):
                    nkc = min(NKC, qt // 4 + 1)
                    prog.op("vector", lambda e: e.tensor_add(thr[:], lo[:], hi[:]),
                            reads=[pfx + "lo", pfx + "hi"], writes=[pfx + "thr"],
                            tag=f"{spfx}bt{qt}_{it}")
                    prog.op("vector", lambda e: e.tensor_scalar(
                            nthr[:], thr[:], -0.5, None, OP.mult),
                            reads=[pfx + "thr"], writes=[pfx + "nthr"],
                            tag=f"{spfx}bn{qt}_{it}")
                    def bsg(e, nkc=nkc):
                        return e.activation(ptb[:, 0:nkc * 512],
                                            ibuf[:, 0:nkc * 512], AF.Sign,
                                            bias=nthr[:], accum_out=cnt[:])
                    prog.op("scalar", bsg, reads=[pfx + "I"], early=[pfx + "nthr"],
                            writes=[pfx + "Pt", pfx + "cnt"],
                            tag=f"{spfx}bs{qt}_{it}")
                    cthr = float(1024 - nkc * 512)
                    prog.op("vector", lambda e, cthr=cthr: e.tensor_scalar(
                            cmask[:], cnt[:], cthr, None, OP.is_ge),
                            reads=[pfx + "cnt"], writes=[pfx + "cmvec"],
                            tag=f"{spfx}bc{qt}_{it}")
                    prog.op("vector", lambda e: e.tensor_scalar(
                            thr[:], thr[:], 0.5, None, OP.mult),
                            reads=[pfx + "thr"], writes=[pfx + "thr"],
                            tag=f"{spfx}bm{qt}_{it}")
                    prog.op("vector", lambda e: e.select(lo[:], cmask[:], thr[:], lo[:]),
                            reads=[pfx + "cmvec", pfx + "thr", pfx + "lo"],
                            writes=[pfx + "lo"], tag=f"{spfx}bu{qt}_{it}")
                    prog.op("vector", lambda e: e.select(hi[:], cmask[:], hi[:], thr[:]),
                            reads=[pfx + "cmvec", pfx + "thr", pfx + "hi"],
                            writes=[pfx + "hi"], tag=f"{spfx}bv{qt}_{it}")

                def mask_build(qt):
                    nkc = min(NKC, qt // 4 + 1)
                    bb = bias[qt % 2]
                    bbk = f"{pfx}biasb{qt % 2}"
                    prog.op("vector", lambda e, nkc=nkc, bb=bb: e.tensor_scalar(
                            bb[:, 0:nkc * 512], ibuf[:, 0:nkc * 512], lo[:],
                            None, OP.is_gt),
                            reads=[pfx + "I"], early=[pfx + "lo"],
                            writes=[bbk], tag=f"{spfx}bb{qt}")
                    prog.op("vector", lambda e, nkc=nkc, bb=bb: e.tensor_scalar(
                            bb[:, 0:nkc * 512], bb[:, 0:nkc * 512], -8.0 * NEG, 8.0 * NEG,
                            OP.mult, OP.add),
                            reads=[bbk], writes=[bbk], tag=f"{spfx}bf{qt}")

                def stageA(qt, h):
                    hc, hp = h // 2, 64 * (h % 2)
                    nkc = min(NKC, qt // 4 + 1)
                    pb = pbuf[h % 2]
                    pk = f"{pfx}P{h % 2}"
                    bb = bias[qt % 2]
                    bbk = f"{pfx}biasb{qt % 2}"
                    hkp = f"{spfx}q{qt}h{h}_"
                    for j in range(nkc):
                        sb_ = j % 2
                        def smm(e, hc=hc, hp=hp, qt=qt, j=j, sb_=sb_):
                            return e.matmul(sc_ps[sb_][:],
                                            qb[hp:hp + 64, hc * T + qt * 128: hc * T + qt * 128 + 128],
                                            kb[hp:hp + 64, hc * T + j * 512: hc * T + j * 512 + 512],
                                            start=True, stop=True)
                        prog.op("tensor", smm, reads=[pfx + "Q", pfx + "K"],
                                writes=[f"{pfx}scps{sb_}"], tag=hkp + f"smm{j}")
                        if qt >= 4:
                            def sev(e, j=j, sb_=sb_, bb=bb):
                                return e.scalar_tensor_tensor(
                                    scores[:, j * 512:(j + 1) * 512], sc_ps[sb_][:],
                                    0.125, bb[:, j * 512:(j + 1) * 512],
                                    OP.mult, OP.add)
                            prog.op("vector", sev,
                                    reads=[f"{pfx}scps{sb_}", bbk],
                                    writes=[pfx + "scores"], tag=hkp + f"sev{j}")
                        else:
                            def sev(e, j=j, sb_=sb_):
                                return e.tensor_scalar(
                                    scores[:, j * 512:(j + 1) * 512], sc_ps[sb_][:],
                                    0.125, None, OP.mult)
                            prog.op("vector", sev, reads=[f"{pfx}scps{sb_}"],
                                    writes=[pfx + "scores"], tag=hkp + f"sev{j}")
                    prog.op("vector", lambda e, nkc=nkc: e.reduce_max(
                            negmax[:], scores[:, 0:nkc * 512], AX, negate=True),
                            reads=[pfx + "scores"], writes=[pfx + "negmax"], tag=hkp + "neg")
                    prog.op("scalar", lambda e, nkc=nkc, pb=pb: e.activation(
                            pb[:, 0:nkc * 512], scores[:, 0:nkc * 512], AF.Exp,
                            bias=negmax[:], accum_out=lrow[:]),
                            reads=[pfx + "scores"], early=[pfx + "negmax"],
                            writes=[pk, pfx + "l"], tag=hkp + "exp")
                    prog.op("vector", lambda e: e.reciprocal(linv[:], lrow[:]),
                            reads=[pfx + "l"], writes=[pfx + "linv"], tag=hkp + "rcp")
                    prog.op("vector", lambda e, nkc=nkc, pb=pb: e.tensor_scalar(
                            pb[:, 0:nkc * 512], pb[:, 0:nkc * 512], linv[:], None, OP.mult),
                            reads=[pk], early=[pfx + "linv"], writes=[pk],
                            tag=hkp + "pscale")

                def stageB(qt, h):
                    hc, hp = h // 2, 64 * (h % 2)
                    qq = qt % 4
                    nkt = min(NKC, qt // 4 + 1) * 4
                    pb = pbuf[h % 2]
                    pk = f"{pfx}P{h % 2}"
                    hkp = f"{spfx}q{qt}h{h}_"
                    nh_half = (nkt + 1) // 2
                    for half in range(2):
                        j0 = half * nh_half
                        j1 = min(nkt, j0 + nh_half)
                        if j0 >= j1:
                            continue
                        def tp(e, j0=j0, j1=j1, half=half, pb=pb):
                            return [e.transpose(pt_ps[half][:, (j - j0) * 128:(j - j0 + 1) * 128],
                                                pb[:, j * 128:(j + 1) * 128],
                                                ctx.ident[:])
                                    for j in range(j0, j1)]
                        prog.op("tensor", tp, reads=[pk, "c:ident"],
                                writes=[f"{pfx}ptps{half}"], tag=hkp + f"tp{half}")
                        def tpe(e, j0=j0, j1=j1, half=half):
                            return e.copy(ptb[:, j0 * 128:j1 * 128],
                                          pt_ps[half][:, 0:(j1 - j0) * 128])
                        prog.op("scalar", tpe, reads=[f"{pfx}ptps{half}"],
                                writes=[pfx + "Pt"], tag=hkp + f"tpe{half}")
                    ab = h % 2
                    def av(e, h=h, ab=ab, nkt=nkt):
                        return [e.matmul(av_ps[ab][0:64, 0:128],
                                         vb[:, tc * D + h * 64: tc * D + h * 64 + 64],
                                         ptb[:, tc * 128:(tc + 1) * 128],
                                         start=(tc == 0), stop=(tc == nkt - 1))
                                for tc in range(nkt)]
                    prog.op("tensor", av, reads=[pfx + "Pt", pfx + "V"],
                            writes=[f"{pfx}avps{ab}"], tag=hkp + "av")
                    def ave(e, hc=hc, hp=hp, ab=ab, qq=qq):
                        return e.copy(
                            attn_out[hp:hp + 64, hc * TT + qq * 128: hc * TT + qq * 128 + 128],
                            av_ps[ab][0:64, 0:128])
                    prog.op("scalar", ave, reads=[f"{pfx}avps{ab}"],
                            writes=[pfx + "ao"], tag=hkp + "ave")

                def phaseC(g):
                    with ExitStack() as phg:
                        wo = load_weight_bf16(ctx, prog, phg, W["ab_wo"], D, D,
                                              pfx + "wo", pfx + f"wo{s}_{g}")
                        prog.op("sync", lambda e, g=g, s=s: e.dma_start(out=cm3(xrt[:], TT),
                                                                        in_=fm3(src[s, g], TT)),
                                reads=[f"xr:{s}:{g}"], writes=[pfx + "xrt"], dma=True,
                                tag=f"{spfx}xr{g}")
                        for mo in range(KC):
                            ob = mo % 2
                            def omm(e, mo=mo, ob=ob, wo=wo):
                                return [e.matmul(oj_ps[ob][:],
                                                 wo[:, c * D + mo * 128: c * D + mo * 128 + 128],
                                                 attn_out[:, c * TT:(c + 1) * TT],
                                                 start=(c == 0), stop=(c == KC - 1))
                                        for c in range(KC)]
                            prog.op("tensor", omm, reads=[pfx + "wo", pfx + "ao"],
                                    writes=[f"{pfx}ojps{ob}"], tag=f"{spfx}omm{g}_{mo}")
                            def oev(e, mo=mo, ob=ob):
                                return e.tensor_add(xrt[:, mo * TT:(mo + 1) * TT],
                                                    xrt[:, mo * TT:(mo + 1) * TT], oj_ps[ob][:])
                            prog.op("vector", oev, reads=[f"{pfx}ojps{ob}", pfx + "xrt"],
                                    writes=[pfx + "xrt"], tag=f"{spfx}oev{g}_{mo}")
                        prog.op("sync", lambda e, g=g, s=s: e.dma_start(out=fm3(dst[s, g], TT),
                                                                        in_=cm3(xrt[:], TT)),
                                reads=[pfx + "xrt"], writes=[f"xr:{s}:{g}"],
                                dma=True, tag=f"{spfx}st{g}")
                        prog.fence()

                def idx_work(qt, h):
                    if qt >= len(idx_sched):
                        return
                    sched = idx_sched[qt]
                    for fn in sched.get(h, ()):
                        fn()

                NQ_ = T // 128
                idx_sched = {}
                for tq in range(4, NQ_):
                    sl = {}
                    sl[0] = [lambda tq=tq: idx_scores(tq)]
                    for it in range(NBIS):
                        slot = 1 + (it * 14) // NBIS
                        sl.setdefault(slot, []).append(lambda tq=tq, it=it: bisect_iter(tq, it))
                    sl[15] = [lambda tq=tq: mask_build(tq)]
                    idx_sched[tq] = sl

                prev = None
                for qt in range(NQ_):
                    for h in range(16):
                        stageA(qt, h)
                        if qt + 1 >= 4 and (qt + 1) in idx_sched:
                            for fn in idx_sched[qt + 1].get(h, ()):
                                fn()
                        if prev is not None:
                            stageB(*prev)
                            if prev[1] == 15 and prev[0] % 4 == 3:
                                phaseC(prev[0] // 4)
                        prev = (qt, h)
                stageB(*prev)
                phaseC(NT - 1)
                prog.fence()
        prog.fence()



def emit_moe_precast(ctx, prog, W):
    nc = ctx.nc
    out = {}
    pcscope = ExitStack()
    stg = [pcscope.enter_context(
        nc.sbuf_tensor(ctx.name("pcst"), [128, 2048], BF16))
        for _ in range(2)]
    pieces = []
    for nm in ("w1x", "w3x", "w2x"):
        d = nc.dram_tensor(ctx.name("moebf_" + nm), [10, 1024, 1024], BF16,
                           kind="Internal").ap()
        out[nm] = d
        for e_ in range(10):
            for q in range(4):
                pieces.append((nm, d, e_, q))
    for i, (nm, d, e_, q) in enumerate(pieces):
        b = i % 2
        key = f"pc{b}"
        def ldp(e, nm=nm, e_=e_, q=q, b=b):
            return e.dma_start(out=stg[b][:].rearrange("p (c t) -> p c t", t=1024),
                               in_=W[f"moe_{nm}"][e_].rearrange("(c p) t -> p c t", p=128)
                               [:, 2 * q:2 * q + 2, :])
        prog.op("gpsimd", ldp, writes=[key], dma=True, tag=f"pcl{i}")
        def stp(e, d=d, e_=e_, q=q, b=b):
            return e.dma_start(out=d[e_].rearrange("(c p) t -> p c t", p=128)
                               [:, 2 * q:2 * q + 2, :],
                               in_=stg[b][:].rearrange("p (c t) -> p c t", t=1024))
        prog.op("sync", stp, reads=[key], writes=[f"moebf:{nm}:{e_}"], dma=True,
                tag=f"pcs{i}")
    prog.fence()
    pcscope.close()
    return out



SUB_EMITTERS = {}


def build_program(T=2048, n_seq=2, subs=("attn_a0",), wnames=(), cnames=()):
    nc = bass.Bass("TRN2", target_bir_lowering=False, detect_race_conditions=False)
    NT = T // TT
    stack = ExitStack()
    ctx = Ctx(nc, stack, T, n_seq)

    xin = nc.dram_tensor("xin", [n_seq, NT, D, TT], F32, kind="ExternalInput").ap()
    xout = nc.dram_tensor("xout", [n_seq, NT, D, TT], F32, kind="ExternalOutput").ap()
    W = {n: nc.dram_tensor(n, list(s), F32, kind="ExternalInput").ap() for n, s in wnames}
    C = {n: nc.dram_tensor(n, list(s), F32, kind="ExternalInput").ap() for n, s in cnames}
    if len(subs) > 1:
        xr = nc.dram_tensor("xr", [n_seq, NT, D, TT], F32, kind="Internal").ap()
    else:
        xr = None

    with stack:
        ctx.ones128 = sbuf(ctx, stack, "g_ones", [128, 128], BF16)
        ctx.ident = sbuf(ctx, stack, "g_ident", [128, 128], BF16)
        prog = Prog(ctx)
        prog.op("gpsimd", lambda e: e.dma_start(out=ctx.ones128[:], in_=C["c_ones"][:]),
                writes=["c:ones"], dma=True, tag="ld_ones")
        prog.op("gpsimd", lambda e: e.dma_start(out=ctx.ident[:], in_=C["c_ident"][:]),
                writes=["c:ident"], dma=True, tag="ld_ident")

        if "moe" in subs and USE_PRECAST:
            ctx.moe_bf = emit_moe_precast(ctx, prog, W)
        for i, sub in enumerate(subs):
            src = xin if i == 0 else xr
            dst = xout if i == len(subs) - 1 else xr
            SUB_EMITTERS[sub](ctx, prog, src, dst, W, C, i)

        prog.final_waits()
        with nc.Block() as blk:
            prog.emit(blk)
    return nc


SUB_EMITTERS["attn_a0"] = emit_attn_a
SUB_EMITTERS["moe"] = emit_moe
SUB_EMITTERS["ffn1"] = emit_l1ffn
SUB_EMITTERS["attn_a1"] = emit_l1aa
SUB_EMITTERS["attn_b1"] = emit_mamba
SUB_EMITTERS["attn_b0"] = emit_attn_b



def _to_fm(x_seqs, T):
    S = x_seqs.shape[0]
    NT = T // TT
    xf = np.ascontiguousarray(x_seqs.transpose(0, 2, 1))
    xf = xf.reshape(S, D, NT, TT).transpose(0, 2, 1, 3)
    return np.ascontiguousarray(xf)


def _from_fm(xf):
    S, NT = xf.shape[0], xf.shape[1]
    x = xf.transpose(0, 2, 1, 3).reshape(S, D, NT * TT)
    return np.ascontiguousarray(x.transpose(0, 2, 1))


def sparse_mask_bias(T):
    r = np.arange(T)
    causal = r[None, :] <= r[:, None]
    local = np.abs(r[:, None] - r[None, :]) <= 128
    strided = (r[None, :] % 64 == 0) & np.ones((T, 1), bool)
    mask = causal & (local | strided)
    return np.where(mask, 0.0, 8.0 * NEG).astype(np.float32)


def host_prep_attn_a(inputs, T):
    W = {
        "aa_wqkv": np.ascontiguousarray(inputs["L0_attn_a_w_qkv"].T),
        "aa_wout": np.ascontiguousarray(inputs["L0_attn_a_w_out"].T),
    }
    p = np.arange(128)[:, None]
    j = np.arange(128)[None, :]
    prevm = (p <= j) | (j == 0) | (j == 64)
    diagm = p >= j
    wb = np.concatenate([np.where(prevm, 0.0, 8.0 * NEG),
                         np.where(diagm, 0.0, 8.0 * NEG)], axis=1).astype(np.float32)
    nstr = T // 64
    selstr = np.zeros((T, nstr), np.float32)
    selstr[np.arange(nstr) * 64, np.arange(nstr)] = 1.0
    C = {"aa_wb": wb, "c_selstr": selstr}
    return W, C


def host_prep_moe(inputs):
    w1 = np.asarray(inputs["L0_ffn_w1"], np.float32)
    w3 = np.asarray(inputs["L0_ffn_w3"], np.float32)
    w2 = np.asarray(inputs["L0_ffn_w2"], np.float32)
    sw1 = np.asarray(inputs["L0_ffn_sw1"], np.float32)
    sw3 = np.asarray(inputs["L0_ffn_sw3"], np.float32)
    sw2 = np.asarray(inputs["L0_ffn_sw2"], np.float32)
    w1x = np.concatenate([w1.transpose(0, 2, 1),
                          sw1.T[None, :, 0:1024], sw1.T[None, :, 1024:2048]], 0)
    w3x = np.concatenate([w3.transpose(0, 2, 1),
                          sw3.T[None, :, 0:1024], sw3.T[None, :, 1024:2048]], 0)
    w2x = np.concatenate([w2.transpose(0, 2, 1),
                          sw2.T[None, 0:1024, :], sw2.T[None, 1024:2048, :]], 0)
    return {
        "moe_w1x": np.ascontiguousarray(w1x),
        "moe_w3x": np.ascontiguousarray(w3x),
        "moe_w2x": np.ascontiguousarray(w2x),
        "moe_gw": np.ascontiguousarray(inputs["L0_ffn_gate_w"].T),
        "moe_gbias": np.ascontiguousarray(inputs["L0_ffn_gate_bias"].reshape(1, 8)),
    }


def host_prep_l1ffn(inputs):
    return {
        "f1_wr": np.ascontiguousarray(inputs["L1_ffn_W_r"].T),
        "f1_wk": np.ascontiguousarray(inputs["L1_ffn_W_k"].T),
        "f1_wv": np.ascontiguousarray(inputs["L1_ffn_W_v"].T),
        "f1_mur": np.ascontiguousarray(inputs["L1_ffn_mu_r"].reshape(8, 128).T),
        "f1_muk": np.ascontiguousarray(inputs["L1_ffn_mu_k"].reshape(8, 128).T),
    }


def rope_factor_a1(T, d_head=64):
    i = np.arange(d_head)
    theta = np.power(10000.0, -2.0 * (i // 2).astype(np.float64) / d_head)
    pos = np.arange(T, dtype=np.float64)
    ang = pos[:, None] * theta[None, :]
    sign = np.where(np.arange(T) % 2 == 0, -1.0, 1.0)[:, None]
    rf = np.cos(ang) + np.sin(ang) * sign
    return rf.astype(np.float32)


def host_prep_l1aa(inputs, T):
    rf = rope_factor_a1(T)
    W = {
        "a1_wq": np.ascontiguousarray(inputs["L1_attn_a_w_q"], np.float32),
        "a1_wk": np.ascontiguousarray(inputs["L1_attn_a_w_k"], np.float32),
        "a1_wv": np.ascontiguousarray(inputs["L1_attn_a_w_v"], np.float32),
        "a1_wo": np.ascontiguousarray(inputs["L1_attn_a_w_o"], np.float32),
        "a1_normw": np.ascontiguousarray(inputs["L1_attn_a_norm_w"].reshape(8, 128).T),
    }
    r = np.arange(T)
    causal = r[None, :] <= r[:, None]
    C = {
        "a1_bias": np.where(causal, 0.0, 8.0 * NEG).astype(np.float32),
        "c_ropea1": np.ascontiguousarray(np.vstack([rf.T, rf.T])),
    }
    return W, C


def host_prep_attn_b(inputs, T):
    dh = 64
    inv_freq = 1.0 / (10000.0 ** (np.arange(0, dh, 2) / dh))
    t = np.arange(T)
    fr = np.outer(t, inv_freq)
    cos = np.cos(fr).astype(np.float32)
    sin = np.sin(fr).astype(np.float32)
    cosd = np.empty((dh, T), np.float32)
    sind = np.empty((dh, T), np.float32)
    cosd[0::2] = cos.T
    cosd[1::2] = cos.T
    sind[0::2] = -sin.T
    sind[1::2] = sin.T
    swap = np.zeros((128, 128), np.float32)
    for i in range(64):
        swap[2 * i, 2 * i + 1] = 1.0
        swap[2 * i + 1, 2 * i] = 1.0
    W = {
        "ab_wqkv": np.ascontiguousarray(inputs["L0_attn_b_qkv_w"].T),
        "ab_wo": np.ascontiguousarray(inputs["L0_attn_b_wo_w"].T),
        "ab_iqw": np.ascontiguousarray(inputs["L0_attn_b_iq_w"].T),
        "ab_ikw": np.ascontiguousarray(inputs["L0_attn_b_ik_w"].T),
        "ab_iww": np.ascontiguousarray(inputs["L0_attn_b_iw_w"].T),
    }
    C = {
        "ab_cos": np.ascontiguousarray(np.vstack([cosd, cosd])),
        "ab_sin": np.ascontiguousarray(np.vstack([sind, sind])),
        "c_swap": swap,
    }
    return W, C


def host_prep_mamba(inputs):
    ipw = np.asarray(inputs["L1_attn_b_in_proj_w"], np.float32).T
    z64 = np.zeros((1024, 64), np.float32)
    ipw_z = ipw[:, 0:2048]
    ipw_r = np.concatenate([ipw[:, 2048:4096], ipw[:, 4096:4160], z64,
                            ipw[:, 4160:4224], z64, ipw[:, 4224:4256]], axis=1)
    cw = np.asarray(inputs["L1_attn_b_conv_w"], np.float32)
    cb = np.asarray(inputs["L1_attn_b_conv_b"], np.float32)
    cw_new = np.zeros((18 * 128, 4), np.float32)
    cb_new = np.zeros(18 * 128, np.float32)
    cw_new[0:2048] = cw[0:2048]
    cw_new[16 * 128:16 * 128 + 64] = cw[2048:2112]
    cw_new[17 * 128:17 * 128 + 64] = cw[2112:2176]
    cb_new[0:2048] = cb[0:2048]
    cb_new[16 * 128:16 * 128 + 64] = cb[2048:2112]
    cb_new[17 * 128:17 * 128 + 64] = cb[2112:2176]
    return {
        "mb_ipwz": np.ascontiguousarray(ipw_z),
        "mb_ipwr": np.ascontiguousarray(ipw_r),
        "mb_opw": np.ascontiguousarray(inputs["L1_attn_b_out_proj_w"].T),
        "mb_convw": np.ascontiguousarray(cw_new.reshape(18, 128, 4).transpose(1, 0, 2)
                                         .reshape(128, 72)),
        "mb_convb": np.ascontiguousarray(cb_new.reshape(18, 128).T),
        "mb_dtb": np.ascontiguousarray(inputs["L1_attn_b_dt_bias"].reshape(1, 32)),
        "mb_alog": np.ascontiguousarray(inputs["L1_attn_b_A_log"].reshape(1, 32)),
        "mb_dpar": np.ascontiguousarray(inputs["L1_attn_b_D_param"].reshape(1, 32)),
        "mb_gnw": np.ascontiguousarray(inputs["L1_attn_b_gnorm_w"].reshape(16, 128).T),
        "mb_nw": np.ascontiguousarray(inputs["L1_attn_b_norm_w"].reshape(8, 128).T),
        "mb_rw1": np.ascontiguousarray(inputs["L1_attn_b_r_w1"].T),
        "mb_rb1": np.ascontiguousarray(inputs["L1_attn_b_r_b1"].reshape(32, 1)),
        "mb_rw2": np.ascontiguousarray(inputs["L1_attn_b_r_w2"].T),
        "mb_rb2": np.ascontiguousarray(inputs["L1_attn_b_r_b2"].reshape(1, 3)),
    }


def base_consts():
    return {
        "c_ones": np.ones((128, 128), np.float32),
        "c_ident": np.eye(128, dtype=np.float32),
        "c_ones1r": np.ones((1, 128), np.float32),
        "c_identf": np.eye(128, dtype=np.float32),
        "c_sel8": np.kron(np.eye(8, dtype=np.float32), np.ones((1, 128), np.float32)),
        "c_triu": np.triu(np.ones((128, 128), np.float32)),
        "c_diagc": np.where(np.arange(128)[None, :] <= np.arange(128)[:, None],
                            0.0, NEG).astype(np.float32),
        "c_sel32": np.kron(np.eye(32, dtype=np.float32), np.ones((1, 128), np.float32)),
    }


CHAIN = ("attn_a0", "moe", "attn_b0", "moe", "attn_a1", "ffn1", "attn_b1", "ffn1")


def host_prep_all(inputs, T, subs):
    W, C = {}, base_consts()
    if any(s.startswith("attn_a0") for s in subs):
        w, c = host_prep_attn_a(inputs, T)
        W.update(w)
        C.update(c)
    if "moe" in subs:
        W.update(host_prep_moe(inputs))
    if "ffn1" in subs:
        W.update(host_prep_l1ffn(inputs))
    if "attn_a1" in subs:
        w, c = host_prep_l1aa(inputs, T)
        W.update(w)
        C.update(c)
    if "attn_b1" in subs:
        W.update(host_prep_mamba(inputs))
    if "attn_b0" in subs:
        w, c = host_prep_attn_b(inputs, T)
        W.update(w)
        C.update(c)
        if "a1_bias" not in C:
            r = np.arange(T)
            causal = r[None, :] <= r[:, None]
            C["a1_bias"] = np.where(causal, 0.0, 8.0 * NEG).astype(np.float32)
    return W, C


def kernel(**inputs):
    T = 2048
    x = np.asarray(inputs["L0_attn_a_x"], np.float32)
    B = x.shape[0]
    n_cores = 8
    spc = B // n_cores

    subs = CHAIN
    W, C = host_prep_all(inputs, T, subs)
    wnames = [(k, v.shape) for k, v in W.items()]
    cnames = [(k, v.shape) for k, v in C.items()]
    nc = build_program(T=T, n_seq=spc, subs=subs, wnames=wnames, cnames=cnames)

    in_maps = []
    for c in range(n_cores):
        m = {"xin": _to_fm(x[c * spc:(c + 1) * spc], T)}
        m.update(W)
        m.update(C)
        in_maps.append(m)
    res = run_bass_kernel_spmd(nc, in_maps, core_ids=list(range(n_cores)))
    outs = [_from_fm(r["xout"]) for r in res.results]
    return np.concatenate(outs, axis=0)
